# version 06
# baseline (speedup 1.0000x reference)
import numpy as np
from contextlib import ExitStack

import concourse.bass as bass
import concourse.mybir as mybir
from concourse.bass_utils import run_bass_kernel_spmd

F32 = mybir.dt.float32
BF16 = mybir.dt.bfloat16
I32 = mybir.dt.int32
ALU = mybir.AluOpType
ACT = mybir.ActivationFunctionType
AXX = mybir.AxisListType.X

LNEPS8 = 18.420680743952367


class Sem:
    def __init__(self, h):
        self.h = h
        self.n = 0

    def bump(self, k=1):
        self.n += k
        return self.n


class EM:
    ENGS = ("t", "v", "s", "g", "y")

    def __init__(self, nc):
        self.nc = nc
        self.cur = None

    def phase(self):
        self.cur = {e: [] for e in self.ENGS}

    def flush(self):
        ph, self.cur = self.cur, None

        def runner(ops):
            def go(e):
                for f in ops:
                    f(e)
            return go

        with self.nc.Block() as blk:
            if ph["y"]:
                blk.sync(runner(ph["y"]))
            if ph["t"]:
                blk.tensor(runner(ph["t"]))
            if ph["v"]:
                blk.vector(runner(ph["v"]))
            if ph["s"]:
                blk.scalar(runner(ph["s"]))
            if ph["g"]:
                blk.gpsimd(runner(ph["g"]))

    def op(self, eng, fn):
        self.cur[eng].append(fn)

    def wait(self, eng, sem, n=None):
        v = sem.n if n is None else n
        if v > 0:
            self.cur[eng].append(lambda e: e.wait_ge(sem.h, v))

    def opi(self, eng, sem, fn, k=1):
        n = sem.bump(k)
        self.cur[eng].append(lambda e: fn(e).then_inc(sem.h, k))
        return n

    def vs(self):
        self.wait("v", None) if False else None

    def dma(self, eng, sem, out_ap, in_ap, **kw):
        return self.opi(eng, sem, lambda e: e.dma_start(out=out_ap, in_=in_ap, **kw), k=16)


class Guard:
    def __init__(self, n):
        self.rec = [None] * n

    def use(self, em, eng, i):
        r = self.rec[i]
        if r is not None:
            em.wait(eng, r[0], r[1])
            self.rec[i] = None

    def done(self, i, sem):
        self.rec[i] = (sem, sem.n)


def build(D, H, NRT, dbg=None):
    NCK = D // 128
    HBN = H // 128
    W = NRT + 2
    R0 = 2
    NBLK = NRT // 128

    def tchunks(s0, e0, step):
        out, s = [], s0
        while s < e0:
            out.append((s, min(s + step, e0)))
            s += step
        return out

    NC_H1 = [(0, 2)] + tchunks(R0, W, 512)
    NC_RE = tchunks(R0, W, 512)
    NC_8 = tchunks(R0, W, 128)
    HBG = min(8, HBN)
    HBS = min(4, HBG)
    PAW = 2048
    while (H * D // 128) % PAW != 0:
        PAW //= 2

    nc = bass.Bass("TRN2", target_bir_lowering=False)

    xT = nc.dram_tensor("xT", [D, W], F32, kind="ExternalInput").ap()
    wnames = ["wr", "wk", "wv", "wo", "cwr", "cwk", "cwv"]
    wt = {n: nc.dram_tensor(n, [NCK, 128, NCK, 128], F32, kind="ExternalInput").ap()
          for n in wnames}
    gt = nc.dram_tensor("gt", [HBN, 128, NCK, 128], F32, kind="ExternalInput").ap()
    ut = nc.dram_tensor("ut", [HBN, 128, NCK, 128], F32, kind="ExternalInput").ap()
    dtw = nc.dram_tensor("dtw", [H, D], F32, kind="ExternalInput").ap()
    vnames = ["ln1g", "ln1b", "ln2g", "ln2b", "tmu", "mur", "muk", "muv", "cmur", "cmuk"]
    vin = {n: nc.dram_tensor("v_" + n, [NCK, 128], F32, kind="ExternalInput").ap()
           for n in vnames}
    maskd = nc.dram_tensor("maskd", [128, 1], F32, kind="ExternalInput").ap()
    y = nc.dram_tensor("y", [NRT, D], F32, kind="ExternalOutput").ap()

    dbg_t = (nc.dram_tensor("dbg", [D, W], F32, kind="ExternalOutput").ap()
             if dbg in ("a", "h", "xmid", "x1", "x2") else None)
    dbgq_t = (nc.dram_tensor("dbg", [D, W], BF16, kind="ExternalOutput").ap()
              if dbg == "xq" else None)
    dbgg_t = (nc.dram_tensor("dbg", [HBN, 128, NRT], F32, kind="ExternalOutput").ap()
              if dbg == "gu" else None)
    dbgc_t = (nc.dram_tensor("dbg", [128, 16], F32, kind="ExternalOutput").ap()
              if dbg == "cc" else None)
    dbgr_t = (nc.dram_tensor("dbg", [128, 3 * W], F32, kind="ExternalOutput").ap()
              if dbg == "repx" else None)
    dbgx2_t = (nc.dram_tensor("dbg", [128, HBS * NRT], BF16, kind="ExternalOutput").ap()
               if dbg == "xq2" else None)
    dbgdw_t = (nc.dram_tensor("dbg", [128, HBS * D], BF16, kind="ExternalOutput").ap()
               if dbg == "dw" else None)
    dbge_t = (nc.dram_tensor("dbg", [128, 512], F32, kind="ExternalOutput").ap()
              if dbg in ("et7", "draw") else None)

    a_d = nc.dram_tensor("a_d", [D, W], F32, kind="Internal").ap()
    x1_d = nc.dram_tensor("x1_d", [D, W], F32, kind="Internal").ap()
    gu_d = nc.dram_tensor("gu_d", [HBN, 128, NRT], F32, kind="Internal").ap()

    em = EM(nc)
    es = ExitStack()
    with es:
        qW = Sem(es.enter_context(nc.semaphore("qW")))
        qA = Sem(es.enter_context(nc.semaphore("qA")))
        sT = Sem(es.enter_context(nc.semaphore("sT")))
        sV = Sem(es.enter_context(nc.semaphore("sV")))
        sS = Sem(es.enter_context(nc.semaphore("sS")))
        sG = Sem(es.enter_context(nc.semaphore("sG")))
        sO = Sem(es.enter_context(nc.semaphore("sO")))

        VEC = es.enter_context(nc.sbuf_tensor("VEC", [128, 12 * NCK], F32))
        CC = es.enter_context(nc.sbuf_tensor("CC", [128, 16], F32))
        IDN = es.enter_context(nc.sbuf_tensor("IDN", [128, 128], F32))
        STAT = es.enter_context(nc.sbuf_tensor("STAT", [128, 256], F32))
        MAXC = es.enter_context(nc.sbuf_tensor("MAXC", [128, 2 * NBLK], F32))
        JNK = es.enter_context(nc.sbuf_tensor("JNK", [128, max(PAW, NRT, W)], F32))
        ROWS = es.enter_context(nc.sbuf_tensor("ROWS", [1, 640 + max(NRT, 128)], F32))
        psb = [es.enter_context(nc.psum_tensor(f"pb{i}", [128, 512], F32)) for i in range(8)]

        C_ONE, C_EPS5, C_EPS6, C_MASK = 0, 1, 2, 3
        C_TAU = {"g": 4, "u": 5, "d": 6}
        C_NTAU = {"g": 7, "u": 8, "d": 9}
        C_ISW = {"g": 10, "u": 11, "d": 12}

        def cc(i):
            return CC[:, i:i + 1]

        free_rec = [None] * 8

        def acquire(bi):
            if free_rec[bi] is not None:
                s, n = free_rec[bi]
                em.wait("t", s, n)
                free_rec[bi] = None

        def release(bi, sem):
            free_rec[bi] = (sem, sem.n)

        def mm_chain(bi, pairs, wq):
            pairs = pairs[::-1]
            L = len(pairs)
            for i, (l, r) in enumerate(pairs[:-1]):
                em.op("t", lambda t, l=l, r=r, i=i, bi=bi, wq=wq: t.matmul(
                    psb[bi][:, 0:wq], l, r, start=(i == 0), stop=False))
            l, r = pairs[-1]
            return em.opi("t", sT, lambda t, l=l, r=r, bi=bi, wq=wq, L=L: t.matmul(
                psb[bi][:, 0:wq], l, r, start=(L == 1), stop=True))

        def rep_from_psum_row(src_ap, W_, rep):
            for (s, e) in tchunks(0, W_, 512):
                acquire(6)
                em.wait("t", sV)
                em.opi("t", sT, lambda t, s=s, e=e: t.matmul(
                    psb[6][0:1, 0:e - s], cc(C_ONE), src_ap[:, s:e], start=True, stop=True))
                em.wait("v", sT)
                em.opi("v", sV, lambda v, s=s, e=e: v.tensor_copy(
                    rep[0:1, s:e], psb[6][0:1, 0:e - s]))
                release(6, sV)
            for (s, e) in tchunks(0, W_, 512):
                acquire(7)
                em.wait("t", sV)
                em.opi("t", sT, lambda t, s=s, e=e: t.matmul(
                    psb[7][:, 0:e - s], ROWS[0:1, 0:128], rep[0:1, s:e],
                    start=True, stop=True))
                em.wait("v", sT)
                em.opi("v", sV, lambda v, s=s, e=e: v.tensor_copy(
                    rep[:, s:e], psb[7][:, 0:e - s]))
                release(7, sV)

        def stats_sums(src, scratch, S1rep, S2rep):
            half = NCK // 2
            em.opi("v", sV, lambda v: v.tensor_add(
                scratch[:, 0:half * W], src[:, 0:half * W], src[:, half * W:NCK * W]))
            n = half
            while n > 1:
                h2 = n // 2
                em.opi("v", sV, lambda v, h2=h2, n=n: v.tensor_add(
                    scratch[:, 0:h2 * W], scratch[:, 0:h2 * W], scratch[:, h2 * W:n * W]))
                n = h2
            rep_from_psum_row(scratch[:, 0:W], W, S1rep)
            em.wait("s", sT)
            em.opi("s", sS, lambda s: s.activation(
                scratch[:, 0:half * W], src[:, 0:half * W], ACT.Square))
            for i in range(half):
                if i > 0:
                    em.wait("s", sV)
                em.opi("s", sS, lambda s, i=i: s.activation(
                    JNK[:, 0:W], src[:, (half + i) * W:(half + 1 + i) * W], ACT.Square))
                em.wait("v", sS)
                em.opi("v", sV, lambda v, i=i: v.tensor_add(
                    scratch[:, i * W:(i + 1) * W], scratch[:, i * W:(i + 1) * W],
                    JNK[:, 0:W]))
            n = half
            while n > 1:
                h2 = n // 2
                em.opi("v", sV, lambda v, h2=h2, n=n: v.tensor_add(
                    scratch[:, 0:h2 * W], scratch[:, 0:h2 * W], scratch[:, h2 * W:n * W]))
                n = h2
            rep_from_psum_row(scratch[:, 0:W], W, S2rep)

        def cols_to_rep(ncols, rep_ap, off):
            for j in range(ncols):
                acquire(6)
                em.wait("t", sV)
                em.opi("t", sT, lambda t, j=j: t.transpose(
                    psb[6][0:1, 0:128], MAXC[:, j:j + 1], IDN[:]))
                em.wait("v", sT)
                em.opi("v", sV, lambda v, j=j: v.tensor_copy(
                    ROWS[0:1, 640 + j * 128:640 + (j + 1) * 128], psb[6][0:1, 0:128]))
                release(6, sV)
            for j in range(ncols):
                acquire(7)
                em.wait("t", sV)
                em.opi("t", sT, lambda t, j=j: t.matmul(
                    psb[7][:, 0:128], ROWS[0:1, 0:128],
                    ROWS[0:1, 640 + j * 128:640 + (j + 1) * 128], start=True, stop=True))
                em.wait("v", sT)
                em.opi("v", sV, lambda v, j=j, off=off: v.tensor_copy(
                    rep_ap[:, off + j * 128:off + (j + 1) * 128], psb[7][:, 0:128]))
                release(7, sV)

        def tap(name, sb):
            if dbg == name:
                em.wait("s", sV)
                em.wait("s", sS)
                em.dma("s", qA, dbg_t.rearrange("(g p) t -> p g t", p=128),
                       sb[:, :].rearrange("p (g t) -> p g t", g=NCK))

        per_mat = H * D
        tiles_per_mat = per_mat // (128 * PAW)
        pa_srcs = [gt.rearrange("a b c d -> (a b c d)"),
                   ut.rearrange("a b c d -> (a b c d)"),
                   dtw.rearrange("a b -> (a b)")]
        pa_total = 3 * tiles_per_mat
        pa_state = {"i": 0}
        pa_guard = Guard(2)
        PA = [None, None]

        def pa_pump(nmax):
            while pa_state["i"] < pa_total and nmax > 0:
                i = pa_state["i"]
                mi, ti = i // tiles_per_mat, i % tiles_per_mat
                buf = PA[i % 2]
                pa_guard.use(em, "y", i % 2)
                base = ti * 128 * PAW
                em.dma("y", qW, buf[:, 0:PAW],
                       pa_srcs[mi][base:base + 128 * PAW].rearrange("(p m) -> p m", p=128))
                em.wait("s", qW)
                em.opi("s", sS, lambda s, buf=buf, i=i: s.activation(
                    buf[:, 0:PAW], buf[:, 0:PAW], ACT.Abs, accum_out=STAT[:, i:i + 1]))
                pa_guard.done(i % 2, sS)
                pa_state["i"] += 1
                nmax -= 1

        stkPA = ExitStack()
        PA[0] = stkPA.enter_context(nc.sbuf_tensor("PA0", [128, PAW], F32, side="left"))
        PA[1] = stkPA.enter_context(nc.sbuf_tensor("PA1", [128, PAW], F32, side="left"))
        stkFx1 = ExitStack()
        Fx1 = stkFx1.enter_context(nc.sbuf_tensor("Fx1", [128, NCK * W], F32, side="left"))
        repA = stkFx1.enter_context(nc.sbuf_tensor("repA", [128, 6 * W], F32, side="left"))
        stkC1 = ExitStack()
        C1 = stkC1.enter_context(nc.sbuf_tensor("C1", [128, (NCK // 2) * W], F32,
                                                side="right"))

        em.phase()
        em.opi("v", sV, lambda v: v.memset(CC[:, :], 0.0))
        em.wait("v", sV)
        em.opi("v", sV, lambda v: v.memset(cc(C_ONE), 1.0))
        em.opi("v", sV, lambda v: v.memset(cc(C_EPS5), 1e-5))
        em.opi("v", sV, lambda v: v.memset(cc(C_EPS6), 1e-6))
        em.opi("v", sV, lambda v: v.memset(ROWS[0:1, 0:128], 1.0))
        em.opi("g", sG, lambda g: g.iota(IDN[:], pattern=[[-1, 128]], base=0,
                                         channel_multiplier=1,
                                         allow_small_or_imprecise_dtypes=True))
        em.opi("g", sG, lambda g: g.tensor_single_scalar(IDN[:], IDN[:], 0.0, ALU.is_equal))
        vecs = {}
        for i, n in enumerate(vnames):
            t = VEC[:, i * NCK:(i + 1) * NCK]
            vecs[n] = t
            em.dma("s", qA, t, vin[n].rearrange("g p -> p g"),
                   allow_slow_non_contiguous=True)
        em.dma("s", qA, cc(C_MASK), maskd[:, :])
        em.dma("s", qA, Fx1[:, :].rearrange("p (g t) -> p g t", g=NCK),
               xT.rearrange("(g p) t -> p g t", p=128))
        em.wait("v", qA)
        UP = VEC[:, 10 * NCK:11 * NCK]
        em.opi("v", sV, lambda v: v.tensor_single_scalar(UP, vecs["tmu"], LNEPS8, ALU.add))
        pa_pump(8)

        S1 = repA[:, 0 * W:1 * W]
        S2 = repA[:, 1 * W:2 * W]
        stats_sums(Fx1[:, :], C1, S1, S2)
        m0 = repA[:, 2 * W:3 * W]
        r0 = repA[:, 3 * W:4 * W]
        r1 = repA[:, 4 * W:5 * W]
        tp = repA[:, 5 * W:6 * W]
        em.opi("v", sV, lambda v: v.tensor_single_scalar(m0, S1, 1.0 / D, ALU.mult))
        em.opi("v", sV, lambda v: v.tensor_single_scalar(S2, S2, 1.0 / D, ALU.mult))
        em.opi("v", sV, lambda v: v.tensor_mul(tp, m0, m0))
        em.opi("v", sV, lambda v: v.tensor_sub(S2, S2, tp))
        em.wait("s", sV)
        em.opi("s", sS, lambda s: s.activation(r0, S2, ACT.Sqrt, bias=cc(C_EPS5)))
        em.wait("v", sS)
        em.opi("v", sV, lambda v: v.reciprocal(r0, r0))
        em.opi("v", sV, lambda v: v.tensor_mul(tp, r0, r0))
        em.opi("v", sV, lambda v: v.tensor_mul(tp, S2, tp))
        em.wait("s", sV)
        em.opi("s", sS, lambda s: s.activation(r1, tp, ACT.Sqrt, bias=cc(C_EPS5)))
        em.wait("v", sS)
        em.opi("v", sV, lambda v: v.reciprocal(r1, r1))
        pa_pump(4)
        em.flush()
        stkC1.close()

        stkB = ExitStack()
        Bh = stkB.enter_context(nc.sbuf_tensor("Bh", [128, NCK * W], F32, side="right"))
        em.phase()
        for c in range(NCK):
            sl = slice(c * W, (c + 1) * W)
            em.opi("v", sV, lambda v, sl=sl: v.tensor_sub(Fx1[:, sl], Fx1[:, sl], m0))
            em.opi("v", sV, lambda v, sl=sl: v.tensor_mul(Fx1[:, sl], Fx1[:, sl], r0))
        em.wait("s", sV)
        em.dma("s", qA, a_d.rearrange("(g p) t -> p g t", p=128),
               Fx1[:, :].rearrange("p (g t) -> p g t", g=NCK))
        a_out_n = qA.n
        for c in range(NCK):
            sl = slice(c * W, (c + 1) * W)
            em.opi("v", sV, lambda v, sl=sl: v.tensor_mul(Bh[:, sl], Fx1[:, sl], r1))
            em.opi("v", sV, lambda v, c=c, sl=sl: v.tensor_scalar(
                Bh[:, sl], Bh[:, sl], vecs["ln1g"][:, c:c + 1],
                vecs["ln1b"][:, c:c + 1], ALU.mult, ALU.add))
            em.wait("v", sV)
            em.opi("v", sV, lambda v, c=c: v.tensor_single_scalar(
                Bh[:, c * W:c * W + 2], Bh[:, c * W:c * W + 2], cc(C_MASK), ALU.mult))
        tap("a", Fx1)
        tap("h", Bh)
        pa_pump(8)
        em.flush()
        stkFx1.close()

        stkT = ExitStack()
        TRIO = stkT.enter_context(nc.sbuf_tensor("TRIO", [128, 3 * NCK * W], BF16,
                                                 side="left"))

        def trio_ap(i, c, s, e):
            return TRIO[:, (i * NCK + c) * W + s:(i * NCK + c) * W + e]

        em.phase()
        em.wait("v", qA)
        for c in range(NCK):
            em.opi("v", sV, lambda v, c=c: v.tensor_sub(
                JNK[:, 0:W - 1], Bh[:, c * W + 1:(c + 1) * W],
                Bh[:, c * W:(c + 1) * W - 1]))
            for i, mu in enumerate(["mur", "muk", "muv"]):
                em.opi("v", sV, lambda v, c=c, i=i, mu=mu: v.scalar_tensor_tensor(
                    trio_ap(i, c, 1, W), JNK[:, 0:W - 1], vecs[mu][:, c:c + 1],
                    Bh[:, c * W:(c + 1) * W - 1], ALU.mult, ALU.add))
                em.opi("v", sV, lambda v, c=c, i=i: v.tensor_copy(
                    trio_ap(i, c, 0, 1), Bh[:, c * W:c * W + 1]))
        pa_pump(4)
        em.flush()
        stkB.close()

        stk3w = ExitStack()
        WSTG = stk3w.enter_context(nc.sbuf_tensor("WSTG", [128, 2 * NCK * 128], F32,
                                                  side="right"))
        WBF = [stk3w.enter_context(nc.sbuf_tensor(f"WBF{i}", [128, NCK * 128], BF16,
                                                  side="right")) for i in range(3)]
        ET = [stk3w.enter_context(nc.sbuf_tensor(f"ET{i}", [128, 512], F32, side="right"))
              for i in range(2)]
        stkR = ExitStack()
        RWKV = stkR.enter_context(nc.sbuf_tensor("RWKV", [128, NCK * W], BF16,
                                                 side="right"))
        wstg_guard = Guard(2)
        wbf_guard = Guard(3)
        et_guard = Guard(1)

        def stream_wblk(src_ap, dc, wi):
            half = dc % 2
            stg = WSTG[:, half * NCK * 128:(half + 1) * NCK * 128]
            wstg_guard.use(em, "y", half)
            em.dma("y", qW, stg, src_ap[dc, :, :, :].rearrange("p k j -> p (k j)"))
            em.wait("v", qW)
            wbf_guard.use(em, "v", wi)
            em.opi("v", sV, lambda v, stg=stg, wi=wi: v.tensor_copy(WBF[wi][:, :], stg))
            wstg_guard.done(half, sV)
            return sV.n

        em.phase()
        em.wait("y", qA)
        em.wait("v", qA)
        for dc in range(NCK):
            c_r = stream_wblk(wt["wr"], dc, 0)
            c_k = stream_wblk(wt["wk"], dc, 1)
            c_v = stream_wblk(wt["wv"], dc, 2)
            b0, b1, b2 = (dc % 2) * 3, (dc % 2) * 3 + 1, (dc % 2) * 3 + 2
            for (s, e) in NC_H1:
                wq = e - s
                acquire(b0)
                em.wait("t", sV, c_r)
                mm_chain(b0, [(WBF[0][:, k * 128:(k + 1) * 128], trio_ap(0, k, s, e))
                              for k in range(NCK)], wq)
                acquire(b1)
                em.wait("t", sV, c_k)
                mm_chain(b1, [(WBF[1][:, k * 128:(k + 1) * 128], trio_ap(1, k, s, e))
                              for k in range(NCK)], wq)
                acquire(b2)
                em.wait("t", sV, c_v)
                mm_chain(b2, [(WBF[2][:, k * 128:(k + 1) * 128], trio_ap(2, k, s, e))
                              for k in range(NCK)], wq)
                et_guard.use(em, "s", 0)
                em.wait("s", sT)
                em.opi("s", sS, lambda s_, b0=b0, wq=wq: s_.activation(
                    ET[0][:, 0:wq], psb[b0][:, 0:wq], ACT.Sigmoid))
                em.opi("s", sS, lambda s_, b1=b1, wq=wq, dc=dc: s_.activation(
                    ET[1][:, 0:wq], psb[b1][:, 0:wq], ACT.Sigmoid, bias=UP[:, dc:dc + 1]))
                release(b0, sS)
                release(b1, sS)
                em.wait("v", sS)
                em.opi("v", sV, lambda v, wq=wq: v.tensor_mul(
                    ET[0][:, 0:wq], ET[0][:, 0:wq], ET[1][:, 0:wq]))
                em.opi("v", sV, lambda v, b2=b2, dc=dc, s=s, e=e, wq=wq: v.tensor_mul(
                    RWKV[:, dc * W + s:dc * W + e], ET[0][:, 0:wq], psb[b2][:, 0:wq]))
                release(b2, sV)
                et_guard.done(0, sV)
            for wi in range(3):
                wbf_guard.done(wi, sT)
            pa_pump(2)
        rwkv_n = sV.n
        em.flush()
        stkT.close()

        stkFx2 = ExitStack()
        Fx2 = stkFx2.enter_context(nc.sbuf_tensor("Fx2", [128, NCK * W], F32, side="left"))
        em.phase()
        em.wait("s", qA, a_out_n)
        em.dma("s", qA, Fx2[:, :].rearrange("p (g t) -> p g t", g=NCK),
               a_d.rearrange("(g p) t -> p g t", p=128))
        a_in_n = qA.n
        for dc in range(NCK):
            c_o = stream_wblk(wt["wo"], dc, 0)
            bi = dc % 4
            for (s, e) in NC_H1:
                wq = e - s
                acquire(bi)
                em.wait("t", sV, c_o)
                em.wait("t", sV, rwkv_n)
                mm_chain(bi, [(WBF[0][:, k * 128:(k + 1) * 128],
                               RWKV[:, k * W + s:k * W + e]) for k in range(NCK)], wq)
                em.wait("v", sT)
                em.wait("v", qA, a_in_n)
                em.opi("v", sV, lambda v, dc=dc, s=s, wq=wq, bi=bi: v.tensor_add(
                    Fx2[:, dc * W + s:dc * W + s + wq],
                    Fx2[:, dc * W + s:dc * W + s + wq], psb[bi][:, 0:wq]))
                release(bi, sV)
            wbf_guard.done(0, sT)
            pa_pump(2)
        tap("xmid", Fx2)
        em.flush()
        stkR.close()

        stkM2 = ExitStack()
        rep4 = stkM2.enter_context(nc.sbuf_tensor("rep4", [128, 2 * W], F32, side="right"))
        stk4s = ExitStack()
        C4 = stk4s.enter_context(nc.sbuf_tensor("C4", [128, (NCK // 2) * W], F32,
                                                side="right"))
        repS = stk4s.enter_context(nc.sbuf_tensor("repS", [128, 2 * W], F32, side="right"))
        em.phase()
        S1b = repS[:, 0 * W:1 * W]
        S2b = repS[:, 1 * W:2 * W]
        stats_sums(Fx2[:, :], C4, S1b, S2b)
        m2 = rep4[:, 0 * W:1 * W]
        r2l = rep4[:, 1 * W:2 * W]
        em.opi("v", sV, lambda v: v.tensor_single_scalar(m2, S1b, 1.0 / D, ALU.mult))
        em.opi("v", sV, lambda v: v.tensor_single_scalar(S2b, S2b, 1.0 / D, ALU.mult))
        em.opi("v", sV, lambda v: v.tensor_mul(S1b, m2, m2))
        em.opi("v", sV, lambda v: v.tensor_sub(S2b, S2b, S1b))
        em.wait("s", sV)
        em.opi("s", sS, lambda s: s.activation(r2l, S2b, ACT.Sqrt, bias=cc(C_EPS5)))
        em.wait("v", sS)
        em.opi("v", sV, lambda v: v.reciprocal(r2l, r2l))
        em.flush()
        stk4s.close()

        stk4 = ExitStack()
        H2Q = stk4.enter_context(nc.sbuf_tensor("H2Q", [128, NCK * 513], F32, side="right"))
        T2Q = stk4.enter_context(nc.sbuf_tensor("T2Q", [128, NCK * 512], BF16, side="right"))
        K2Q = stk4.enter_context(nc.sbuf_tensor("K2Q", [128, NCK * 512], BF16, side="right"))
        em.phase()
        STASH = VEC[:, 11 * NCK:12 * NCK]
        for ci, (s, e) in enumerate(NC_RE):
            if ci == 0:
                continue
            for c in range(NCK):
                em.opi("v", sV, lambda v, c=c, s=s: v.tensor_copy(
                    STASH[:, c:c + 1], Fx2[:, c * W + s - 1:c * W + s]))
        for ci, (s, e) in enumerate(NC_RE):
            wq = e - s
            em.wait("v", sT)
            em.wait("s", sT)
            if ci == 0:
                for c in range(NCK):
                    em.opi("v", sV, lambda v, c=c, s=s, e=e, wq=wq: v.tensor_sub(
                        H2Q[:, c * 513:c * 513 + wq + 1],
                        Fx2[:, c * W + s - 1:c * W + e], m2[:, s - 1:e]))
                    em.opi("v", sV, lambda v, c=c, s=s, e=e, wq=wq: v.tensor_mul(
                        H2Q[:, c * 513:c * 513 + wq + 1],
                        H2Q[:, c * 513:c * 513 + wq + 1], r2l[:, s - 1:e]))
                    em.opi("v", sV, lambda v, c=c, wq=wq: v.tensor_scalar(
                        H2Q[:, c * 513:c * 513 + wq + 1],
                        H2Q[:, c * 513:c * 513 + wq + 1],
                        vecs["ln2g"][:, c:c + 1], vecs["ln2b"][:, c:c + 1],
                        ALU.mult, ALU.add))
                    em.wait("v", sV)
                    em.opi("v", sV, lambda v, c=c: v.tensor_single_scalar(
                        H2Q[:, c * 513:c * 513 + 1], H2Q[:, c * 513:c * 513 + 1],
                        cc(C_MASK), ALU.mult))
            else:
                for c in range(NCK):
                    em.opi("v", sV, lambda v, c=c, s=s, e=e, wq=wq: v.tensor_sub(
                        H2Q[:, c * 513 + 1:c * 513 + wq + 1],
                        Fx2[:, c * W + s:c * W + e], m2[:, s:e]))
                    em.opi("v", sV, lambda v, c=c, s=s, e=e, wq=wq: v.tensor_mul(
                        H2Q[:, c * 513 + 1:c * 513 + wq + 1],
                        H2Q[:, c * 513 + 1:c * 513 + wq + 1], r2l[:, s:e]))
                    em.opi("v", sV, lambda v, c=c, wq=wq: v.tensor_scalar(
                        H2Q[:, c * 513 + 1:c * 513 + wq + 1],
                        H2Q[:, c * 513 + 1:c * 513 + wq + 1],
                        vecs["ln2g"][:, c:c + 1], vecs["ln2b"][:, c:c + 1],
                        ALU.mult, ALU.add))
                em.wait("v", sV)
                em.opi("v", sV, lambda v, s=s: v.tensor_single_scalar(
                    JNK[:, 0:NCK], STASH[:, 0:NCK], m2[:, s - 1:s], ALU.subtract))
                em.wait("v", sV)
                em.opi("v", sV, lambda v, s=s: v.tensor_single_scalar(
                    JNK[:, 0:NCK], JNK[:, 0:NCK], r2l[:, s - 1:s], ALU.mult))
                em.wait("v", sV)
                for c in range(NCK):
                    em.opi("v", sV, lambda v, c=c: v.tensor_scalar(
                        H2Q[:, c * 513:c * 513 + 1], JNK[:, c:c + 1],
                        vecs["ln2g"][:, c:c + 1], vecs["ln2b"][:, c:c + 1],
                        ALU.mult, ALU.add))
                em.wait("v", sV)
            for c in range(NCK):
                em.opi("v", sV, lambda v, c=c, wq=wq: v.tensor_sub(
                    JNK[:, 0:wq], H2Q[:, c * 513 + 1:c * 513 + 1 + wq],
                    H2Q[:, c * 513:c * 513 + wq]))
                em.opi("v", sV, lambda v, c=c, wq=wq: v.scalar_tensor_tensor(
                    T2Q[:, c * 512:c * 512 + wq], JNK[:, 0:wq],
                    vecs["cmuk"][:, c:c + 1], H2Q[:, c * 513:c * 513 + wq],
                    ALU.mult, ALU.add))
            xk2_n = sV.n
            for dc in range(NCK):
                c_w = stream_wblk(wt["cwk"], dc, 0)
                bi = dc % 4
                acquire(bi)
                em.wait("t", sV, max(c_w, xk2_n))
                mm_chain(bi, [(WBF[0][:, k * 128:(k + 1) * 128],
                               T2Q[:, k * 512:k * 512 + wq]) for k in range(NCK)], wq)
                et_guard.use(em, "v", 0)
                em.wait("v", sT)
                em.opi("v", sV, lambda v, bi=bi, wq=wq: v.tensor_single_scalar(
                    ET[0][:, 0:wq], psb[bi][:, 0:wq], 0.0, ALU.max))
                release(bi, sV)
                em.wait("s", sV)
                em.opi("s", sS, lambda s_, dc=dc, wq=wq: s_.activation(
                    K2Q[:, dc * 512:dc * 512 + wq], ET[0][:, 0:wq], ACT.Square))
                et_guard.done(0, sS)
            wbf_guard.done(0, sT)
            k2_n = sS.n
            em.wait("v", sT)
            for c in range(NCK):
                em.opi("v", sV, lambda v, c=c, wq=wq: v.tensor_sub(
                    JNK[:, 0:wq], H2Q[:, c * 513 + 1:c * 513 + 1 + wq],
                    H2Q[:, c * 513:c * 513 + wq]))
                em.opi("v", sV, lambda v, c=c, wq=wq: v.scalar_tensor_tensor(
                    T2Q[:, c * 512:c * 512 + wq], JNK[:, 0:wq],
                    vecs["cmur"][:, c:c + 1], H2Q[:, c * 513:c * 513 + wq],
                    ALU.mult, ALU.add))
            xr2_n = sV.n
            for dc in range(NCK):
                c_r2 = stream_wblk(wt["cwr"], dc, 0)
                c_v2 = stream_wblk(wt["cwv"], dc, 1)
                b0, b1 = (dc % 2) * 2, (dc % 2) * 2 + 1
                acquire(b0)
                em.wait("t", sV, max(c_r2, xr2_n))
                mm_chain(b0, [(WBF[0][:, k * 128:(k + 1) * 128],
                               T2Q[:, k * 512:k * 512 + wq]) for k in range(NCK)], wq)
                acquire(b1)
                em.wait("t", sV, c_v2)
                em.wait("t", sS, k2_n)
                mm_chain(b1, [(WBF[1][:, k * 128:(k + 1) * 128],
                               K2Q[:, k * 512:k * 512 + wq]) for k in range(NCK)], wq)
                et_guard.use(em, "s", 0)
                em.wait("s", sT)
                em.opi("s", sS, lambda s_, b0=b0, wq=wq: s_.activation(
                    ET[0][:, 0:wq], psb[b0][:, 0:wq], ACT.Sigmoid))
                release(b0, sS)
                em.wait("v", sS)
                em.opi("v", sV, lambda v, b1=b1, wq=wq: v.tensor_mul(
                    ET[0][:, 0:wq], ET[0][:, 0:wq], psb[b1][:, 0:wq]))
                em.opi("v", sV, lambda v, dc=dc, s=s, wq=wq: v.tensor_add(
                    Fx2[:, dc * W + s:dc * W + s + wq],
                    Fx2[:, dc * W + s:dc * W + s + wq], ET[0][:, 0:wq]))
                release(b1, sV)
                et_guard.done(0, sV)
            for wi in range(2):
                wbf_guard.done(wi, sT)
            pa_pump(4)
        for c in range(NCK):
            em.wait("s", sV)
            em.dma("s", qA, JNK[:, 0:W], xT[c * 128:(c + 1) * 128, :])
            em.wait("v", qA)
            em.opi("v", sV, lambda v, c=c: v.tensor_add(
                Fx2[:, c * W:(c + 1) * W], Fx2[:, c * W:(c + 1) * W], JNK[:, 0:W]))
        tap("x1", Fx2)
        em.flush()
        stk4.close()
        stkM2.close()
        stk3w.close()

        stkRX = ExitStack()
        REPX = stkRX.enter_context(nc.sbuf_tensor("REPX", [128, 3 * W], F32, side="right"))
        stkXQ = ExitStack()
        XQ = stkXQ.enter_context(nc.sbuf_tensor("XQ", [128, NCK * W], BF16, side="right"))
        stk5 = ExitStack()
        C5W = max((NCK // 2) * W, 2 * W + 3 * NCK * 128)
        C5 = stk5.enter_context(nc.sbuf_tensor("C5", [128, C5W], F32, side="right"))
        rep5 = stk5.enter_context(nc.sbuf_tensor("rep5", [128, 3 * W], F32, side="right"))
        RI = stk5.enter_context(nc.sbuf_tensor("RI", [128, NCK * 128], I32, side="right"))

        em.phase()
        pa_pump(pa_total)
        em.wait("v", sS)
        for mi, key in enumerate(["g", "u", "d"]):
            s0, e0 = mi * tiles_per_mat, (mi + 1) * tiles_per_mat
            em.opi("v", sV, lambda v, s0=s0, e0=e0: v.reduce_sum(
                STAT[:, 254:255], STAT[:, s0:e0], AXX))
            acquire(6)
            em.wait("t", sV)
            em.opi("t", sT, lambda t: t.matmul(
                psb[6][0:1, 0:1], cc(C_ONE), STAT[:, 254:255], start=True, stop=True))
            em.wait("v", sT)
            em.opi("v", sV, lambda v: v.tensor_copy(ROWS[0:1, 600:601], psb[6][0:1, 0:1]))
            release(6, sV)
            acquire(7)
            em.wait("t", sV)
            em.opi("t", sT, lambda t: t.matmul(
                psb[7][:, 0:1], ROWS[0:1, 0:128], ROWS[0:1, 600:601],
                start=True, stop=True))
            em.wait("v", sT)
            em.opi("v", sV, lambda v, key=key: v.tensor_copy(cc(C_ISW[key]), psb[7][:, 0:1]))
            release(7, sV)
            em.wait("v", sV)
            em.opi("v", sV, lambda v, key=key: v.tensor_single_scalar(
                cc(C_ISW[key]), cc(C_ISW[key]), 1.0 / (H * D), ALU.mult))
            em.wait("v", sV)
            em.opi("v", sV, lambda v, key=key: v.tensor_single_scalar(
                cc(C_ISW[key]), cc(C_ISW[key]), 1e-5, ALU.max))
            em.wait("v", sV)
            em.opi("v", sV, lambda v, key=key: v.tensor_single_scalar(
                cc(C_TAU[key]), cc(C_ISW[key]), 0.5, ALU.mult))
            em.wait("v", sV)
            em.opi("v", sV, lambda v, key=key: v.tensor_single_scalar(
                cc(C_NTAU[key]), cc(C_TAU[key]), -1.0, ALU.mult))
            em.wait("v", sV)

        S1c = rep5[:, 1 * W:2 * W]
        S2c = rep5[:, 2 * W:3 * W]
        stats_sums(Fx2[:, :], C5, S1c, S2c)
        m3 = rep5[:, 0 * W:1 * W]
        rrho = REPX[:, 0 * W:1 * W]
        FREP = REPX[:, 1 * W:2 * W]
        IXR = REPX[:, 2 * W:3 * W]
        tpc = S1c
        em.opi("v", sV, lambda v: v.tensor_single_scalar(m3, S1c, 1.0 / D, ALU.mult))
        em.opi("v", sV, lambda v: v.tensor_single_scalar(S2c, S2c, 1.0 / D, ALU.mult))
        em.opi("v", sV, lambda v: v.tensor_mul(tpc, m3, m3))
        em.opi("v", sV, lambda v: v.tensor_sub(S2c, S2c, tpc))
        em.wait("s", sV)
        em.opi("s", sS, lambda s: s.activation(rrho, S2c, ACT.Sqrt, bias=cc(C_EPS5)))
        em.wait("v", sS)
        em.opi("v", sV, lambda v: v.reciprocal(rrho, rrho))
        em.opi("v", sV, lambda v: v.tensor_mul(tpc, rrho, rrho))
        em.opi("v", sV, lambda v: v.tensor_mul(tpc, S2c, tpc))
        em.wait("s", sV)
        em.opi("s", sS, lambda s: s.activation(tpc, tpc, ACT.Sqrt, bias=cc(C_EPS6)))
        em.wait("v", sS)
        em.opi("v", sV, lambda v: v.reciprocal(tpc, tpc))
        em.opi("v", sV, lambda v: v.tensor_mul(rrho, rrho, tpc))

        MXP = C5[:, 0:W]
        MNP = C5[:, W:2 * W]
        TQ = C5[:, 2 * W:2 * W + NCK * 128]
        RF = C5[:, 2 * W + NCK * 128:2 * W + 2 * NCK * 128]
        FIX = C5[:, 2 * W + 2 * NCK * 128:2 * W + 3 * NCK * 128]
        assert 2 * W + 3 * NCK * 128 <= C5W

        def minmax_piece(s, e, op, dstP):
            wq = e - s
            for c in range(NCK):
                em.opi("v", sV, lambda v, c=c, s=s, wq=wq: v.tensor_sub(
                    TQ[:, c * 128:c * 128 + wq], Fx2[:, c * W + s:c * W + s + wq],
                    m3[:, s:s + wq]))
            n = NCK
            while n > 1:
                h2 = n // 2
                for c in range(h2):
                    em.opi("v", sV, lambda v, c=c, h2=h2, wq=wq, op=op: v.tensor_tensor(
                        TQ[:, c * 128:c * 128 + wq], TQ[:, c * 128:c * 128 + wq],
                        TQ[:, (c + h2) * 128:(c + h2) * 128 + wq], op))
                n = h2
            em.opi("v", sV, lambda v, s=s, wq=wq, dstP=dstP: v.tensor_copy(
                dstP[:, s:s + wq], TQ[:, 0:wq]))

        for (s, e) in NC_8:
            minmax_piece(s, e, ALU.max, MXP)
            minmax_piece(s, e, ALU.min, MNP)
        for j in range(NBLK):
            acquire(4)
            em.wait("t", sV)
            em.opi("t", sT, lambda t, j=j: t.transpose(
                psb[4][:, 0:128], MXP[:, R0 + j * 128:R0 + (j + 1) * 128], IDN[:]))
            em.wait("v", sT)
            em.opi("v", sV, lambda v, j=j: v.reduce_max(
                MAXC[:, j:j + 1], psb[4][:, 0:128], AXX))
            release(4, sV)
            acquire(5)
            em.wait("t", sV)
            em.opi("t", sT, lambda t, j=j: t.transpose(
                psb[5][:, 0:128], MNP[:, R0 + j * 128:R0 + (j + 1) * 128], IDN[:]))
            em.wait("v", sT)
            em.opi("v", sV, lambda v, j=j: v.tensor_reduce(
                MAXC[:, NBLK + j:NBLK + j + 1], psb[5][:, 0:128], AXX, ALU.min))
            release(5, sV)
        em.wait("v", sV)
        em.opi("v", sV, lambda v: v.tensor_single_scalar(
            MAXC[:, NBLK:2 * NBLK], MAXC[:, NBLK:2 * NBLK], -1.0, ALU.mult))
        em.wait("v", sV)
        em.opi("v", sV, lambda v: v.tensor_max(
            MAXC[:, 0:NBLK], MAXC[:, 0:NBLK], MAXC[:, NBLK:2 * NBLK]))
        em.wait("v", sV)
        cols_to_rep(NBLK, tpc, R0)
        em.opi("v", sV, lambda v: v.tensor_mul(tpc, tpc, rrho))
        em.opi("v", sV, lambda v: v.tensor_single_scalar(tpc, tpc, 1e-5, ALU.max))
        em.opi("v", sV, lambda v: v.tensor_single_scalar(IXR, tpc, 1.0 / 127.0, ALU.mult))
        em.opi("v", sV, lambda v: v.reciprocal(tpc, tpc))
        em.opi("v", sV, lambda v: v.tensor_mul(tpc, tpc, rrho))
        em.opi("v", sV, lambda v: v.tensor_single_scalar(FREP, tpc, 127.0, ALU.mult))

        em.wait("s", sV)
        em.dma("s", qA, x1_d.rearrange("(g p) t -> p g t", p=128),
               Fx2[:, :].rearrange("p (g t) -> p g t", g=NCK))
        x1_out_n = qA.n

        for (s, e) in NC_8:
            wq = e - s
            for c in range(NCK):
                em.opi("v", sV, lambda v, c=c, s=s, wq=wq: v.tensor_sub(
                    TQ[:, c * 128:c * 128 + wq], Fx2[:, c * W + s:c * W + s + wq],
                    m3[:, s:s + wq]))
                em.opi("v", sV, lambda v, c=c, s=s, wq=wq: v.tensor_mul(
                    TQ[:, c * 128:c * 128 + wq], TQ[:, c * 128:c * 128 + wq],
                    FREP[:, s:s + wq]))
            em.wait("s", sV)
            em.opi("s", sS, lambda s_: s_.copy(RI[:, :], TQ[:, 0:NCK * 128]))
            em.opi("s", sS, lambda s_: s_.copy(RF[:, 0:NCK * 128], RI[:, :]))
            em.wait("v", sS)
            em.opi("v", sV, lambda v: v.tensor_sub(
                TQ[:, 0:NCK * 128], TQ[:, 0:NCK * 128], RF[:, 0:NCK * 128]))
            em.opi("v", sV, lambda v: v.tensor_single_scalar(
                FIX[:, 0:NCK * 128], TQ[:, 0:NCK * 128], 0.5, ALU.is_gt))
            em.opi("v", sV, lambda v: v.tensor_add(
                RF[:, 0:NCK * 128], RF[:, 0:NCK * 128], FIX[:, 0:NCK * 128]))
            em.opi("v", sV, lambda v: v.tensor_scalar(
                TQ[:, 0:NCK * 128], TQ[:, 0:NCK * 128], -0.5, -1.0, ALU.is_ge, ALU.add))
            for c in range(NCK):
                em.opi("v", sV, lambda v, c=c, s=s, wq=wq: v.tensor_add(
                    XQ[:, c * W + s:c * W + s + wq], RF[:, c * 128:c * 128 + wq],
                    TQ[:, c * 128:c * 128 + wq]))
        if dbg == "xq":
            em.wait("s", sV)
            em.dma("s", qA, dbgq_t.rearrange("(g p) t -> p g t", p=128),
                   XQ[:, :].rearrange("p (g t) -> p g t", g=NCK))
        if dbg == "cc":
            em.wait("s", sV)
            em.dma("s", qA, dbgc_t[:, :], CC[:, :])
        em.flush()
        stk5.close()
        stkFx2.close()
        stkPA.close()

        stk6 = ExitStack()
        GW = stk6.enter_context(nc.sbuf_tensor("GW", [128, 2 * NCK * 128], F32,
                                               side="left"))
        QBF = [stk6.enter_context(nc.sbuf_tensor(f"QBF{i}", [128, NCK * 128], BF16,
                                                 side="left")) for i in range(2)]
        QTT = [stk6.enter_context(nc.sbuf_tensor(f"QTT{i}", [128, NCK * 128], F32,
                                                 side="left")) for i in range(2)]
        GSB = stk6.enter_context(nc.sbuf_tensor("GSB", [128, 2 * NRT], F32, side="right"))
        GUB = [stk6.enter_context(nc.sbuf_tensor(f"GUB{i}", [128, NRT], F32, side="right"))
               for i in range(2)]
        AC3 = stk6.enter_context(nc.sbuf_tensor("AC3", [128, 3 * NRT], F32, side="right"))
        gw_guard = Guard(2)
        qbf_guard = Guard(2)
        qtt_guard = Guard(1)
        gub_guard = Guard(2)

        gtf = gt.rearrange("a b c d -> (a b c d)")
        utf = ut.rearrange("a b c d -> (a b c d)")

        def quantize_gu_blk(src_flat, hb, half, wi, key):
            stg = GW[:, half * NCK * 128:(half + 1) * NCK * 128]
            gw_guard.use(em, "y", half)
            base = hb * 128 * NCK * 128
            em.dma("y", qW, stg, src_flat[base:base + 128 * NCK * 128]
                   .rearrange("(p m) -> p m", p=128))
            em.wait("v", qW)
            qtt_guard.use(em, "v", 0)
            em.opi("v", sV, lambda v, stg=stg, key=key: v.tensor_single_scalar(
                QTT[0][:, :], stg, cc(C_TAU[key]), ALU.is_ge))
            em.opi("v", sV, lambda v, stg=stg, key=key: v.tensor_scalar(
                QTT[1][:, :], stg, cc(C_NTAU[key]), -1.0, ALU.is_ge, ALU.add))
            gw_guard.done(half, sV)
            em.wait("g", sV)
            qbf_guard.use(em, "g", wi)
            em.opi("g", sG, lambda g, wi=wi: g.tensor_add(
                QBF[wi][:, :], QTT[0][:, :], QTT[1][:, :]))
            qtt_guard.done(0, sG)
            return sG.n

        em.phase()
        em.wait("y", qA, x1_out_n)
        em.wait("v", qA, x1_out_n)
        em.wait("g", qA, x1_out_n)
        em.wait("s", qA, x1_out_n)
        for hb in range(HBN):
            qg = quantize_gu_blk(gtf, hb, 0, 0, "g")
            qu = quantize_gu_blk(utf, hb, 1, 1, "u")
            gub_guard.use(em, "v", hb % 2)
            for ci, (s, e) in enumerate(NC_RE):
                wq = e - s
                so = s - R0
                b0 = (hb % 2) * 2
                b1 = b0 + 1
                acquire(b0)
                em.wait("t", sG, qg)
                mm_chain(b0, [(QBF[0][:, k * 128:(k + 1) * 128],
                               XQ[:, k * W + s:k * W + e]) for k in range(NCK)], wq)
                acquire(b1)
                em.wait("t", sG, qu)
                mm_chain(b1, [(QBF[1][:, k * 128:(k + 1) * 128],
                               XQ[:, k * W + s:k * W + e]) for k in range(NCK)], wq)
                em.wait("v", sT)
                em.opi("v", sV, lambda v, b0=b0, so=so, wq=wq, s=s: v.scalar_tensor_tensor(
                    GSB[:, so:so + wq], psb[b0][:, 0:wq], cc(C_ISW["g"]),
                    IXR[:, s:s + wq], ALU.mult, ALU.mult))
                em.opi("v", sV, lambda v, b1=b1, so=so, wq=wq, s=s: v.scalar_tensor_tensor(
                    GSB[:, NRT + so:NRT + so + wq], psb[b1][:, 0:wq], cc(C_ISW["u"]),
                    IXR[:, s:s + wq], ALU.mult, ALU.mult))
                release(b0, sV)
                release(b1, sV)
                em.wait("s", sV)
                em.opi("s", sS, lambda s_, so=so, wq=wq: s_.activation(
                    GSB[:, so:so + wq], GSB[:, so:so + wq], ACT.Silu))
                em.wait("v", sS)
                em.opi("v", sV, lambda v, so=so, wq=wq, hb=hb: v.tensor_mul(
                    GUB[hb % 2][:, so:so + wq], GSB[:, so:so + wq],
                    GSB[:, NRT + so:NRT + so + wq]))
            qbf_guard.done(0, sT)
            qbf_guard.done(1, sT)
            em.wait("s", sV)
            em.dma("s", qA, gu_d[hb, :, :], GUB[hb % 2][:, :])
            em.opi("s", sS, lambda s_, hb=hb: s_.activation(
                JNK[:, 0:NRT], GUB[hb % 2][:, :], ACT.Square))
            em.wait("v", sS)
            if hb == 0:
                em.opi("v", sV, lambda v: v.tensor_copy(AC3[:, 0:NRT], JNK[:, 0:NRT]))
                em.opi("v", sV, lambda v, hb=hb: v.tensor_copy(
                    AC3[:, NRT:2 * NRT], GUB[hb % 2][:, :]))
                em.opi("v", sV, lambda v, hb=hb: v.tensor_copy(
                    AC3[:, 2 * NRT:3 * NRT], GUB[hb % 2][:, :]))
            else:
                em.opi("v", sV, lambda v: v.tensor_add(
                    AC3[:, 0:NRT], AC3[:, 0:NRT], JNK[:, 0:NRT]))
                em.opi("v", sV, lambda v, hb=hb: v.tensor_max(
                    AC3[:, NRT:2 * NRT], AC3[:, NRT:2 * NRT], GUB[hb % 2][:, :]))
                em.opi("v", sV, lambda v, hb=hb: v.tensor_tensor(
                    AC3[:, 2 * NRT:3 * NRT], AC3[:, 2 * NRT:3 * NRT],
                    GUB[hb % 2][:, :], ALU.min))
            gub_guard.done(hb % 2, sV)

        SQ2 = rep5 if False else None
        rep_from_psum_row(AC3[:, 0:NRT], NRT, rrho[:, 0:1] if False else rrho)
        for j in range(NBLK):
            acquire(4)
            em.wait("t", sV)
            em.opi("t", sT, lambda t, j=j: t.transpose(
                psb[4][:, 0:128], AC3[:, NRT + j * 128:NRT + (j + 1) * 128], IDN[:]))
            em.wait("v", sT)
            em.opi("v", sV, lambda v, j=j: v.reduce_max(
                MAXC[:, j:j + 1], psb[4][:, 0:128], AXX))
            release(4, sV)
            acquire(5)
            em.wait("t", sV)
            em.opi("t", sT, lambda t, j=j: t.transpose(
                psb[5][:, 0:128], AC3[:, 2 * NRT + j * 128:2 * NRT + (j + 1) * 128],
                IDN[:]))
            em.wait("v", sT)
            em.opi("v", sV, lambda v, j=j: v.tensor_reduce(
                MAXC[:, NBLK + j:NBLK + j + 1], psb[5][:, 0:128], AXX, ALU.min))
            release(5, sV)
        em.wait("v", sV)
        em.opi("v", sV, lambda v: v.tensor_single_scalar(
            MAXC[:, NBLK:2 * NBLK], MAXC[:, NBLK:2 * NBLK], -1.0, ALU.mult))
        em.wait("v", sV)
        em.opi("v", sV, lambda v: v.tensor_max(
            MAXC[:, 0:NBLK], MAXC[:, 0:NBLK], MAXC[:, NBLK:2 * NBLK]))
        em.wait("v", sV)
        cols_to_rep(NBLK, FREP, 0)
        em.opi("v", sV, lambda v: v.tensor_single_scalar(
            rrho[:, 0:NRT], rrho[:, 0:NRT], 1.0 / H, ALU.mult))
        em.wait("s", sV)
        em.opi("s", sS, lambda s_: s_.activation(
            rrho[:, 0:NRT], rrho[:, 0:NRT], ACT.Sqrt, bias=cc(C_EPS6)))
        em.wait("v", sS)
        em.opi("v", sV, lambda v: v.reciprocal(rrho[:, 0:NRT], rrho[:, 0:NRT]))
        em.opi("v", sV, lambda v: v.tensor_mul(
            FREP[:, 0:NRT], FREP[:, 0:NRT], rrho[:, 0:NRT]))
        em.opi("v", sV, lambda v: v.tensor_single_scalar(
            FREP[:, 0:NRT], FREP[:, 0:NRT], 1e-5, ALU.max))
        em.opi("v", sV, lambda v: v.tensor_single_scalar(
            IXR[:, 0:NRT], FREP[:, 0:NRT], 1.0 / 127.0, ALU.mult))
        em.opi("v", sV, lambda v: v.reciprocal(FREP[:, 0:NRT], FREP[:, 0:NRT]))
        em.opi("v", sV, lambda v: v.tensor_mul(
            FREP[:, 0:NRT], FREP[:, 0:NRT], rrho[:, 0:NRT]))
        em.opi("v", sV, lambda v: v.tensor_single_scalar(
            FREP[:, 0:NRT], FREP[:, 0:NRT], 127.0, ALU.mult))
        if dbg == "gu":
            em.wait("s", sV)
            em.wait("s", qA)
            em.dma("s", qA, dbgg_t[:, :, :], gu_d[:, :, :])
        if dbg == "repx":
            em.wait("s", sV)
            em.dma("s", qA, dbgr_t[:, :], REPX[:, :])
        em.flush()
        stk6.close()
        stkXQ.close()

        stkFx3 = ExitStack()
        Fx3 = stkFx3.enter_context(nc.sbuf_tensor("Fx3", [128, NCK * W], F32, side="left"))
        stk7 = ExitStack()
        XQ2 = [stk7.enter_context(nc.sbuf_tensor(f"XQ2_{i}", [128, HBS * NRT], BF16,
                                                 side="right")) for i in range(2)]
        DW = [stk7.enter_context(nc.sbuf_tensor(f"DW{i}", [128, HBS * D], BF16,
                                                side="right")) for i in range(2)]
        DSTG = stk7.enter_context(nc.sbuf_tensor("DSTG", [128, D], F32, side="left"))
        QT7 = [stk7.enter_context(nc.sbuf_tensor(f"QT7_{i}", [128, D], F32, side="left"))
               for i in range(2)]
        GIN = [stk7.enter_context(nc.sbuf_tensor(f"GIN{i}", [128, NRT], F32, side="right"))
               for i in range(2)]
        RI2 = stk7.enter_context(nc.sbuf_tensor("RI2", [128, NRT], I32, side="right"))
        RF2 = stk7.enter_context(nc.sbuf_tensor("RF2", [128, NRT], F32, side="right"))
        ET7 = stk7.enter_context(nc.sbuf_tensor("ET7", [128, 512], F32, side="right"))

        dstg_guard = Guard(1)
        qt7_guard = Guard(1)
        dw_guard = Guard(2)
        xq2_guard = Guard(2)
        gin_guard = Guard(2)
        rif_guard = Guard(1)
        et7_guard = Guard(1)

        dtf = dtw.rearrange("a b -> (a b)")
        n_groups = HBN // HBG
        subs = HBG // HBS

        em.phase()
        em.wait("y", qA)
        em.wait("v", qA)
        em.wait("g", qA)
        em.wait("s", qA)
        em.dma("s", qA, Fx3[:, :].rearrange("p (g t) -> p g t", g=NCK),
               x1_d.rearrange("(g p) t -> p g t", p=128))
        x1_in_n = qA.n
        for gi in range(n_groups):
            sub_ready = []
            for si in range(subs):
                p = si % 2
                dw_guard.use(em, "g", p)
                xq2_guard.use(em, "v", p)
                for k in range(HBS):
                    hb = gi * HBG + si * HBS + k
                    dstg_guard.use(em, "y", 0)
                    base = hb * 128 * D
                    em.dma("y", qW, DSTG[:, :], dtf[base:base + 128 * D]
                           .rearrange("(p m) -> p m", p=128))
                    em.wait("v", qW)
                    qt7_guard.use(em, "v", 0)
                    em.opi("v", sV, lambda v: v.tensor_single_scalar(
                        QT7[0][:, :], DSTG[:, :], cc(C_TAU["d"]), ALU.is_ge))
                    em.opi("v", sV, lambda v: v.tensor_scalar(
                        QT7[1][:, :], DSTG[:, :], cc(C_NTAU["d"]), -1.0,
                        ALU.is_ge, ALU.add))
                    dstg_guard.done(0, sV)
                    em.wait("g", sV)
                    em.opi("g", sG, lambda g, p=p, k=k: g.tensor_add(
                        DW[p][:, k * D:(k + 1) * D], QT7[0][:, :], QT7[1][:, :]))
                    qt7_guard.done(0, sG)
                    gin_guard.use(em, "s", k % 2)
                    em.dma("s", qA, GIN[k % 2][:, :], gu_d[hb, :, :])
                    em.wait("v", qA)
                    em.opi("v", sV, lambda v, k=k: v.tensor_mul(
                        GIN[k % 2][:, :], GIN[k % 2][:, :], FREP[:, 0:NRT]))
                    rif_guard.use(em, "s", 0)
                    em.wait("s", sV)
                    em.opi("s", sS, lambda s_, k=k: s_.copy(RI2[:, :], GIN[k % 2][:, :]))
                    em.opi("s", sS, lambda s_: s_.copy(RF2[:, :], RI2[:, :]))
                    em.wait("v", sS)
                    em.opi("v", sV, lambda v, k=k: v.tensor_sub(
                        GIN[k % 2][:, :], GIN[k % 2][:, :], RF2[:, :]))
                    em.opi("v", sV, lambda v, k=k: v.tensor_single_scalar(
                        JNK[:, 0:NRT], GIN[k % 2][:, :], 0.5, ALU.is_gt))
                    em.opi("v", sV, lambda v: v.tensor_add(
                        RF2[:, :], RF2[:, :], JNK[:, 0:NRT]))
                    em.opi("v", sV, lambda v, k=k: v.tensor_scalar(
                        GIN[k % 2][:, :], GIN[k % 2][:, :], -0.5, -1.0,
                        ALU.is_ge, ALU.add))
                    em.opi("v", sV, lambda v, p=p, k=k: v.tensor_add(
                        XQ2[p][:, k * NRT:(k + 1) * NRT], RF2[:, :], GIN[k % 2][:, :]))
                    rif_guard.done(0, sV)
                    gin_guard.done(k % 2, sV)
                sub_ready.append((sG.n, sV.n))
                if gi == 0 and si == 0 and dbg == "xq2":
                    em.wait("s", sV)
                    em.dma("s", qA, dbgx2_t[:, :], XQ2[0][:, :])
                if gi == 0 and si == 0 and dbg == "dw":
                    em.wait("s", sG)
                    em.dma("s", qA, dbgdw_t[:, :], DW[0][:, :])
            for ci, (s, e) in enumerate(NC_RE):
                wq = e - s
                for dc in range(NCK):
                    bi = dc % 4
                    acquire(bi)
                    for si in range(subs):
                        gn, vn = sub_ready[si]
                        em.wait("t", sG, gn)
                        em.wait("t", sV, vn)
                        p = si % 2
                        so = s - R0
                        for k in range(HBS):
                            st = (si == 0 and k == 0)
                            sp = (si == subs - 1 and k == HBS - 1)
                            l = DW[p][:, k * D + dc * 128:k * D + (dc + 1) * 128]
                            r = XQ2[p][:, k * NRT + so:k * NRT + so + wq]
                            if sp:
                                em.opi("t", sT, lambda t, l=l, r=r, st=st, bi=bi, wq=wq:
                                       t.matmul(psb[bi][:, 0:wq], l, r, start=st,
                                                stop=True))
                            else:
                                em.op("t", lambda t, l=l, r=r, st=st, bi=bi, wq=wq:
                                      t.matmul(psb[bi][:, 0:wq], l, r, start=st,
                                               stop=False))
                    et7_guard.use(em, "v", 0)
                    em.wait("v", sT)
                    em.wait("v", qA, x1_in_n)
                    em.opi("v", sV, lambda v, bi=bi, wq=wq, s=s: v.scalar_tensor_tensor(
                        ET7[:, 0:wq], psb[bi][:, 0:wq], cc(C_ISW["d"]),
                        IXR[:, s - R0:s - R0 + wq], ALU.mult, ALU.mult))
                    em.opi("v", sV, lambda v, dc=dc, s=s, wq=wq: v.tensor_add(
                        Fx3[:, dc * W + s:dc * W + s + wq],
                        Fx3[:, dc * W + s:dc * W + s + wq], ET7[:, 0:wq]))
                    if dbg == "et7" and gi == 0 and ci == 0 and dc == 0:
                        em.wait("s", sV)
                        em.dma("s", qA, dbge_t[:, 0:wq], ET7[:, 0:wq])
                    if dbg == "draw" and gi == 0 and ci == 0 and dc == 0:
                        em.opi("v", sV, lambda v, bi=bi, wq=wq: v.tensor_copy(
                            JNK[:, 0:wq], psb[bi][:, 0:wq]))
                        em.wait("s", sV)
                        em.dma("s", qA, dbge_t[:, 0:wq], JNK[:, 0:wq])
                    release(bi, sV)
                    et7_guard.done(0, sV)
            for p in range(2):
                dw_guard.done(p, sT)
                xq2_guard.done(p, sT)
        em.flush()
        stk7.close()
        stkRX.close()

        stk8 = ExitStack()
        OUTT = [stk8.enter_context(nc.sbuf_tensor(f"OUTT{i}", [128, D], F32, side="right"))
                for i in range(2)]
        outt_guard = Guard(2)
        em.phase()
        em.wait("t", sV)
        for j in range(NBLK):
            ob = OUTT[j % 2]
            outt_guard.use(em, "v", j % 2)
            for g in range(NCK):
                bi = 4 + (g % 2)
                acquire(bi)
                em.opi("t", sT, lambda t, g=g, j=j, bi=bi: t.transpose(
                    psb[bi][:, 0:128],
                    Fx3[:, g * W + R0 + j * 128:g * W + R0 + (j + 1) * 128], IDN[:]))
                em.wait("v", sT)
                em.opi("v", sV, lambda v, g=g, bi=bi, ob=ob: v.tensor_copy(
                    ob[:, g * 128:(g + 1) * 128], psb[bi][:, 0:128]))
                release(bi, sV)
            em.wait("s", sV)
            em.dma("s", sO, y[j * 128:(j + 1) * 128, :], ob[:, :])
            outt_guard.done(j % 2, sO)
        tap("x2", Fx3)
        em.wait("y", sO)
        em.flush()
        stk8.close()
        stkFx3.close()

    return nc


_CACHE = {}


def _blk(Wt):
    Din, Dout = Wt.shape
    K, C = Din // 128, Dout // 128
    return np.ascontiguousarray(Wt.reshape(K, 128, C, 128).transpose(2, 1, 0, 3))


def kernel(L0_attn_x, L0_attn_ln1_g, L0_attn_ln1_b, L0_attn_ln2_g, L0_attn_ln2_b,
           L0_attn_tm_w, L0_attn_tm_u, L0_attn_tm_mu_r, L0_attn_tm_mu_k,
           L0_attn_tm_mu_v, L0_attn_tm_Wr, L0_attn_tm_Wk, L0_attn_tm_Wv,
           L0_attn_tm_Wout, L0_attn_cm_mu_r, L0_attn_cm_mu_k, L0_attn_cm_Wr,
           L0_attn_cm_Wk, L0_attn_cm_Wv, L0_ffn_w_gate, L0_ffn_w_up, L0_ffn_w_down):
    x = np.ascontiguousarray(np.asarray(L0_attn_x, dtype=np.float32))
    Bsz, T, D = x.shape
    H = np.asarray(L0_ffn_w_gate).shape[0]
    n_cores = 8
    NRT = (Bsz * T) // n_cores

    import os
    dbg = os.environ.get("KDBG") or None
    key = (D, H, NRT, dbg)
    if key not in _CACHE:
        _CACHE[key] = build(D, H, NRT, dbg)
    nc = _CACHE[key]

    def P(a):
        return np.ascontiguousarray(np.asarray(a, dtype=np.float32))

    wts = {
        "wr": _blk(P(L0_attn_tm_Wr).T), "wk": _blk(P(L0_attn_tm_Wk).T),
        "wv": _blk(P(L0_attn_tm_Wv).T), "wo": _blk(P(L0_attn_tm_Wout).T),
        "cwr": _blk(P(L0_attn_cm_Wr).T), "cwk": _blk(P(L0_attn_cm_Wk).T),
        "cwv": _blk(P(L0_attn_cm_Wv).T),
        "gt": _blk(P(L0_ffn_w_gate).T), "ut": _blk(P(L0_ffn_w_up).T),
        "dtw": np.ascontiguousarray(P(L0_ffn_w_down).T),
    }
    NCK = D // 128
    vecs = {
        "ln1g": P(L0_attn_ln1_g), "ln1b": P(L0_attn_ln1_b),
        "ln2g": P(L0_attn_ln2_g), "ln2b": P(L0_attn_ln2_b),
        "tmu": P(L0_attn_tm_u), "mur": P(L0_attn_tm_mu_r),
        "muk": P(L0_attn_tm_mu_k), "muv": P(L0_attn_tm_mu_v),
        "cmur": P(L0_attn_cm_mu_r), "cmuk": P(L0_attn_cm_mu_k),
    }

    halves = T // NRT
    in_maps = []
    for c in range(n_cores):
        b, half = c // halves, c % halves
        t0 = half * NRT
        if t0 == 0:
            sl = np.concatenate([np.zeros((2, D), np.float32), x[b, 0:NRT]], axis=0)
        else:
            sl = x[b, t0 - 2:t0 + NRT]
        m = {"xT": np.ascontiguousarray(sl.T),
             "maskd": np.full((128, 1), 0.0 if t0 == 0 else 1.0, np.float32)}
        m.update(wts)
        for k, v in vecs.items():
            m["v_" + k] = np.ascontiguousarray(v.reshape(NCK, 128))
        in_maps.append(m)

    res = run_bass_kernel_spmd(nc, in_maps, core_ids=list(range(n_cores)))
    if dbg:
        kernel._dbg = [np.asarray(res.results[c]["dbg"]) for c in range(n_cores)]
    out = np.empty((Bsz, T, D), np.float32)
    for c in range(n_cores):
        b, half = c // halves, c % halves
        out[b, half * NRT:(half + 1) * NRT] = np.asarray(res.results[c]["y"])
    return out


# version 07
# speedup vs baseline: 1.1211x; 1.1211x over previous
import numpy as np
from contextlib import ExitStack

import concourse.bass as bass
import concourse.mybir as mybir
from concourse.bass_utils import run_bass_kernel_spmd

F32 = mybir.dt.float32
BF16 = mybir.dt.bfloat16
I32 = mybir.dt.int32
ALU = mybir.AluOpType
ACT = mybir.ActivationFunctionType
AXX = mybir.AxisListType.X

LNEPS8 = 18.420680743952367


class Sem:
    def __init__(self, h):
        self.h = h
        self.n = 0

    def bump(self, k=1):
        self.n += k
        return self.n


class EM:
    ENGS = ("t", "v", "s", "g", "y")

    def __init__(self, nc):
        self.nc = nc
        self.cur = None

    def phase(self):
        self.cur = {e: [] for e in self.ENGS}

    def flush(self):
        ph, self.cur = self.cur, None

        def runner(ops):
            def go(e):
                for f in ops:
                    f(e)
            return go

        with self.nc.Block() as blk:
            if ph["y"]:
                blk.sync(runner(ph["y"]))
            if ph["t"]:
                blk.tensor(runner(ph["t"]))
            if ph["v"]:
                blk.vector(runner(ph["v"]))
            if ph["s"]:
                blk.scalar(runner(ph["s"]))
            if ph["g"]:
                blk.gpsimd(runner(ph["g"]))

    def op(self, eng, fn):
        self.cur[eng].append(fn)

    def wait(self, eng, sem, n=None):
        v = sem.n if n is None else n
        if v > 0:
            self.cur[eng].append(lambda e: e.wait_ge(sem.h, v))

    def opi(self, eng, sem, fn, k=1):
        n = sem.bump(k)
        self.cur[eng].append(lambda e: fn(e).then_inc(sem.h, k))
        return n

    def vs(self):
        self.wait("v", None) if False else None

    def dma(self, eng, sem, out_ap, in_ap, **kw):
        return self.opi(eng, sem, lambda e: e.dma_start(out=out_ap, in_=in_ap, **kw), k=16)


class Guard:
    def __init__(self, n):
        self.rec = [None] * n

    def use(self, em, eng, i):
        r = self.rec[i]
        if r is not None:
            em.wait(eng, r[0], r[1])
            self.rec[i] = None

    def done(self, i, sem):
        self.rec[i] = (sem, sem.n)


def build(D, H, NRT, dbg=None):
    NCK = D // 128
    HBN = H // 128
    W = NRT + 2
    R0 = 2
    NBLK = NRT // 128

    def tchunks(s0, e0, step):
        out, s = [], s0
        while s < e0:
            out.append((s, min(s + step, e0)))
            s += step
        return out

    NC_H1 = [(0, 2)] + tchunks(R0, W, 512)
    NC_RE = tchunks(R0, W, 512)
    NC_8 = tchunks(R0, W, 128)
    HBG = min(8, HBN)
    HBS = min(4, HBG)
    PAW = 2048
    while (H * D // 128) % PAW != 0:
        PAW //= 2

    nc = bass.Bass("TRN2", target_bir_lowering=False)

    xT = nc.dram_tensor("xT", [D, W], F32, kind="ExternalInput").ap()
    wnames = ["wr", "wk", "wv", "wo", "cwr", "cwk", "cwv"]
    wt = {n: nc.dram_tensor(n, [NCK, 128, NCK, 128], F32, kind="ExternalInput").ap()
          for n in wnames}
    gt = nc.dram_tensor("gt", [HBN, 128, NCK, 128], F32, kind="ExternalInput").ap()
    ut = nc.dram_tensor("ut", [HBN, 128, NCK, 128], F32, kind="ExternalInput").ap()
    dtw = nc.dram_tensor("dtw", [H, D], F32, kind="ExternalInput").ap()
    vnames = ["ln1g", "ln1b", "ln2g", "ln2b", "tmu", "mur", "muk", "muv", "cmur", "cmuk"]
    vin = {n: nc.dram_tensor("v_" + n, [NCK, 128], F32, kind="ExternalInput").ap()
           for n in vnames}
    maskd = nc.dram_tensor("maskd", [128, 1], F32, kind="ExternalInput").ap()
    y = nc.dram_tensor("y", [NRT, D], F32, kind="ExternalOutput").ap()

    dbg_t = (nc.dram_tensor("dbg", [D, W], F32, kind="ExternalOutput").ap()
             if dbg in ("a", "h", "xmid", "x1", "x2") else None)
    dbgq_t = (nc.dram_tensor("dbg", [D, W], BF16, kind="ExternalOutput").ap()
              if dbg == "xq" else None)
    dbgg_t = (nc.dram_tensor("dbg", [HBN, 128, NRT], F32, kind="ExternalOutput").ap()
              if dbg == "gu" else None)
    dbgc_t = (nc.dram_tensor("dbg", [128, 16], F32, kind="ExternalOutput").ap()
              if dbg == "cc" else None)
    dbgr_t = (nc.dram_tensor("dbg", [128, 3 * W], F32, kind="ExternalOutput").ap()
              if dbg == "repx" else None)
    dbgx2_t = (nc.dram_tensor("dbg", [128, HBS * NRT], BF16, kind="ExternalOutput").ap()
               if dbg == "xq2" else None)
    dbgdw_t = (nc.dram_tensor("dbg", [128, HBS * D], BF16, kind="ExternalOutput").ap()
               if dbg == "dw" else None)
    dbge_t = (nc.dram_tensor("dbg", [128, 512], F32, kind="ExternalOutput").ap()
              if dbg in ("et7", "draw") else None)

    a_d = nc.dram_tensor("a_d", [D, W], F32, kind="Internal").ap()
    x1_d = nc.dram_tensor("x1_d", [D, W], F32, kind="Internal").ap()
    gu_d = nc.dram_tensor("gu_d", [HBN, 128, NRT], F32, kind="Internal").ap()

    em = EM(nc)
    es = ExitStack()
    with es:
        qW = Sem(es.enter_context(nc.semaphore("qW")))
        qA = Sem(es.enter_context(nc.semaphore("qA")))
        sT = Sem(es.enter_context(nc.semaphore("sT")))
        sV = Sem(es.enter_context(nc.semaphore("sV")))
        sS = Sem(es.enter_context(nc.semaphore("sS")))
        sG = Sem(es.enter_context(nc.semaphore("sG")))
        sO = Sem(es.enter_context(nc.semaphore("sO")))

        VEC = es.enter_context(nc.sbuf_tensor("VEC", [128, 12 * NCK], F32))
        CC = es.enter_context(nc.sbuf_tensor("CC", [128, 16], F32))
        IDN = es.enter_context(nc.sbuf_tensor("IDN", [128, 128], F32))
        STAT = es.enter_context(nc.sbuf_tensor("STAT", [128, 256], F32))
        MAXC = es.enter_context(nc.sbuf_tensor("MAXC", [128, 2 * NBLK], F32))
        JNK = es.enter_context(nc.sbuf_tensor("JNK", [128, max(PAW, NRT, W)], F32))
        ROWS = es.enter_context(nc.sbuf_tensor("ROWS", [1, 640 + max(NRT, 128)], F32))
        psb = [es.enter_context(nc.psum_tensor(f"pb{i}", [128, 512], F32)) for i in range(8)]

        C_ONE, C_EPS5, C_EPS6, C_MASK = 0, 1, 2, 3
        C_TAU = {"g": 4, "u": 5, "d": 6}
        C_NTAU = {"g": 7, "u": 8, "d": 9}
        C_ISW = {"g": 10, "u": 11, "d": 12}

        def cc(i):
            return CC[:, i:i + 1]

        free_rec = [None] * 8

        def acquire(bi):
            if free_rec[bi] is not None:
                s, n = free_rec[bi]
                em.wait("t", s, n)
                free_rec[bi] = None

        def release(bi, sem):
            free_rec[bi] = (sem, sem.n)

        def mm_chain(bi, pairs, wq):
            pairs = pairs[::-1]
            L = len(pairs)
            for i, (l, r) in enumerate(pairs[:-1]):
                em.op("t", lambda t, l=l, r=r, i=i, bi=bi, wq=wq: t.matmul(
                    psb[bi][:, 0:wq], l, r, start=(i == 0), stop=False))
            l, r = pairs[-1]
            return em.opi("t", sT, lambda t, l=l, r=r, bi=bi, wq=wq, L=L: t.matmul(
                psb[bi][:, 0:wq], l, r, start=(L == 1), stop=True))

        def rep_from_psum_row(src_ap, W_, rep):
            for (s, e) in tchunks(0, W_, 512):
                acquire(6)
                em.wait("t", sV)
                em.opi("t", sT, lambda t, s=s, e=e: t.matmul(
                    psb[6][0:1, 0:e - s], cc(C_ONE), src_ap[:, s:e], start=True, stop=True))
                em.wait("v", sT)
                em.opi("v", sV, lambda v, s=s, e=e: v.tensor_copy(
                    rep[0:1, s:e], psb[6][0:1, 0:e - s]))
                release(6, sV)
            for (s, e) in tchunks(0, W_, 512):
                acquire(7)
                em.wait("t", sV)
                em.opi("t", sT, lambda t, s=s, e=e: t.matmul(
                    psb[7][:, 0:e - s], ROWS[0:1, 0:128], rep[0:1, s:e],
                    start=True, stop=True))
                em.wait("v", sT)
                em.opi("v", sV, lambda v, s=s, e=e: v.tensor_copy(
                    rep[:, s:e], psb[7][:, 0:e - s]))
                release(7, sV)

        def stats_sums(src, scratch, S1rep, S2rep):
            half = NCK // 2
            em.opi("v", sV, lambda v: v.tensor_add(
                scratch[:, 0:half * W], src[:, 0:half * W], src[:, half * W:NCK * W]))
            n = half
            while n > 1:
                h2 = n // 2
                em.opi("v", sV, lambda v, h2=h2, n=n: v.tensor_add(
                    scratch[:, 0:h2 * W], scratch[:, 0:h2 * W], scratch[:, h2 * W:n * W]))
                n = h2
            rep_from_psum_row(scratch[:, 0:W], W, S1rep)
            em.wait("s", sT)
            em.opi("s", sS, lambda s: s.activation(
                scratch[:, 0:half * W], src[:, 0:half * W], ACT.Square))
            for i in range(half):
                if i > 0:
                    em.wait("s", sV)
                em.opi("s", sS, lambda s, i=i: s.activation(
                    JNK[:, 0:W], src[:, (half + i) * W:(half + 1 + i) * W], ACT.Square))
                em.wait("v", sS)
                em.opi("v", sV, lambda v, i=i: v.tensor_add(
                    scratch[:, i * W:(i + 1) * W], scratch[:, i * W:(i + 1) * W],
                    JNK[:, 0:W]))
            n = half
            while n > 1:
                h2 = n // 2
                em.opi("v", sV, lambda v, h2=h2, n=n: v.tensor_add(
                    scratch[:, 0:h2 * W], scratch[:, 0:h2 * W], scratch[:, h2 * W:n * W]))
                n = h2
            rep_from_psum_row(scratch[:, 0:W], W, S2rep)

        def cols_to_rep(ncols, rep_ap, off):
            for j in range(ncols):
                acquire(6)
                em.wait("t", sV)
                em.opi("t", sT, lambda t, j=j: t.transpose(
                    psb[6][0:1, 0:128], MAXC[:, j:j + 1], IDN[:]))
                em.wait("v", sT)
                em.opi("v", sV, lambda v, j=j: v.tensor_copy(
                    ROWS[0:1, 640 + j * 128:640 + (j + 1) * 128], psb[6][0:1, 0:128]))
                release(6, sV)
            for j in range(ncols):
                acquire(7)
                em.wait("t", sV)
                em.opi("t", sT, lambda t, j=j: t.matmul(
                    psb[7][:, 0:128], ROWS[0:1, 0:128],
                    ROWS[0:1, 640 + j * 128:640 + (j + 1) * 128], start=True, stop=True))
                em.wait("v", sT)
                em.opi("v", sV, lambda v, j=j, off=off: v.tensor_copy(
                    rep_ap[:, off + j * 128:off + (j + 1) * 128], psb[7][:, 0:128]))
                release(7, sV)

        def tap(name, sb):
            if dbg == name:
                em.wait("s", sV)
                em.wait("s", sS)
                em.dma("s", qA, dbg_t.rearrange("(g p) t -> p g t", p=128),
                       sb[:, :].rearrange("p (g t) -> p g t", g=NCK))

        per_mat = H * D
        tiles_per_mat = per_mat // (128 * PAW)
        pa_srcs = [gt.rearrange("a b c d -> (a b c d)"),
                   ut.rearrange("a b c d -> (a b c d)"),
                   dtw.rearrange("a b -> (a b)")]
        pa_total = 3 * tiles_per_mat
        pa_state = {"i": 0}
        pa_guard = Guard(2)
        PA = [None, None]

        def pa_pump(nmax):
            while pa_state["i"] < pa_total and nmax > 0:
                i = pa_state["i"]
                mi, ti = i // tiles_per_mat, i % tiles_per_mat
                buf = PA[i % 2]
                pa_guard.use(em, "y", i % 2)
                base = ti * 128 * PAW
                em.dma("y", qW, buf[:, 0:PAW],
                       pa_srcs[mi][base:base + 128 * PAW].rearrange("(p m) -> p m", p=128))
                em.wait("s", qW)
                em.opi("s", sS, lambda s, buf=buf, i=i: s.activation(
                    buf[:, 0:PAW], buf[:, 0:PAW], ACT.Abs, accum_out=STAT[:, i:i + 1]))
                pa_guard.done(i % 2, sS)
                pa_state["i"] += 1
                nmax -= 1

        stkPA = ExitStack()
        PA[0] = stkPA.enter_context(nc.sbuf_tensor("PA0", [128, PAW], F32, side="left"))
        PA[1] = stkPA.enter_context(nc.sbuf_tensor("PA1", [128, PAW], F32, side="left"))
        stkFx1 = ExitStack()
        Fx1 = stkFx1.enter_context(nc.sbuf_tensor("Fx1", [128, NCK * W], F32, side="left"))
        repA = stkFx1.enter_context(nc.sbuf_tensor("repA", [128, 6 * W], F32, side="left"))
        stkC1 = ExitStack()
        C1 = stkC1.enter_context(nc.sbuf_tensor("C1", [128, (NCK // 2) * W], F32,
                                                side="right"))

        em.phase()
        em.opi("v", sV, lambda v: v.memset(CC[:, :], 0.0))
        em.wait("v", sV)
        em.opi("v", sV, lambda v: v.memset(cc(C_ONE), 1.0))
        em.opi("v", sV, lambda v: v.memset(cc(C_EPS5), 1e-5))
        em.opi("v", sV, lambda v: v.memset(cc(C_EPS6), 1e-6))
        em.opi("v", sV, lambda v: v.memset(ROWS[0:1, 0:128], 1.0))
        em.opi("g", sG, lambda g: g.iota(IDN[:], pattern=[[-1, 128]], base=0,
                                         channel_multiplier=1,
                                         allow_small_or_imprecise_dtypes=True))
        em.opi("g", sG, lambda g: g.tensor_single_scalar(IDN[:], IDN[:], 0.0, ALU.is_equal))
        vecs = {}
        for i, n in enumerate(vnames):
            t = VEC[:, i * NCK:(i + 1) * NCK]
            vecs[n] = t
            em.dma("s", qA, t, vin[n].rearrange("g p -> p g"),
                   allow_slow_non_contiguous=True)
        em.dma("s", qA, cc(C_MASK), maskd[:, :])
        em.dma("s", qA, Fx1[:, :].rearrange("p (g t) -> p g t", g=NCK),
               xT.rearrange("(g p) t -> p g t", p=128))
        em.wait("v", qA)
        UP = VEC[:, 10 * NCK:11 * NCK]
        em.opi("v", sV, lambda v: v.tensor_single_scalar(UP, vecs["tmu"], LNEPS8, ALU.add))
        pa_pump(8)

        S1 = repA[:, 0 * W:1 * W]
        S2 = repA[:, 1 * W:2 * W]
        stats_sums(Fx1[:, :], C1, S1, S2)
        m0 = repA[:, 2 * W:3 * W]
        r0 = repA[:, 3 * W:4 * W]
        r1 = repA[:, 4 * W:5 * W]
        tp = repA[:, 5 * W:6 * W]
        em.opi("v", sV, lambda v: v.tensor_single_scalar(m0, S1, 1.0 / D, ALU.mult))
        em.opi("v", sV, lambda v: v.tensor_single_scalar(S2, S2, 1.0 / D, ALU.mult))
        em.opi("v", sV, lambda v: v.tensor_mul(tp, m0, m0))
        em.opi("v", sV, lambda v: v.tensor_sub(S2, S2, tp))
        em.wait("s", sV)
        em.opi("s", sS, lambda s: s.activation(r0, S2, ACT.Sqrt, bias=cc(C_EPS5)))
        em.wait("v", sS)
        em.opi("v", sV, lambda v: v.reciprocal(r0, r0))
        em.opi("v", sV, lambda v: v.tensor_mul(tp, r0, r0))
        em.opi("v", sV, lambda v: v.tensor_mul(tp, S2, tp))
        em.wait("s", sV)
        em.opi("s", sS, lambda s: s.activation(r1, tp, ACT.Sqrt, bias=cc(C_EPS5)))
        em.wait("v", sS)
        em.opi("v", sV, lambda v: v.reciprocal(r1, r1))
        pa_pump(4)
        em.flush()
        stkC1.close()

        stkB = ExitStack()
        Bh = stkB.enter_context(nc.sbuf_tensor("Bh", [128, NCK * W], F32, side="right"))
        em.phase()
        for c in range(NCK):
            sl = slice(c * W, (c + 1) * W)
            em.opi("v", sV, lambda v, sl=sl: v.tensor_sub(Fx1[:, sl], Fx1[:, sl], m0))
            em.opi("v", sV, lambda v, sl=sl: v.tensor_mul(Fx1[:, sl], Fx1[:, sl], r0))
        em.wait("s", sV)
        em.dma("s", qA, a_d.rearrange("(g p) t -> p g t", p=128),
               Fx1[:, :].rearrange("p (g t) -> p g t", g=NCK))
        a_out_n = qA.n
        for c in range(NCK):
            sl = slice(c * W, (c + 1) * W)
            em.opi("v", sV, lambda v, sl=sl: v.tensor_mul(Bh[:, sl], Fx1[:, sl], r1))
            em.opi("v", sV, lambda v, c=c, sl=sl: v.tensor_scalar(
                Bh[:, sl], Bh[:, sl], vecs["ln1g"][:, c:c + 1],
                vecs["ln1b"][:, c:c + 1], ALU.mult, ALU.add))
            em.wait("v", sV)
            em.opi("v", sV, lambda v, c=c: v.tensor_single_scalar(
                Bh[:, c * W:c * W + 2], Bh[:, c * W:c * W + 2], cc(C_MASK), ALU.mult))
        tap("a", Fx1)
        tap("h", Bh)
        pa_pump(8)
        em.flush()
        stkFx1.close()

        stkT = ExitStack()
        TRIO = stkT.enter_context(nc.sbuf_tensor("TRIO", [128, 3 * NCK * W], BF16,
                                                 side="left"))

        def trio_ap(i, c, s, e):
            return TRIO[:, (i * NCK + c) * W + s:(i * NCK + c) * W + e]

        em.phase()
        em.wait("v", qA)
        for c in range(NCK):
            em.opi("v", sV, lambda v, c=c: v.tensor_sub(
                JNK[:, 0:W - 1], Bh[:, c * W + 1:(c + 1) * W],
                Bh[:, c * W:(c + 1) * W - 1]))
            for i, mu in enumerate(["mur", "muk", "muv"]):
                em.opi("v", sV, lambda v, c=c, i=i, mu=mu: v.scalar_tensor_tensor(
                    trio_ap(i, c, 1, W), JNK[:, 0:W - 1], vecs[mu][:, c:c + 1],
                    Bh[:, c * W:(c + 1) * W - 1], ALU.mult, ALU.add))
                em.opi("v", sV, lambda v, c=c, i=i: v.tensor_copy(
                    trio_ap(i, c, 0, 1), Bh[:, c * W:c * W + 1]))
        pa_pump(4)
        em.flush()
        stkB.close()

        stk3w = ExitStack()
        WSTG = stk3w.enter_context(nc.sbuf_tensor("WSTG", [128, NCK * 128], F32,
                                                  side="right"))
        WBF = [stk3w.enter_context(nc.sbuf_tensor(f"WBF{i}", [128, NCK * 128], BF16,
                                                  side="right")) for i in range(6)]
        ET = [stk3w.enter_context(nc.sbuf_tensor(f"ET{i}", [128, 512], F32, side="right"))
              for i in range(2)]
        stkR = ExitStack()
        RWKV = stkR.enter_context(nc.sbuf_tensor("RWKV", [128, NCK * W], BF16,
                                                 side="right"))
        wstg_guard = Guard(1)
        wbf_guard = Guard(6)
        et_guard = Guard(1)

        def stream_wblk(src_ap, dc, mat):
            wi = mat * 2 + dc % 2
            stg = WSTG[:, :]
            wstg_guard.use(em, "y", 0)
            em.dma("y", qW, stg, src_ap[dc, :, :, :].rearrange("p k j -> p (k j)"))
            em.wait("v", qW)
            wbf_guard.use(em, "v", wi)
            em.opi("v", sV, lambda v, stg=stg, wi=wi: v.tensor_copy(WBF[wi][:, :], stg))
            wstg_guard.done(0, sV)
            return wi, sV.n

        em.phase()
        em.wait("y", qA)
        em.wait("v", qA)
        for dc in range(NCK):
            w_r, c_r = stream_wblk(wt["wr"], dc, 0)
            w_k, c_k = stream_wblk(wt["wk"], dc, 1)
            w_v, c_v = stream_wblk(wt["wv"], dc, 2)
            b0, b1, b2 = (dc % 2) * 3, (dc % 2) * 3 + 1, (dc % 2) * 3 + 2
            for (s, e) in NC_H1:
                wq = e - s
                acquire(b0)
                em.wait("t", sV, c_r)
                mm_chain(b0, [(WBF[w_r][:, k * 128:(k + 1) * 128], trio_ap(0, k, s, e))
                              for k in range(NCK)], wq)
                acquire(b1)
                em.wait("t", sV, c_k)
                mm_chain(b1, [(WBF[w_k][:, k * 128:(k + 1) * 128], trio_ap(1, k, s, e))
                              for k in range(NCK)], wq)
                acquire(b2)
                em.wait("t", sV, c_v)
                mm_chain(b2, [(WBF[w_v][:, k * 128:(k + 1) * 128], trio_ap(2, k, s, e))
                              for k in range(NCK)], wq)
                et_guard.use(em, "s", 0)
                em.wait("s", sT)
                em.opi("s", sS, lambda s_, b0=b0, wq=wq: s_.activation(
                    ET[0][:, 0:wq], psb[b0][:, 0:wq], ACT.Sigmoid))
                em.opi("s", sS, lambda s_, b1=b1, wq=wq, dc=dc: s_.activation(
                    ET[1][:, 0:wq], psb[b1][:, 0:wq], ACT.Sigmoid, bias=UP[:, dc:dc + 1]))
                release(b0, sS)
                release(b1, sS)
                em.wait("v", sS)
                em.opi("v", sV, lambda v, wq=wq: v.tensor_mul(
                    ET[0][:, 0:wq], ET[0][:, 0:wq], ET[1][:, 0:wq]))
                em.opi("v", sV, lambda v, b2=b2, dc=dc, s=s, e=e, wq=wq: v.tensor_mul(
                    RWKV[:, dc * W + s:dc * W + e], ET[0][:, 0:wq], psb[b2][:, 0:wq]))
                release(b2, sV)
                et_guard.done(0, sV)
            for wi in (w_r, w_k, w_v):
                wbf_guard.done(wi, sT)
            pa_pump(2)
        rwkv_n = sV.n
        em.flush()
        stkT.close()

        stkFx2 = ExitStack()
        Fx2 = stkFx2.enter_context(nc.sbuf_tensor("Fx2", [128, NCK * W], F32, side="left"))
        em.phase()
        em.wait("s", qA, a_out_n)
        em.dma("s", qA, Fx2[:, :].rearrange("p (g t) -> p g t", g=NCK),
               a_d.rearrange("(g p) t -> p g t", p=128))
        a_in_n = qA.n
        for dc in range(NCK):
            w_o, c_o = stream_wblk(wt["wo"], dc, 0)
            bi = dc % 4
            for (s, e) in NC_H1:
                wq = e - s
                acquire(bi)
                em.wait("t", sV, c_o)
                em.wait("t", sV, rwkv_n)
                mm_chain(bi, [(WBF[w_o][:, k * 128:(k + 1) * 128],
                               RWKV[:, k * W + s:k * W + e]) for k in range(NCK)], wq)
                em.wait("v", sT)
                em.wait("v", qA, a_in_n)
                em.opi("v", sV, lambda v, dc=dc, s=s, wq=wq, bi=bi: v.tensor_add(
                    Fx2[:, dc * W + s:dc * W + s + wq],
                    Fx2[:, dc * W + s:dc * W + s + wq], psb[bi][:, 0:wq]))
                release(bi, sV)
            wbf_guard.done(w_o, sT)
            pa_pump(2)
        tap("xmid", Fx2)
        em.flush()
        stkR.close()

        stkM2 = ExitStack()
        rep4 = stkM2.enter_context(nc.sbuf_tensor("rep4", [128, 2 * W], F32, side="right"))
        stk4s = ExitStack()
        C4 = stk4s.enter_context(nc.sbuf_tensor("C4", [128, (NCK // 2) * W], F32,
                                                side="right"))
        repS = stk4s.enter_context(nc.sbuf_tensor("repS", [128, 2 * W], F32, side="right"))
        em.phase()
        S1b = repS[:, 0 * W:1 * W]
        S2b = repS[:, 1 * W:2 * W]
        stats_sums(Fx2[:, :], C4, S1b, S2b)
        m2 = rep4[:, 0 * W:1 * W]
        r2l = rep4[:, 1 * W:2 * W]
        em.opi("v", sV, lambda v: v.tensor_single_scalar(m2, S1b, 1.0 / D, ALU.mult))
        em.opi("v", sV, lambda v: v.tensor_single_scalar(S2b, S2b, 1.0 / D, ALU.mult))
        em.opi("v", sV, lambda v: v.tensor_mul(S1b, m2, m2))
        em.opi("v", sV, lambda v: v.tensor_sub(S2b, S2b, S1b))
        em.wait("s", sV)
        em.opi("s", sS, lambda s: s.activation(r2l, S2b, ACT.Sqrt, bias=cc(C_EPS5)))
        em.wait("v", sS)
        em.opi("v", sV, lambda v: v.reciprocal(r2l, r2l))
        em.flush()
        stk4s.close()

        stk4 = ExitStack()
        H2Q = stk4.enter_context(nc.sbuf_tensor("H2Q", [128, NCK * 513], F32, side="right"))
        T2Q = stk4.enter_context(nc.sbuf_tensor("T2Q", [128, NCK * 512], BF16, side="right"))
        K2Q = stk4.enter_context(nc.sbuf_tensor("K2Q", [128, NCK * 512], BF16, side="right"))
        em.phase()
        STASH = VEC[:, 11 * NCK:12 * NCK]
        for ci, (s, e) in enumerate(NC_RE):
            if ci == 0:
                continue
            for c in range(NCK):
                em.opi("v", sV, lambda v, c=c, s=s: v.tensor_copy(
                    STASH[:, c:c + 1], Fx2[:, c * W + s - 1:c * W + s]))
        for ci, (s, e) in enumerate(NC_RE):
            wq = e - s
            em.wait("v", sT)
            em.wait("s", sT)
            if ci == 0:
                for c in range(NCK):
                    em.opi("v", sV, lambda v, c=c, s=s, e=e, wq=wq: v.tensor_sub(
                        H2Q[:, c * 513:c * 513 + wq + 1],
                        Fx2[:, c * W + s - 1:c * W + e], m2[:, s - 1:e]))
                    em.opi("v", sV, lambda v, c=c, s=s, e=e, wq=wq: v.tensor_mul(
                        H2Q[:, c * 513:c * 513 + wq + 1],
                        H2Q[:, c * 513:c * 513 + wq + 1], r2l[:, s - 1:e]))
                    em.opi("v", sV, lambda v, c=c, wq=wq: v.tensor_scalar(
                        H2Q[:, c * 513:c * 513 + wq + 1],
                        H2Q[:, c * 513:c * 513 + wq + 1],
                        vecs["ln2g"][:, c:c + 1], vecs["ln2b"][:, c:c + 1],
                        ALU.mult, ALU.add))
                    em.wait("v", sV)
                    em.opi("v", sV, lambda v, c=c: v.tensor_single_scalar(
                        H2Q[:, c * 513:c * 513 + 1], H2Q[:, c * 513:c * 513 + 1],
                        cc(C_MASK), ALU.mult))
            else:
                for c in range(NCK):
                    em.opi("v", sV, lambda v, c=c, s=s, e=e, wq=wq: v.tensor_sub(
                        H2Q[:, c * 513 + 1:c * 513 + wq + 1],
                        Fx2[:, c * W + s:c * W + e], m2[:, s:e]))
                    em.opi("v", sV, lambda v, c=c, s=s, e=e, wq=wq: v.tensor_mul(
                        H2Q[:, c * 513 + 1:c * 513 + wq + 1],
                        H2Q[:, c * 513 + 1:c * 513 + wq + 1], r2l[:, s:e]))
                    em.opi("v", sV, lambda v, c=c, wq=wq: v.tensor_scalar(
                        H2Q[:, c * 513 + 1:c * 513 + wq + 1],
                        H2Q[:, c * 513 + 1:c * 513 + wq + 1],
                        vecs["ln2g"][:, c:c + 1], vecs["ln2b"][:, c:c + 1],
                        ALU.mult, ALU.add))
                em.wait("v", sV)
                em.opi("v", sV, lambda v, s=s: v.tensor_single_scalar(
                    JNK[:, 0:NCK], STASH[:, 0:NCK], m2[:, s - 1:s], ALU.subtract))
                em.wait("v", sV)
                em.opi("v", sV, lambda v, s=s: v.tensor_single_scalar(
                    JNK[:, 0:NCK], JNK[:, 0:NCK], r2l[:, s - 1:s], ALU.mult))
                em.wait("v", sV)
                for c in range(NCK):
                    em.opi("v", sV, lambda v, c=c: v.tensor_scalar(
                        H2Q[:, c * 513:c * 513 + 1], JNK[:, c:c + 1],
                        vecs["ln2g"][:, c:c + 1], vecs["ln2b"][:, c:c + 1],
                        ALU.mult, ALU.add))
                em.wait("v", sV)
            for c in range(NCK):
                em.opi("v", sV, lambda v, c=c, wq=wq: v.tensor_sub(
                    JNK[:, 0:wq], H2Q[:, c * 513 + 1:c * 513 + 1 + wq],
                    H2Q[:, c * 513:c * 513 + wq]))
                em.opi("v", sV, lambda v, c=c, wq=wq: v.scalar_tensor_tensor(
                    T2Q[:, c * 512:c * 512 + wq], JNK[:, 0:wq],
                    vecs["cmuk"][:, c:c + 1], H2Q[:, c * 513:c * 513 + wq],
                    ALU.mult, ALU.add))
            xk2_n = sV.n
            for dc in range(NCK):
                w_w, c_w = stream_wblk(wt["cwk"], dc, 0)
                bi = dc % 4
                acquire(bi)
                em.wait("t", sV, max(c_w, xk2_n))
                mm_chain(bi, [(WBF[w_w][:, k * 128:(k + 1) * 128],
                               T2Q[:, k * 512:k * 512 + wq]) for k in range(NCK)], wq)
                et_guard.use(em, "v", 0)
                em.wait("v", sT)
                em.opi("v", sV, lambda v, bi=bi, wq=wq: v.tensor_single_scalar(
                    ET[0][:, 0:wq], psb[bi][:, 0:wq], 0.0, ALU.max))
                release(bi, sV)
                em.wait("s", sV)
                em.opi("s", sS, lambda s_, dc=dc, wq=wq: s_.activation(
                    K2Q[:, dc * 512:dc * 512 + wq], ET[0][:, 0:wq], ACT.Square))
                et_guard.done(0, sS)
                wbf_guard.done(w_w, sT)
            k2_n = sS.n
            em.wait("v", sT)
            for c in range(NCK):
                em.opi("v", sV, lambda v, c=c, wq=wq: v.tensor_sub(
                    JNK[:, 0:wq], H2Q[:, c * 513 + 1:c * 513 + 1 + wq],
                    H2Q[:, c * 513:c * 513 + wq]))
                em.opi("v", sV, lambda v, c=c, wq=wq: v.scalar_tensor_tensor(
                    T2Q[:, c * 512:c * 512 + wq], JNK[:, 0:wq],
                    vecs["cmur"][:, c:c + 1], H2Q[:, c * 513:c * 513 + wq],
                    ALU.mult, ALU.add))
            xr2_n = sV.n
            for dc in range(NCK):
                w_r2, c_r2 = stream_wblk(wt["cwr"], dc, 0)
                w_v2, c_v2 = stream_wblk(wt["cwv"], dc, 1)
                b0, b1 = (dc % 2) * 2, (dc % 2) * 2 + 1
                acquire(b0)
                em.wait("t", sV, max(c_r2, xr2_n))
                mm_chain(b0, [(WBF[w_r2][:, k * 128:(k + 1) * 128],
                               T2Q[:, k * 512:k * 512 + wq]) for k in range(NCK)], wq)
                acquire(b1)
                em.wait("t", sV, c_v2)
                em.wait("t", sS, k2_n)
                mm_chain(b1, [(WBF[w_v2][:, k * 128:(k + 1) * 128],
                               K2Q[:, k * 512:k * 512 + wq]) for k in range(NCK)], wq)
                et_guard.use(em, "s", 0)
                em.wait("s", sT)
                em.opi("s", sS, lambda s_, b0=b0, wq=wq: s_.activation(
                    ET[0][:, 0:wq], psb[b0][:, 0:wq], ACT.Sigmoid))
                release(b0, sS)
                em.wait("v", sS)
                em.opi("v", sV, lambda v, b1=b1, wq=wq: v.tensor_mul(
                    ET[0][:, 0:wq], ET[0][:, 0:wq], psb[b1][:, 0:wq]))
                em.opi("v", sV, lambda v, dc=dc, s=s, wq=wq: v.tensor_add(
                    Fx2[:, dc * W + s:dc * W + s + wq],
                    Fx2[:, dc * W + s:dc * W + s + wq], ET[0][:, 0:wq]))
                release(b1, sV)
                et_guard.done(0, sV)
                wbf_guard.done(w_r2, sT)
                wbf_guard.done(w_v2, sT)
            pa_pump(4)
        for c in range(NCK):
            em.wait("s", sV)
            em.dma("s", qA, JNK[:, 0:W], xT[c * 128:(c + 1) * 128, :])
            em.wait("v", qA)
            em.opi("v", sV, lambda v, c=c: v.tensor_add(
                Fx2[:, c * W:(c + 1) * W], Fx2[:, c * W:(c + 1) * W], JNK[:, 0:W]))
        tap("x1", Fx2)
        em.flush()
        stk4.close()
        stkM2.close()
        stk3w.close()

        stkRX = ExitStack()
        REPX = stkRX.enter_context(nc.sbuf_tensor("REPX", [128, 3 * W], F32, side="right"))
        stkXQ = ExitStack()
        XQ = stkXQ.enter_context(nc.sbuf_tensor("XQ", [128, NCK * W], BF16, side="right"))
        stk5 = ExitStack()
        C5W = max((NCK // 2) * W, 2 * W + 3 * NCK * 128)
        C5 = stk5.enter_context(nc.sbuf_tensor("C5", [128, C5W], F32, side="right"))
        rep5 = stk5.enter_context(nc.sbuf_tensor("rep5", [128, 3 * W], F32, side="right"))
        RI = stk5.enter_context(nc.sbuf_tensor("RI", [128, NCK * 128], I32, side="right"))

        em.phase()
        pa_pump(pa_total)
        em.wait("v", sS)
        for mi, key in enumerate(["g", "u", "d"]):
            s0, e0 = mi * tiles_per_mat, (mi + 1) * tiles_per_mat
            em.opi("v", sV, lambda v, s0=s0, e0=e0: v.reduce_sum(
                STAT[:, 254:255], STAT[:, s0:e0], AXX))
            acquire(6)
            em.wait("t", sV)
            em.opi("t", sT, lambda t: t.matmul(
                psb[6][0:1, 0:1], cc(C_ONE), STAT[:, 254:255], start=True, stop=True))
            em.wait("v", sT)
            em.opi("v", sV, lambda v: v.tensor_copy(ROWS[0:1, 600:601], psb[6][0:1, 0:1]))
            release(6, sV)
            acquire(7)
            em.wait("t", sV)
            em.opi("t", sT, lambda t: t.matmul(
                psb[7][:, 0:1], ROWS[0:1, 0:128], ROWS[0:1, 600:601],
                start=True, stop=True))
            em.wait("v", sT)
            em.opi("v", sV, lambda v, key=key: v.tensor_copy(cc(C_ISW[key]), psb[7][:, 0:1]))
            release(7, sV)
            em.wait("v", sV)
            em.opi("v", sV, lambda v, key=key: v.tensor_single_scalar(
                cc(C_ISW[key]), cc(C_ISW[key]), 1.0 / (H * D), ALU.mult))
            em.wait("v", sV)
            em.opi("v", sV, lambda v, key=key: v.tensor_single_scalar(
                cc(C_ISW[key]), cc(C_ISW[key]), 1e-5, ALU.max))
            em.wait("v", sV)
            em.opi("v", sV, lambda v, key=key: v.tensor_single_scalar(
                cc(C_TAU[key]), cc(C_ISW[key]), 0.5, ALU.mult))
            em.wait("v", sV)
            em.opi("v", sV, lambda v, key=key: v.tensor_single_scalar(
                cc(C_NTAU[key]), cc(C_TAU[key]), -1.0, ALU.mult))
            em.wait("v", sV)

        S1c = rep5[:, 1 * W:2 * W]
        S2c = rep5[:, 2 * W:3 * W]
        stats_sums(Fx2[:, :], C5, S1c, S2c)
        m3 = rep5[:, 0 * W:1 * W]
        rrho = REPX[:, 0 * W:1 * W]
        FREP = REPX[:, 1 * W:2 * W]
        IXR = REPX[:, 2 * W:3 * W]
        tpc = S1c
        em.opi("v", sV, lambda v: v.tensor_single_scalar(m3, S1c, 1.0 / D, ALU.mult))
        em.opi("v", sV, lambda v: v.tensor_single_scalar(S2c, S2c, 1.0 / D, ALU.mult))
        em.opi("v", sV, lambda v: v.tensor_mul(tpc, m3, m3))
        em.opi("v", sV, lambda v: v.tensor_sub(S2c, S2c, tpc))
        em.wait("s", sV)
        em.opi("s", sS, lambda s: s.activation(rrho, S2c, ACT.Sqrt, bias=cc(C_EPS5)))
        em.wait("v", sS)
        em.opi("v", sV, lambda v: v.reciprocal(rrho, rrho))
        em.opi("v", sV, lambda v: v.tensor_mul(tpc, rrho, rrho))
        em.opi("v", sV, lambda v: v.tensor_mul(tpc, S2c, tpc))
        em.wait("s", sV)
        em.opi("s", sS, lambda s: s.activation(tpc, tpc, ACT.Sqrt, bias=cc(C_EPS6)))
        em.wait("v", sS)
        em.opi("v", sV, lambda v: v.reciprocal(tpc, tpc))
        em.opi("v", sV, lambda v: v.tensor_mul(rrho, rrho, tpc))

        MXP = C5[:, 0:W]
        MNP = C5[:, W:2 * W]
        TQ = C5[:, 2 * W:2 * W + NCK * 128]
        RF = C5[:, 2 * W + NCK * 128:2 * W + 2 * NCK * 128]
        FIX = C5[:, 2 * W + 2 * NCK * 128:2 * W + 3 * NCK * 128]
        assert 2 * W + 3 * NCK * 128 <= C5W

        def minmax_piece(s, e, op, dstP):
            wq = e - s
            for c in range(NCK):
                em.opi("v", sV, lambda v, c=c, s=s, wq=wq: v.tensor_sub(
                    TQ[:, c * 128:c * 128 + wq], Fx2[:, c * W + s:c * W + s + wq],
                    m3[:, s:s + wq]))
            n = NCK
            while n > 1:
                h2 = n // 2
                for c in range(h2):
                    em.opi("v", sV, lambda v, c=c, h2=h2, wq=wq, op=op: v.tensor_tensor(
                        TQ[:, c * 128:c * 128 + wq], TQ[:, c * 128:c * 128 + wq],
                        TQ[:, (c + h2) * 128:(c + h2) * 128 + wq], op))
                n = h2
            em.opi("v", sV, lambda v, s=s, wq=wq, dstP=dstP: v.tensor_copy(
                dstP[:, s:s + wq], TQ[:, 0:wq]))

        for (s, e) in NC_8:
            minmax_piece(s, e, ALU.max, MXP)
            minmax_piece(s, e, ALU.min, MNP)
        for j in range(NBLK):
            acquire(4)
            em.wait("t", sV)
            em.opi("t", sT, lambda t, j=j: t.transpose(
                psb[4][:, 0:128], MXP[:, R0 + j * 128:R0 + (j + 1) * 128], IDN[:]))
            em.wait("v", sT)
            em.opi("v", sV, lambda v, j=j: v.reduce_max(
                MAXC[:, j:j + 1], psb[4][:, 0:128], AXX))
            release(4, sV)
            acquire(5)
            em.wait("t", sV)
            em.opi("t", sT, lambda t, j=j: t.transpose(
                psb[5][:, 0:128], MNP[:, R0 + j * 128:R0 + (j + 1) * 128], IDN[:]))
            em.wait("v", sT)
            em.opi("v", sV, lambda v, j=j: v.tensor_reduce(
                MAXC[:, NBLK + j:NBLK + j + 1], psb[5][:, 0:128], AXX, ALU.min))
            release(5, sV)
        em.wait("v", sV)
        em.opi("v", sV, lambda v: v.tensor_single_scalar(
            MAXC[:, NBLK:2 * NBLK], MAXC[:, NBLK:2 * NBLK], -1.0, ALU.mult))
        em.wait("v", sV)
        em.opi("v", sV, lambda v: v.tensor_max(
            MAXC[:, 0:NBLK], MAXC[:, 0:NBLK], MAXC[:, NBLK:2 * NBLK]))
        em.wait("v", sV)
        cols_to_rep(NBLK, tpc, R0)
        em.opi("v", sV, lambda v: v.tensor_mul(tpc, tpc, rrho))
        em.opi("v", sV, lambda v: v.tensor_single_scalar(tpc, tpc, 1e-5, ALU.max))
        em.opi("v", sV, lambda v: v.tensor_single_scalar(IXR, tpc, 1.0 / 127.0, ALU.mult))
        em.opi("v", sV, lambda v: v.reciprocal(tpc, tpc))
        em.opi("v", sV, lambda v: v.tensor_mul(tpc, tpc, rrho))
        em.opi("v", sV, lambda v: v.tensor_single_scalar(FREP, tpc, 127.0, ALU.mult))

        em.wait("s", sV)
        em.dma("s", qA, x1_d.rearrange("(g p) t -> p g t", p=128),
               Fx2[:, :].rearrange("p (g t) -> p g t", g=NCK))
        x1_out_n = qA.n

        for (s, e) in NC_8:
            wq = e - s
            for c in range(NCK):
                em.opi("v", sV, lambda v, c=c, s=s, wq=wq: v.tensor_sub(
                    TQ[:, c * 128:c * 128 + wq], Fx2[:, c * W + s:c * W + s + wq],
                    m3[:, s:s + wq]))
                em.opi("v", sV, lambda v, c=c, s=s, wq=wq: v.tensor_mul(
                    TQ[:, c * 128:c * 128 + wq], TQ[:, c * 128:c * 128 + wq],
                    FREP[:, s:s + wq]))
            em.wait("s", sV)
            em.opi("s", sS, lambda s_: s_.copy(RI[:, :], TQ[:, 0:NCK * 128]))
            em.opi("s", sS, lambda s_: s_.copy(RF[:, 0:NCK * 128], RI[:, :]))
            em.wait("v", sS)
            em.opi("v", sV, lambda v: v.tensor_sub(
                TQ[:, 0:NCK * 128], TQ[:, 0:NCK * 128], RF[:, 0:NCK * 128]))
            em.opi("v", sV, lambda v: v.tensor_single_scalar(
                FIX[:, 0:NCK * 128], TQ[:, 0:NCK * 128], 0.5, ALU.is_gt))
            em.opi("v", sV, lambda v: v.tensor_add(
                RF[:, 0:NCK * 128], RF[:, 0:NCK * 128], FIX[:, 0:NCK * 128]))
            em.opi("v", sV, lambda v: v.tensor_scalar(
                TQ[:, 0:NCK * 128], TQ[:, 0:NCK * 128], -0.5, -1.0, ALU.is_ge, ALU.add))
            for c in range(NCK):
                em.opi("v", sV, lambda v, c=c, s=s, wq=wq: v.tensor_add(
                    XQ[:, c * W + s:c * W + s + wq], RF[:, c * 128:c * 128 + wq],
                    TQ[:, c * 128:c * 128 + wq]))
        if dbg == "xq":
            em.wait("s", sV)
            em.dma("s", qA, dbgq_t.rearrange("(g p) t -> p g t", p=128),
                   XQ[:, :].rearrange("p (g t) -> p g t", g=NCK))
        if dbg == "cc":
            em.wait("s", sV)
            em.dma("s", qA, dbgc_t[:, :], CC[:, :])
        em.flush()
        stk5.close()
        stkFx2.close()
        stkPA.close()

        stk6 = ExitStack()
        GW = stk6.enter_context(nc.sbuf_tensor("GW", [128, 4 * NCK * 128], F32,
                                               side="left"))
        QBF = [stk6.enter_context(nc.sbuf_tensor(f"QBF{i}", [128, NCK * 128], BF16,
                                                 side="left")) for i in range(4)]
        QTT = [stk6.enter_context(nc.sbuf_tensor(f"QTT{i}", [128, NCK * 128], F32,
                                                 side="left")) for i in range(4)]
        GSB = stk6.enter_context(nc.sbuf_tensor("GSB", [128, 2 * NRT], F32, side="right"))
        GUB = [stk6.enter_context(nc.sbuf_tensor(f"GUB{i}", [128, NRT], F32, side="right"))
               for i in range(2)]
        AC3 = stk6.enter_context(nc.sbuf_tensor("AC3", [128, 3 * NRT], F32, side="right"))
        gw_guard = Guard(4)
        qbf_guard = Guard(4)
        qtt_guard = Guard(2)
        gub_guard = Guard(2)

        gtf = gt.rearrange("a b c d -> (a b c d)")
        utf = ut.rearrange("a b c d -> (a b c d)")

        def quantize_gu_blk(src_flat, hb, mat, key):
            half = mat * 2 + hb % 2
            qt = (hb % 2) * 2
            wi = mat * 2 + hb % 2
            stg = GW[:, half * NCK * 128:(half + 1) * NCK * 128]
            gw_guard.use(em, "y", half)
            base = hb * 128 * NCK * 128
            em.dma("y", qW, stg, src_flat[base:base + 128 * NCK * 128]
                   .rearrange("(p m) -> p m", p=128))
            em.wait("v", qW)
            qtt_guard.use(em, "v", hb % 2)
            em.opi("v", sV, lambda v, stg=stg, key=key, qt=qt: v.tensor_single_scalar(
                QTT[qt][:, :], stg, cc(C_TAU[key]), ALU.is_ge))
            em.opi("v", sV, lambda v, stg=stg, key=key, qt=qt: v.tensor_scalar(
                QTT[qt + 1][:, :], stg, cc(C_NTAU[key]), -1.0, ALU.is_ge, ALU.add))
            gw_guard.done(half, sV)
            em.wait("g", sV)
            qbf_guard.use(em, "g", wi)
            em.opi("g", sG, lambda g, wi=wi, qt=qt: g.tensor_add(
                QBF[wi][:, :], QTT[qt][:, :], QTT[qt + 1][:, :]))
            qtt_guard.done(hb % 2, sG)
            return wi, sG.n

        em.phase()
        em.wait("y", qA, x1_out_n)
        em.wait("v", qA, x1_out_n)
        em.wait("g", qA, x1_out_n)
        em.wait("s", qA, x1_out_n)
        for hb in range(HBN):
            wg_i, qg = quantize_gu_blk(gtf, hb, 0, "g")
            wu_i, qu = quantize_gu_blk(utf, hb, 1, "u")
            gub_guard.use(em, "v", hb % 2)
            for ci, (s, e) in enumerate(NC_RE):
                wq = e - s
                so = s - R0
                b0 = (hb % 2) * 2
                b1 = b0 + 1
                acquire(b0)
                em.wait("t", sG, qg)
                mm_chain(b0, [(QBF[wg_i][:, k * 128:(k + 1) * 128],
                               XQ[:, k * W + s:k * W + e]) for k in range(NCK)], wq)
                acquire(b1)
                em.wait("t", sG, qu)
                mm_chain(b1, [(QBF[wu_i][:, k * 128:(k + 1) * 128],
                               XQ[:, k * W + s:k * W + e]) for k in range(NCK)], wq)
                em.wait("v", sT)
                em.opi("v", sV, lambda v, b0=b0, so=so, wq=wq, s=s: v.scalar_tensor_tensor(
                    GSB[:, so:so + wq], psb[b0][:, 0:wq], cc(C_ISW["g"]),
                    IXR[:, s:s + wq], ALU.mult, ALU.mult))
                em.opi("v", sV, lambda v, b1=b1, so=so, wq=wq, s=s: v.scalar_tensor_tensor(
                    GSB[:, NRT + so:NRT + so + wq], psb[b1][:, 0:wq], cc(C_ISW["u"]),
                    IXR[:, s:s + wq], ALU.mult, ALU.mult))
                release(b0, sV)
                release(b1, sV)
                em.wait("s", sV)
                em.opi("s", sS, lambda s_, so=so, wq=wq: s_.activation(
                    GSB[:, so:so + wq], GSB[:, so:so + wq], ACT.Silu))
                em.wait("v", sS)
                em.opi("v", sV, lambda v, so=so, wq=wq, hb=hb: v.tensor_mul(
                    GUB[hb % 2][:, so:so + wq], GSB[:, so:so + wq],
                    GSB[:, NRT + so:NRT + so + wq]))
            qbf_guard.done(wg_i, sT)
            qbf_guard.done(wu_i, sT)
            em.wait("s", sV)
            em.dma("s", qA, gu_d[hb, :, :], GUB[hb % 2][:, :])
            em.opi("s", sS, lambda s_, hb=hb: s_.activation(
                JNK[:, 0:NRT], GUB[hb % 2][:, :], ACT.Square))
            em.wait("v", sS)
            if hb == 0:
                em.opi("v", sV, lambda v: v.tensor_copy(AC3[:, 0:NRT], JNK[:, 0:NRT]))
                em.opi("v", sV, lambda v, hb=hb: v.tensor_copy(
                    AC3[:, NRT:2 * NRT], GUB[hb % 2][:, :]))
                em.opi("v", sV, lambda v, hb=hb: v.tensor_copy(
                    AC3[:, 2 * NRT:3 * NRT], GUB[hb % 2][:, :]))
            else:
                em.opi("v", sV, lambda v: v.tensor_add(
                    AC3[:, 0:NRT], AC3[:, 0:NRT], JNK[:, 0:NRT]))
                em.opi("v", sV, lambda v, hb=hb: v.tensor_max(
                    AC3[:, NRT:2 * NRT], AC3[:, NRT:2 * NRT], GUB[hb % 2][:, :]))
                em.opi("v", sV, lambda v, hb=hb: v.tensor_tensor(
                    AC3[:, 2 * NRT:3 * NRT], AC3[:, 2 * NRT:3 * NRT],
                    GUB[hb % 2][:, :], ALU.min))
            gub_guard.done(hb % 2, sV)

        SQ2 = rep5 if False else None
        rep_from_psum_row(AC3[:, 0:NRT], NRT, rrho[:, 0:1] if False else rrho)
        for j in range(NBLK):
            acquire(4)
            em.wait("t", sV)
            em.opi("t", sT, lambda t, j=j: t.transpose(
                psb[4][:, 0:128], AC3[:, NRT + j * 128:NRT + (j + 1) * 128], IDN[:]))
            em.wait("v", sT)
            em.opi("v", sV, lambda v, j=j: v.reduce_max(
                MAXC[:, j:j + 1], psb[4][:, 0:128], AXX))
            release(4, sV)
            acquire(5)
            em.wait("t", sV)
            em.opi("t", sT, lambda t, j=j: t.transpose(
                psb[5][:, 0:128], AC3[:, 2 * NRT + j * 128:2 * NRT + (j + 1) * 128],
                IDN[:]))
            em.wait("v", sT)
            em.opi("v", sV, lambda v, j=j: v.tensor_reduce(
                MAXC[:, NBLK + j:NBLK + j + 1], psb[5][:, 0:128], AXX, ALU.min))
            release(5, sV)
        em.wait("v", sV)
        em.opi("v", sV, lambda v: v.tensor_single_scalar(
            MAXC[:, NBLK:2 * NBLK], MAXC[:, NBLK:2 * NBLK], -1.0, ALU.mult))
        em.wait("v", sV)
        em.opi("v", sV, lambda v: v.tensor_max(
            MAXC[:, 0:NBLK], MAXC[:, 0:NBLK], MAXC[:, NBLK:2 * NBLK]))
        em.wait("v", sV)
        cols_to_rep(NBLK, FREP, 0)
        em.opi("v", sV, lambda v: v.tensor_single_scalar(
            rrho[:, 0:NRT], rrho[:, 0:NRT], 1.0 / H, ALU.mult))
        em.wait("s", sV)
        em.opi("s", sS, lambda s_: s_.activation(
            rrho[:, 0:NRT], rrho[:, 0:NRT], ACT.Sqrt, bias=cc(C_EPS6)))
        em.wait("v", sS)
        em.opi("v", sV, lambda v: v.reciprocal(rrho[:, 0:NRT], rrho[:, 0:NRT]))
        em.opi("v", sV, lambda v: v.tensor_mul(
            FREP[:, 0:NRT], FREP[:, 0:NRT], rrho[:, 0:NRT]))
        em.opi("v", sV, lambda v: v.tensor_single_scalar(
            FREP[:, 0:NRT], FREP[:, 0:NRT], 1e-5, ALU.max))
        em.opi("v", sV, lambda v: v.tensor_single_scalar(
            IXR[:, 0:NRT], FREP[:, 0:NRT], 1.0 / 127.0, ALU.mult))
        em.opi("v", sV, lambda v: v.reciprocal(FREP[:, 0:NRT], FREP[:, 0:NRT]))
        em.opi("v", sV, lambda v: v.tensor_mul(
            FREP[:, 0:NRT], FREP[:, 0:NRT], rrho[:, 0:NRT]))
        em.opi("v", sV, lambda v: v.tensor_single_scalar(
            FREP[:, 0:NRT], FREP[:, 0:NRT], 127.0, ALU.mult))
        if dbg == "gu":
            em.wait("s", sV)
            em.wait("s", qA)
            em.dma("s", qA, dbgg_t[:, :, :], gu_d[:, :, :])
        if dbg == "repx":
            em.wait("s", sV)
            em.dma("s", qA, dbgr_t[:, :], REPX[:, :])
        em.flush()
        stk6.close()
        stkXQ.close()

        stkFx3 = ExitStack()
        Fx3 = stkFx3.enter_context(nc.sbuf_tensor("Fx3", [128, NCK * W], F32, side="left"))
        stk7 = ExitStack()
        XQ2 = [stk7.enter_context(nc.sbuf_tensor(f"XQ2_{i}", [128, HBS * NRT], BF16,
                                                 side="right")) for i in range(2)]
        DW = [stk7.enter_context(nc.sbuf_tensor(f"DW{i}", [128, HBS * D], BF16,
                                                side="right")) for i in range(2)]
        DSTG = stk7.enter_context(nc.sbuf_tensor("DSTG", [128, D], F32, side="left"))
        QT7 = [stk7.enter_context(nc.sbuf_tensor(f"QT7_{i}", [128, D], F32, side="left"))
               for i in range(2)]
        GIN = [stk7.enter_context(nc.sbuf_tensor(f"GIN{i}", [128, NRT], F32, side="right"))
               for i in range(2)]
        RI2 = [stk7.enter_context(nc.sbuf_tensor(f"RI2_{i}", [128, NRT], I32, side="right"))
               for i in range(2)]
        RF2 = [stk7.enter_context(nc.sbuf_tensor(f"RF2_{i}", [128, NRT], F32, side="right"))
               for i in range(2)]
        ET7 = stk7.enter_context(nc.sbuf_tensor("ET7", [128, 512], F32, side="right"))

        dstg_guard = Guard(1)
        qt7_guard = Guard(1)
        dw_guard = Guard(2)
        xq2_guard = Guard(2)
        gin_guard = Guard(2)
        rif_guard = Guard(2)
        et7_guard = Guard(1)

        dtf = dtw.rearrange("a b -> (a b)")
        n_groups = HBN // HBG
        subs = HBG // HBS

        em.phase()
        em.wait("y", qA)
        em.wait("v", qA)
        em.wait("g", qA)
        em.wait("s", qA)
        em.dma("s", qA, Fx3[:, :].rearrange("p (g t) -> p g t", g=NCK),
               x1_d.rearrange("(g p) t -> p g t", p=128))
        x1_in_n = qA.n
        for gi in range(n_groups):
            sub_ready = []
            for si in range(subs):
                p = si % 2
                dw_guard.use(em, "g", p)
                xq2_guard.use(em, "g", p)
                for k in range(HBS):
                    hb = gi * HBG + si * HBS + k
                    dstg_guard.use(em, "y", 0)
                    base = hb * 128 * D
                    em.dma("y", qW, DSTG[:, :], dtf[base:base + 128 * D]
                           .rearrange("(p m) -> p m", p=128))
                    em.wait("v", qW)
                    qt7_guard.use(em, "v", 0)
                    em.opi("v", sV, lambda v: v.tensor_single_scalar(
                        QT7[0][:, :], DSTG[:, :], cc(C_TAU["d"]), ALU.is_ge))
                    em.opi("v", sV, lambda v: v.tensor_scalar(
                        QT7[1][:, :], DSTG[:, :], cc(C_NTAU["d"]), -1.0,
                        ALU.is_ge, ALU.add))
                    dstg_guard.done(0, sV)
                    em.wait("g", sV)
                    em.opi("g", sG, lambda g, p=p, k=k: g.tensor_add(
                        DW[p][:, k * D:(k + 1) * D], QT7[0][:, :], QT7[1][:, :]))
                    qt7_guard.done(0, sG)
                    kb = k % 2
                    gin_guard.use(em, "g", kb)
                    em.dma("s", qA, GIN[kb][:, :], gu_d[hb, :, :])
                    em.wait("g", qA)
                    em.opi("g", sG, lambda g, kb=kb: g.tensor_mul(
                        GIN[kb][:, :], GIN[kb][:, :], FREP[:, 0:NRT]))
                    rif_guard.use(em, "s", kb)
                    em.wait("s", sG)
                    em.opi("s", sS, lambda s_, kb=kb: s_.copy(RI2[kb][:, :], GIN[kb][:, :]))
                    em.opi("s", sS, lambda s_, kb=kb: s_.copy(RF2[kb][:, :], RI2[kb][:, :]))
                    em.wait("v", sS)
                    em.opi("v", sV, lambda v, kb=kb: v.tensor_sub(
                        GIN[kb][:, :], GIN[kb][:, :], RF2[kb][:, :]))
                    em.opi("v", sV, lambda v, kb=kb: v.tensor_single_scalar(
                        JNK[:, 0:NRT], GIN[kb][:, :], 0.5, ALU.is_gt))
                    em.opi("v", sV, lambda v, kb=kb: v.tensor_add(
                        RF2[kb][:, :], RF2[kb][:, :], JNK[:, 0:NRT]))
                    em.opi("v", sV, lambda v, kb=kb: v.tensor_scalar(
                        GIN[kb][:, :], GIN[kb][:, :], -0.5, -1.0,
                        ALU.is_ge, ALU.add))
                    em.wait("g", sV)
                    em.opi("g", sG, lambda g, p=p, k=k, kb=kb: g.tensor_add(
                        XQ2[p][:, k * NRT:(k + 1) * NRT], RF2[kb][:, :], GIN[kb][:, :]))
                    rif_guard.done(kb, sG)
                    gin_guard.done(kb, sG)
                sub_ready.append((sG.n, sV.n))
                if gi == 0 and si == 0 and dbg == "xq2":
                    em.wait("s", sV)
                    em.dma("s", qA, dbgx2_t[:, :], XQ2[0][:, :])
                if gi == 0 and si == 0 and dbg == "dw":
                    em.wait("s", sG)
                    em.dma("s", qA, dbgdw_t[:, :], DW[0][:, :])
            for ci, (s, e) in enumerate(NC_RE):
                wq = e - s
                for dc in range(NCK):
                    bi = dc % 4
                    acquire(bi)
                    for si in range(subs):
                        gn, vn = sub_ready[si]
                        em.wait("t", sG, gn)
                        em.wait("t", sV, vn)
                        p = si % 2
                        so = s - R0
                        for k in range(HBS):
                            st = (si == 0 and k == 0)
                            sp = (si == subs - 1 and k == HBS - 1)
                            l = DW[p][:, k * D + dc * 128:k * D + (dc + 1) * 128]
                            r = XQ2[p][:, k * NRT + so:k * NRT + so + wq]
                            if sp:
                                em.opi("t", sT, lambda t, l=l, r=r, st=st, bi=bi, wq=wq:
                                       t.matmul(psb[bi][:, 0:wq], l, r, start=st,
                                                stop=True))
                            else:
                                em.op("t", lambda t, l=l, r=r, st=st, bi=bi, wq=wq:
                                      t.matmul(psb[bi][:, 0:wq], l, r, start=st,
                                               stop=False))
                    et7_guard.use(em, "v", 0)
                    em.wait("v", sT)
                    em.wait("v", qA, x1_in_n)
                    em.opi("v", sV, lambda v, bi=bi, wq=wq, s=s: v.scalar_tensor_tensor(
                        ET7[:, 0:wq], psb[bi][:, 0:wq], cc(C_ISW["d"]),
                        IXR[:, s - R0:s - R0 + wq], ALU.mult, ALU.mult))
                    em.opi("v", sV, lambda v, dc=dc, s=s, wq=wq: v.tensor_add(
                        Fx3[:, dc * W + s:dc * W + s + wq],
                        Fx3[:, dc * W + s:dc * W + s + wq], ET7[:, 0:wq]))
                    if dbg == "et7" and gi == 0 and ci == 0 and dc == 0:
                        em.wait("s", sV)
                        em.dma("s", qA, dbge_t[:, 0:wq], ET7[:, 0:wq])
                    if dbg == "draw" and gi == 0 and ci == 0 and dc == 0:
                        em.opi("v", sV, lambda v, bi=bi, wq=wq: v.tensor_copy(
                            JNK[:, 0:wq], psb[bi][:, 0:wq]))
                        em.wait("s", sV)
                        em.dma("s", qA, dbge_t[:, 0:wq], JNK[:, 0:wq])
                    release(bi, sV)
                    et7_guard.done(0, sV)
            for p in range(2):
                dw_guard.done(p, sT)
                xq2_guard.done(p, sT)
        em.flush()
        stk7.close()
        stkRX.close()

        stk8 = ExitStack()
        OUTT = [stk8.enter_context(nc.sbuf_tensor(f"OUTT{i}", [128, D], F32, side="right"))
                for i in range(2)]
        outt_guard = Guard(2)
        em.phase()
        em.wait("t", sV)
        for j in range(NBLK):
            ob = OUTT[j % 2]
            outt_guard.use(em, "v", j % 2)
            for g in range(NCK):
                bi = 4 + (g % 2)
                acquire(bi)
                em.opi("t", sT, lambda t, g=g, j=j, bi=bi: t.transpose(
                    psb[bi][:, 0:128],
                    Fx3[:, g * W + R0 + j * 128:g * W + R0 + (j + 1) * 128], IDN[:]))
                em.wait("v", sT)
                em.opi("v", sV, lambda v, g=g, bi=bi, ob=ob: v.tensor_copy(
                    ob[:, g * 128:(g + 1) * 128], psb[bi][:, 0:128]))
                release(bi, sV)
            em.wait("s", sV)
            em.dma("s", sO, y[j * 128:(j + 1) * 128, :], ob[:, :])
            outt_guard.done(j % 2, sO)
        tap("x2", Fx3)
        em.wait("y", sO)
        em.flush()
        stk8.close()
        stkFx3.close()

    return nc


_CACHE = {}


def _blk(Wt):
    Din, Dout = Wt.shape
    K, C = Din // 128, Dout // 128
    return np.ascontiguousarray(Wt.reshape(K, 128, C, 128).transpose(2, 1, 0, 3))


def kernel(L0_attn_x, L0_attn_ln1_g, L0_attn_ln1_b, L0_attn_ln2_g, L0_attn_ln2_b,
           L0_attn_tm_w, L0_attn_tm_u, L0_attn_tm_mu_r, L0_attn_tm_mu_k,
           L0_attn_tm_mu_v, L0_attn_tm_Wr, L0_attn_tm_Wk, L0_attn_tm_Wv,
           L0_attn_tm_Wout, L0_attn_cm_mu_r, L0_attn_cm_mu_k, L0_attn_cm_Wr,
           L0_attn_cm_Wk, L0_attn_cm_Wv, L0_ffn_w_gate, L0_ffn_w_up, L0_ffn_w_down):
    x = np.ascontiguousarray(np.asarray(L0_attn_x, dtype=np.float32))
    Bsz, T, D = x.shape
    H = np.asarray(L0_ffn_w_gate).shape[0]
    n_cores = 8
    NRT = (Bsz * T) // n_cores

    import os
    dbg = os.environ.get("KDBG") or None
    key = (D, H, NRT, dbg)
    if key not in _CACHE:
        _CACHE[key] = build(D, H, NRT, dbg)
    nc = _CACHE[key]

    def P(a):
        return np.ascontiguousarray(np.asarray(a, dtype=np.float32))

    wts = {
        "wr": _blk(P(L0_attn_tm_Wr).T), "wk": _blk(P(L0_attn_tm_Wk).T),
        "wv": _blk(P(L0_attn_tm_Wv).T), "wo": _blk(P(L0_attn_tm_Wout).T),
        "cwr": _blk(P(L0_attn_cm_Wr).T), "cwk": _blk(P(L0_attn_cm_Wk).T),
        "cwv": _blk(P(L0_attn_cm_Wv).T),
        "gt": _blk(P(L0_ffn_w_gate).T), "ut": _blk(P(L0_ffn_w_up).T),
        "dtw": np.ascontiguousarray(P(L0_ffn_w_down).T),
    }
    NCK = D // 128
    vecs = {
        "ln1g": P(L0_attn_ln1_g), "ln1b": P(L0_attn_ln1_b),
        "ln2g": P(L0_attn_ln2_g), "ln2b": P(L0_attn_ln2_b),
        "tmu": P(L0_attn_tm_u), "mur": P(L0_attn_tm_mu_r),
        "muk": P(L0_attn_tm_mu_k), "muv": P(L0_attn_tm_mu_v),
        "cmur": P(L0_attn_cm_mu_r), "cmuk": P(L0_attn_cm_mu_k),
    }

    halves = T // NRT
    in_maps = []
    for c in range(n_cores):
        b, half = c // halves, c % halves
        t0 = half * NRT
        if t0 == 0:
            sl = np.concatenate([np.zeros((2, D), np.float32), x[b, 0:NRT]], axis=0)
        else:
            sl = x[b, t0 - 2:t0 + NRT]
        m = {"xT": np.ascontiguousarray(sl.T),
             "maskd": np.full((128, 1), 0.0 if t0 == 0 else 1.0, np.float32)}
        m.update(wts)
        for k, v in vecs.items():
            m["v_" + k] = np.ascontiguousarray(v.reshape(NCK, 128))
        in_maps.append(m)

    res = run_bass_kernel_spmd(nc, in_maps, core_ids=list(range(n_cores)))
    if dbg:
        kernel._dbg = [np.asarray(res.results[c]["dbg"]) for c in range(n_cores)]
    out = np.empty((Bsz, T, D), np.float32)
    for c in range(n_cores):
        b, half = c // halves, c % halves
        out[b, half * NRT:(half + 1) * NRT] = np.asarray(res.results[c]["y"])
    return out


# version 13
# speedup vs baseline: 1.3353x; 1.1911x over previous
import numpy as np
from contextlib import ExitStack

import concourse.bass as bass
import concourse.mybir as mybir
from concourse.bass_utils import run_bass_kernel_spmd

F32 = mybir.dt.float32
BF16 = mybir.dt.bfloat16
I32 = mybir.dt.int32
ALU = mybir.AluOpType
ACT = mybir.ActivationFunctionType
AXX = mybir.AxisListType.X

LNEPS8 = 18.420680743952367


class Sem:
    def __init__(self, h):
        self.h = h
        self.n = 0

    def bump(self, k=1):
        self.n += k
        return self.n


class EM:
    ENGS = ("t", "v", "s", "g", "y")

    def __init__(self, nc):
        self.nc = nc
        self.cur = None

    def phase(self):
        self.cur = {e: [] for e in self.ENGS}

    def flush(self):
        ph, self.cur = self.cur, None

        def runner(ops):
            def go(e):
                for f in ops:
                    f(e)
            return go

        with self.nc.Block() as blk:
            if ph["y"]:
                blk.sync(runner(ph["y"]))
            if ph["t"]:
                blk.tensor(runner(ph["t"]))
            if ph["v"]:
                blk.vector(runner(ph["v"]))
            if ph["s"]:
                blk.scalar(runner(ph["s"]))
            if ph["g"]:
                blk.gpsimd(runner(ph["g"]))

    def op(self, eng, fn):
        self.cur[eng].append(fn)

    def wait(self, eng, sem, n=None):
        v = sem.n if n is None else n
        if v > 0:
            self.cur[eng].append(lambda e: e.wait_ge(sem.h, v))

    def opi(self, eng, sem, fn, k=1):
        n = sem.bump(k)
        self.cur[eng].append(lambda e: fn(e).then_inc(sem.h, k))
        return n

    def vs(self):
        self.wait("v", None) if False else None

    def dma(self, eng, sem, out_ap, in_ap, **kw):
        return self.opi(eng, sem, lambda e: e.dma_start(out=out_ap, in_=in_ap, **kw), k=16)


class Guard:
    def __init__(self, n):
        self.rec = [None] * n

    def use(self, em, eng, i):
        r = self.rec[i]
        if r is not None:
            em.wait(eng, r[0], r[1])
            self.rec[i] = None

    def done(self, i, sem):
        self.rec[i] = (sem, sem.n)


def build(D, H, NRT, dbg=None):
    NCK = D // 128
    HBN = H // 128
    W = NRT + 2
    R0 = 2
    NBLK = NRT // 128

    def tchunks(s0, e0, step):
        out, s = [], s0
        while s < e0:
            out.append((s, min(s + step, e0)))
            s += step
        return out

    NC_H1 = [(0, 2)] + tchunks(R0, W, 512)
    NC_RE = tchunks(R0, W, 512)
    NC_8 = tchunks(R0, W, 128)
    HBG = min(8, HBN)
    HBS = min(4, HBG)
    PAW = 2048
    while (H * D // 128) % PAW != 0:
        PAW //= 2

    nc = bass.Bass("TRN2", target_bir_lowering=False)

    xT = nc.dram_tensor("xT", [D, W], F32, kind="ExternalInput").ap()
    wnames = ["wr", "wk", "wv", "wo", "cwr", "cwk", "cwv"]
    wt = {n: nc.dram_tensor(n, [NCK, 128, NCK, 128], F32, kind="ExternalInput").ap()
          for n in wnames}
    gt = nc.dram_tensor("gt", [HBN, 128, NCK, 128], F32, kind="ExternalInput").ap()
    ut = nc.dram_tensor("ut", [HBN, 128, NCK, 128], F32, kind="ExternalInput").ap()
    dtw = nc.dram_tensor("dtw", [H, D], F32, kind="ExternalInput").ap()
    vnames = ["ln1g", "ln1b", "ln2g", "ln2b", "tmu", "mur", "muk", "muv", "cmur", "cmuk"]
    vin = {n: nc.dram_tensor("v_" + n, [NCK, 128], F32, kind="ExternalInput").ap()
           for n in vnames}
    maskd = nc.dram_tensor("maskd", [128, 1], F32, kind="ExternalInput").ap()
    y = nc.dram_tensor("y", [NRT, D], F32, kind="ExternalOutput").ap()

    dbg_t = (nc.dram_tensor("dbg", [D, W], F32, kind="ExternalOutput").ap()
             if dbg in ("a", "h", "xmid", "x1", "x2") else None)
    dbgq_t = (nc.dram_tensor("dbg", [D, W], BF16, kind="ExternalOutput").ap()
              if dbg == "xq" else None)
    dbgg_t = (nc.dram_tensor("dbg", [HBN, 128, NRT], F32, kind="ExternalOutput").ap()
              if dbg == "gu" else None)
    dbgc_t = (nc.dram_tensor("dbg", [128, 16], F32, kind="ExternalOutput").ap()
              if dbg == "cc" else None)
    dbgr_t = (nc.dram_tensor("dbg", [128, 3 * W], F32, kind="ExternalOutput").ap()
              if dbg == "repx" else None)
    dbgx2_t = (nc.dram_tensor("dbg", [128, HBS * NRT], BF16, kind="ExternalOutput").ap()
               if dbg == "xq2" else None)
    dbgdw_t = (nc.dram_tensor("dbg", [128, HBS * D], BF16, kind="ExternalOutput").ap()
               if dbg == "dw" else None)
    dbge_t = (nc.dram_tensor("dbg", [128, 512], F32, kind="ExternalOutput").ap()
              if dbg in ("et7", "draw") else None)

    a_d = nc.dram_tensor("a_d", [D, W], F32, kind="Internal").ap()
    x1_d = nc.dram_tensor("x1_d", [D, W], F32, kind="Internal").ap()
    gu_d = nc.dram_tensor("gu_d", [HBN, 128, NRT], F32, kind="Internal").ap()

    em = EM(nc)
    es = ExitStack()
    with es:
        qW = Sem(es.enter_context(nc.semaphore("qW")))
        qA = Sem(es.enter_context(nc.semaphore("qA")))
        qG = Sem(es.enter_context(nc.semaphore("qG")))
        sT = Sem(es.enter_context(nc.semaphore("sT")))
        sV = Sem(es.enter_context(nc.semaphore("sV")))
        sS = Sem(es.enter_context(nc.semaphore("sS")))
        sG = Sem(es.enter_context(nc.semaphore("sG")))
        sO = Sem(es.enter_context(nc.semaphore("sO")))

        VEC = es.enter_context(nc.sbuf_tensor("VEC", [128, 12 * NCK], F32))
        CC = es.enter_context(nc.sbuf_tensor("CC", [128, 16], F32))
        IDN = es.enter_context(nc.sbuf_tensor("IDN", [128, 128], F32))
        STAT = es.enter_context(nc.sbuf_tensor("STAT", [128, 256], F32))
        MAXC = es.enter_context(nc.sbuf_tensor("MAXC", [128, 2 * NBLK], F32))
        JNK = es.enter_context(nc.sbuf_tensor("JNK", [128, max(PAW, NRT, W)], F32))
        ROWS = es.enter_context(nc.sbuf_tensor("ROWS", [1, 640 + max(NRT, 128)], F32))
        psb = [es.enter_context(nc.psum_tensor(f"pb{i}", [128, 512], F32)) for i in range(8)]

        C_ONE, C_EPS5, C_EPS6, C_MASK = 0, 1, 2, 3
        C_TAU = {"g": 4, "u": 5, "d": 6}
        C_NTAU = {"g": 7, "u": 8, "d": 9}
        C_ISW = {"g": 10, "u": 11, "d": 12}

        def cc(i):
            return CC[:, i:i + 1]

        free_rec = [None] * 8

        def acquire(bi):
            if free_rec[bi] is not None:
                s, n = free_rec[bi]
                em.wait("t", s, n)
                free_rec[bi] = None

        def release(bi, sem):
            free_rec[bi] = (sem, sem.n)

        def mm_chain(bi, pairs, wq):
            pairs = pairs[::-1]
            L = len(pairs)
            for i, (l, r) in enumerate(pairs[:-1]):
                em.op("t", lambda t, l=l, r=r, i=i, bi=bi, wq=wq: t.matmul(
                    psb[bi][:, 0:wq], l, r, start=(i == 0), stop=False))
            l, r = pairs[-1]
            return em.opi("t", sT, lambda t, l=l, r=r, bi=bi, wq=wq, L=L: t.matmul(
                psb[bi][:, 0:wq], l, r, start=(L == 1), stop=True))

        def rep_from_psum_row(src_ap, W_, rep):
            for (s, e) in tchunks(0, W_, 512):
                acquire(6)
                em.wait("t", sV)
                em.opi("t", sT, lambda t, s=s, e=e: t.matmul(
                    psb[6][0:1, 0:e - s], cc(C_ONE), src_ap[:, s:e], start=True, stop=True))
                em.wait("v", sT)
                em.opi("v", sV, lambda v, s=s, e=e: v.tensor_copy(
                    rep[0:1, s:e], psb[6][0:1, 0:e - s]))
                release(6, sV)
            for (s, e) in tchunks(0, W_, 512):
                acquire(7)
                em.wait("t", sV)
                em.opi("t", sT, lambda t, s=s, e=e: t.matmul(
                    psb[7][:, 0:e - s], ROWS[0:1, 0:128], rep[0:1, s:e],
                    start=True, stop=True))
                em.wait("v", sT)
                em.opi("v", sV, lambda v, s=s, e=e: v.tensor_copy(
                    rep[:, s:e], psb[7][:, 0:e - s]))
                release(7, sV)

        def stats_sums(src, scratch, S1rep, S2rep):
            half = NCK // 2
            em.opi("v", sV, lambda v: v.tensor_add(
                scratch[:, 0:half * W], src[:, 0:half * W], src[:, half * W:NCK * W]))
            n = half
            while n > 1:
                h2 = n // 2
                em.opi("v", sV, lambda v, h2=h2, n=n: v.tensor_add(
                    scratch[:, 0:h2 * W], scratch[:, 0:h2 * W], scratch[:, h2 * W:n * W]))
                n = h2
            rep_from_psum_row(scratch[:, 0:W], W, S1rep)
            em.wait("s", sT)
            em.opi("s", sS, lambda s: s.activation(
                scratch[:, 0:half * W], src[:, 0:half * W], ACT.Square))
            for i in range(half):
                if i > 0:
                    em.wait("s", sV)
                em.opi("s", sS, lambda s, i=i: s.activation(
                    JNK[:, 0:W], src[:, (half + i) * W:(half + 1 + i) * W], ACT.Square))
                em.wait("v", sS)
                em.opi("v", sV, lambda v, i=i: v.tensor_add(
                    scratch[:, i * W:(i + 1) * W], scratch[:, i * W:(i + 1) * W],
                    JNK[:, 0:W]))
            n = half
            while n > 1:
                h2 = n // 2
                em.opi("v", sV, lambda v, h2=h2, n=n: v.tensor_add(
                    scratch[:, 0:h2 * W], scratch[:, 0:h2 * W], scratch[:, h2 * W:n * W]))
                n = h2
            rep_from_psum_row(scratch[:, 0:W], W, S2rep)

        def cols_to_rep(ncols, rep_ap, off, src_off=0):
            for j in range(ncols):
                acquire(6)
                em.wait("t", sV)
                em.opi("t", sT, lambda t, j=j, src_off=src_off: t.transpose(
                    psb[6][0:1, 0:128], MAXC[:, src_off + j:src_off + j + 1], IDN[:]))
                em.wait("v", sT)
                em.opi("v", sV, lambda v, j=j: v.tensor_copy(
                    ROWS[0:1, 640 + j * 128:640 + (j + 1) * 128], psb[6][0:1, 0:128]))
                release(6, sV)
            for j in range(ncols):
                acquire(7)
                em.wait("t", sV)
                em.opi("t", sT, lambda t, j=j: t.matmul(
                    psb[7][:, 0:128], ROWS[0:1, 0:128],
                    ROWS[0:1, 640 + j * 128:640 + (j + 1) * 128], start=True, stop=True))
                em.wait("v", sT)
                em.opi("v", sV, lambda v, j=j, off=off: v.tensor_copy(
                    rep_ap[:, off + j * 128:off + (j + 1) * 128], psb[7][:, 0:128]))
                release(7, sV)

        def tap(name, sb):
            if dbg == name:
                em.wait("s", sV)
                em.wait("s", sS)
                em.dma("s", qA, dbg_t.rearrange("(g p) t -> p g t", p=128),
                       sb[:, :].rearrange("p (g t) -> p g t", g=NCK))

        per_mat = H * D
        tiles_per_mat = per_mat // (128 * PAW)
        pa_srcs = [gt.rearrange("a b c d -> (a b c d)"),
                   ut.rearrange("a b c d -> (a b c d)"),
                   dtw.rearrange("a b -> (a b)")]
        pa_total = 3 * tiles_per_mat
        pa_state = {"i": 0}
        pa_guard = Guard(2)
        PA = [None, None]

        def pa_pump(nmax):
            while pa_state["i"] < pa_total and nmax > 0:
                i = pa_state["i"]
                mi, ti = i // tiles_per_mat, i % tiles_per_mat
                buf = PA[i % 2]
                pa_guard.use(em, "y", i % 2)
                base = ti * 128 * PAW
                em.dma("y", qW, buf[:, 0:PAW],
                       pa_srcs[mi][base:base + 128 * PAW].rearrange("(p m) -> p m", p=128))
                em.wait("s", qW)
                em.opi("s", sS, lambda s, buf=buf, i=i: s.activation(
                    buf[:, 0:PAW], buf[:, 0:PAW], ACT.Abs, accum_out=STAT[:, i:i + 1]))
                pa_guard.done(i % 2, sS)
                pa_state["i"] += 1
                nmax -= 1

        stkPA = ExitStack()
        PA[0] = stkPA.enter_context(nc.sbuf_tensor("PA0", [128, PAW], F32, side="left"))
        PA[1] = stkPA.enter_context(nc.sbuf_tensor("PA1", [128, PAW], F32, side="left"))
        stkFx1 = ExitStack()
        Fx1 = stkFx1.enter_context(nc.sbuf_tensor("Fx1", [128, NCK * W], F32, side="left"))
        repA = stkFx1.enter_context(nc.sbuf_tensor("repA", [128, 6 * W], F32, side="left"))
        stkC1 = ExitStack()
        C1 = stkC1.enter_context(nc.sbuf_tensor("C1", [128, (NCK // 2) * W], F32,
                                                side="right"))

        em.phase()
        em.opi("v", sV, lambda v: v.memset(CC[:, :], 0.0))
        em.wait("v", sV)
        em.opi("v", sV, lambda v: v.memset(cc(C_ONE), 1.0))
        em.opi("v", sV, lambda v: v.memset(cc(C_EPS5), 1e-5))
        em.opi("v", sV, lambda v: v.memset(cc(C_EPS6), 1e-6))
        em.opi("v", sV, lambda v: v.memset(ROWS[0:1, 0:128], 1.0))
        em.opi("g", sG, lambda g: g.iota(IDN[:], pattern=[[-1, 128]], base=0,
                                         channel_multiplier=1,
                                         allow_small_or_imprecise_dtypes=True))
        em.opi("g", sG, lambda g: g.tensor_single_scalar(IDN[:], IDN[:], 0.0, ALU.is_equal))
        vecs = {}
        for i, n in enumerate(vnames):
            t = VEC[:, i * NCK:(i + 1) * NCK]
            vecs[n] = t
            em.dma("s", qA, t, vin[n].rearrange("g p -> p g"),
                   allow_slow_non_contiguous=True)
        em.dma("s", qA, cc(C_MASK), maskd[:, :])
        em.dma("s", qA, Fx1[:, :].rearrange("p (g t) -> p g t", g=NCK),
               xT.rearrange("(g p) t -> p g t", p=128))
        em.wait("v", qA)
        UP = VEC[:, 10 * NCK:11 * NCK]
        em.opi("v", sV, lambda v: v.tensor_single_scalar(UP, vecs["tmu"], LNEPS8, ALU.add))
        pa_pump(8)

        S1 = repA[:, 0 * W:1 * W]
        S2 = repA[:, 1 * W:2 * W]
        stats_sums(Fx1[:, :], C1, S1, S2)
        m0 = repA[:, 2 * W:3 * W]
        r0 = repA[:, 3 * W:4 * W]
        r1 = repA[:, 4 * W:5 * W]
        tp = repA[:, 5 * W:6 * W]
        em.opi("v", sV, lambda v: v.tensor_single_scalar(m0, S1, 1.0 / D, ALU.mult))
        em.opi("v", sV, lambda v: v.tensor_single_scalar(S2, S2, 1.0 / D, ALU.mult))
        em.opi("v", sV, lambda v: v.tensor_mul(tp, m0, m0))
        em.opi("v", sV, lambda v: v.tensor_sub(S2, S2, tp))
        em.wait("s", sV)
        em.opi("s", sS, lambda s: s.activation(r0, S2, ACT.Sqrt, bias=cc(C_EPS5)))
        em.wait("v", sS)
        em.opi("v", sV, lambda v: v.reciprocal(r0, r0))
        em.opi("v", sV, lambda v: v.tensor_mul(tp, r0, r0))
        em.opi("v", sV, lambda v: v.tensor_mul(tp, S2, tp))
        em.wait("s", sV)
        em.opi("s", sS, lambda s: s.activation(r1, tp, ACT.Sqrt, bias=cc(C_EPS5)))
        em.wait("v", sS)
        em.opi("v", sV, lambda v: v.reciprocal(r1, r1))
        pa_pump(4)
        em.flush()
        stkC1.close()

        stkB = ExitStack()
        Bh = stkB.enter_context(nc.sbuf_tensor("Bh", [128, NCK * W], F32, side="right"))
        em.phase()
        for c in range(NCK):
            sl = slice(c * W, (c + 1) * W)
            em.opi("v", sV, lambda v, sl=sl: v.tensor_sub(Fx1[:, sl], Fx1[:, sl], m0))
            em.opi("v", sV, lambda v, sl=sl: v.tensor_mul(Fx1[:, sl], Fx1[:, sl], r0))
        em.wait("s", sV)
        em.dma("s", qA, a_d.rearrange("(g p) t -> p g t", p=128),
               Fx1[:, :].rearrange("p (g t) -> p g t", g=NCK))
        a_out_n = qA.n
        for c in range(NCK):
            sl = slice(c * W, (c + 1) * W)
            em.opi("v", sV, lambda v, sl=sl: v.tensor_mul(Bh[:, sl], Fx1[:, sl], r1))
            em.opi("v", sV, lambda v, c=c, sl=sl: v.tensor_scalar(
                Bh[:, sl], Bh[:, sl], vecs["ln1g"][:, c:c + 1],
                vecs["ln1b"][:, c:c + 1], ALU.mult, ALU.add))
            em.wait("v", sV)
            em.opi("v", sV, lambda v, c=c: v.tensor_single_scalar(
                Bh[:, c * W:c * W + 2], Bh[:, c * W:c * W + 2], cc(C_MASK), ALU.mult))
        tap("a", Fx1)
        tap("h", Bh)
        pa_pump(8)
        em.flush()
        stkFx1.close()

        stkT = ExitStack()
        TRIO = stkT.enter_context(nc.sbuf_tensor("TRIO", [128, 3 * NCK * W], BF16,
                                                 side="left"))

        def trio_ap(i, c, s, e):
            return TRIO[:, (i * NCK + c) * W + s:(i * NCK + c) * W + e]

        em.phase()
        em.wait("v", qA)
        for c in range(NCK):
            em.opi("v", sV, lambda v, c=c: v.tensor_sub(
                JNK[:, 0:W - 1], Bh[:, c * W + 1:(c + 1) * W],
                Bh[:, c * W:(c + 1) * W - 1]))
            for i, mu in enumerate(["mur", "muk", "muv"]):
                em.opi("v", sV, lambda v, c=c, i=i, mu=mu: v.scalar_tensor_tensor(
                    trio_ap(i, c, 1, W), JNK[:, 0:W - 1], vecs[mu][:, c:c + 1],
                    Bh[:, c * W:(c + 1) * W - 1], ALU.mult, ALU.add))
                em.opi("v", sV, lambda v, c=c, i=i: v.tensor_copy(
                    trio_ap(i, c, 0, 1), Bh[:, c * W:c * W + 1]))
        pa_pump(4)
        em.flush()
        stkB.close()

        stk3w = ExitStack()
        WBF = [stk3w.enter_context(nc.sbuf_tensor(f"WBF{i}", [128, NCK * 128], BF16,
                                                  side="right")) for i in range(6)]
        ET = [stk3w.enter_context(nc.sbuf_tensor(f"ET{i}", [128, 512], F32, side="right"))
              for i in range(2)]
        stkR = ExitStack()
        RWKV = stkR.enter_context(nc.sbuf_tensor("RWKV", [128, NCK * W], BF16,
                                                 side="right"))
        wbf_guard = Guard(6)
        et_guard = Guard(1)

        def stream_wblk(src_ap, dc, mat):
            wi = mat * 2 + dc % 2
            wbf_guard.use(em, "g", wi)
            em.dma("g", qG, WBF[wi][:, :],
                   src_ap[dc, :, :, :].rearrange("p k j -> p (k j)"))
            return wi, qG.n

        em.phase()
        em.wait("y", qA)
        em.wait("v", qA)
        for dc in range(NCK):
            w_r, c_r = stream_wblk(wt["wr"], dc, 0)
            w_k, c_k = stream_wblk(wt["wk"], dc, 1)
            w_v, c_v = stream_wblk(wt["wv"], dc, 2)
            b0, b1, b2 = (dc % 2) * 3, (dc % 2) * 3 + 1, (dc % 2) * 3 + 2
            for (s, e) in NC_H1:
                wq = e - s
                acquire(b0)
                em.wait("t", qG, c_r)
                mm_chain(b0, [(WBF[w_r][:, k * 128:(k + 1) * 128], trio_ap(0, k, s, e))
                              for k in range(NCK)], wq)
                acquire(b1)
                em.wait("t", qG, c_k)
                mm_chain(b1, [(WBF[w_k][:, k * 128:(k + 1) * 128], trio_ap(1, k, s, e))
                              for k in range(NCK)], wq)
                acquire(b2)
                em.wait("t", qG, c_v)
                mm_chain(b2, [(WBF[w_v][:, k * 128:(k + 1) * 128], trio_ap(2, k, s, e))
                              for k in range(NCK)], wq)
                et_guard.use(em, "s", 0)
                em.wait("s", sT)
                em.opi("s", sS, lambda s_, b0=b0, wq=wq: s_.activation(
                    ET[0][:, 0:wq], psb[b0][:, 0:wq], ACT.Sigmoid))
                em.opi("s", sS, lambda s_, b1=b1, wq=wq, dc=dc: s_.activation(
                    ET[1][:, 0:wq], psb[b1][:, 0:wq], ACT.Sigmoid, bias=UP[:, dc:dc + 1]))
                release(b0, sS)
                release(b1, sS)
                em.wait("v", sS)
                em.opi("v", sV, lambda v, wq=wq: v.tensor_mul(
                    ET[0][:, 0:wq], ET[0][:, 0:wq], ET[1][:, 0:wq]))
                em.opi("v", sV, lambda v, b2=b2, dc=dc, s=s, e=e, wq=wq: v.tensor_mul(
                    RWKV[:, dc * W + s:dc * W + e], ET[0][:, 0:wq], psb[b2][:, 0:wq]))
                release(b2, sV)
                et_guard.done(0, sV)
            for wi in (w_r, w_k, w_v):
                wbf_guard.done(wi, sT)
            pa_pump(6)
        rwkv_n = sV.n
        em.flush()
        stkT.close()

        stkFx2 = ExitStack()
        Fx2 = stkFx2.enter_context(nc.sbuf_tensor("Fx2", [128, NCK * W], F32, side="left"))
        em.phase()
        em.wait("s", qA, a_out_n)
        em.dma("s", qA, Fx2[:, :].rearrange("p (g t) -> p g t", g=NCK),
               a_d.rearrange("(g p) t -> p g t", p=128))
        a_in_n = qA.n
        for dc in range(NCK):
            w_o, c_o = stream_wblk(wt["wo"], dc, 0)
            bi = dc % 4
            for (s, e) in NC_H1:
                wq = e - s
                acquire(bi)
                em.wait("t", qG, c_o)
                em.wait("t", sV, rwkv_n)
                mm_chain(bi, [(WBF[w_o][:, k * 128:(k + 1) * 128],
                               RWKV[:, k * W + s:k * W + e]) for k in range(NCK)], wq)
                em.wait("v", sT)
                em.wait("v", qA, a_in_n)
                em.opi("v", sV, lambda v, dc=dc, s=s, wq=wq, bi=bi: v.tensor_add(
                    Fx2[:, dc * W + s:dc * W + s + wq],
                    Fx2[:, dc * W + s:dc * W + s + wq], psb[bi][:, 0:wq]))
                release(bi, sV)
            wbf_guard.done(w_o, sT)
            pa_pump(6)
        tap("xmid", Fx2)
        em.flush()
        stkR.close()

        stkM2 = ExitStack()
        rep4 = stkM2.enter_context(nc.sbuf_tensor("rep4", [128, 2 * W], F32, side="right"))
        stk4s = ExitStack()
        C4 = stk4s.enter_context(nc.sbuf_tensor("C4", [128, (NCK // 2) * W], F32,
                                                side="right"))
        repS = stk4s.enter_context(nc.sbuf_tensor("repS", [128, 2 * W], F32, side="right"))
        em.phase()
        S1b = repS[:, 0 * W:1 * W]
        S2b = repS[:, 1 * W:2 * W]
        stats_sums(Fx2[:, :], C4, S1b, S2b)
        m2 = rep4[:, 0 * W:1 * W]
        r2l = rep4[:, 1 * W:2 * W]
        em.opi("v", sV, lambda v: v.tensor_single_scalar(m2, S1b, 1.0 / D, ALU.mult))
        em.opi("v", sV, lambda v: v.tensor_single_scalar(S2b, S2b, 1.0 / D, ALU.mult))
        em.opi("v", sV, lambda v: v.tensor_mul(S1b, m2, m2))
        em.opi("v", sV, lambda v: v.tensor_sub(S2b, S2b, S1b))
        em.wait("s", sV)
        em.opi("s", sS, lambda s: s.activation(r2l, S2b, ACT.Sqrt, bias=cc(C_EPS5)))
        em.wait("v", sS)
        em.opi("v", sV, lambda v: v.reciprocal(r2l, r2l))
        em.flush()
        stk4s.close()

        stk4 = ExitStack()
        H2Q = stk4.enter_context(nc.sbuf_tensor("H2Q", [128, NCK * 513], F32, side="right"))
        T2Q = stk4.enter_context(nc.sbuf_tensor("T2Q", [128, NCK * 512], BF16, side="right"))
        K2Q = stk4.enter_context(nc.sbuf_tensor("K2Q", [128, NCK * 512], BF16, side="right"))
        em.phase()
        STASH = VEC[:, 11 * NCK:12 * NCK]
        for ci, (s, e) in enumerate(NC_RE):
            if ci == 0:
                continue
            for c in range(NCK):
                em.opi("v", sV, lambda v, c=c, s=s: v.tensor_copy(
                    STASH[:, c:c + 1], Fx2[:, c * W + s - 1:c * W + s]))
        for ci, (s, e) in enumerate(NC_RE):
            wq = e - s
            em.wait("v", sT)
            em.wait("s", sT)
            if ci == 0:
                for c in range(NCK):
                    em.opi("v", sV, lambda v, c=c, s=s, e=e, wq=wq: v.tensor_sub(
                        H2Q[:, c * 513:c * 513 + wq + 1],
                        Fx2[:, c * W + s - 1:c * W + e], m2[:, s - 1:e]))
                    em.opi("v", sV, lambda v, c=c, s=s, e=e, wq=wq: v.tensor_mul(
                        H2Q[:, c * 513:c * 513 + wq + 1],
                        H2Q[:, c * 513:c * 513 + wq + 1], r2l[:, s - 1:e]))
                    em.opi("v", sV, lambda v, c=c, wq=wq: v.tensor_scalar(
                        H2Q[:, c * 513:c * 513 + wq + 1],
                        H2Q[:, c * 513:c * 513 + wq + 1],
                        vecs["ln2g"][:, c:c + 1], vecs["ln2b"][:, c:c + 1],
                        ALU.mult, ALU.add))
                    em.wait("v", sV)
                    em.opi("v", sV, lambda v, c=c: v.tensor_single_scalar(
                        H2Q[:, c * 513:c * 513 + 1], H2Q[:, c * 513:c * 513 + 1],
                        cc(C_MASK), ALU.mult))
            else:
                for c in range(NCK):
                    em.opi("v", sV, lambda v, c=c, s=s, e=e, wq=wq: v.tensor_sub(
                        H2Q[:, c * 513 + 1:c * 513 + wq + 1],
                        Fx2[:, c * W + s:c * W + e], m2[:, s:e]))
                    em.opi("v", sV, lambda v, c=c, s=s, e=e, wq=wq: v.tensor_mul(
                        H2Q[:, c * 513 + 1:c * 513 + wq + 1],
                        H2Q[:, c * 513 + 1:c * 513 + wq + 1], r2l[:, s:e]))
                    em.opi("v", sV, lambda v, c=c, wq=wq: v.tensor_scalar(
                        H2Q[:, c * 513 + 1:c * 513 + wq + 1],
                        H2Q[:, c * 513 + 1:c * 513 + wq + 1],
                        vecs["ln2g"][:, c:c + 1], vecs["ln2b"][:, c:c + 1],
                        ALU.mult, ALU.add))
                em.wait("v", sV)
                em.opi("v", sV, lambda v, s=s: v.tensor_single_scalar(
                    JNK[:, 0:NCK], STASH[:, 0:NCK], m2[:, s - 1:s], ALU.subtract))
                em.wait("v", sV)
                em.opi("v", sV, lambda v, s=s: v.tensor_single_scalar(
                    JNK[:, 0:NCK], JNK[:, 0:NCK], r2l[:, s - 1:s], ALU.mult))
                em.wait("v", sV)
                for c in range(NCK):
                    em.opi("v", sV, lambda v, c=c: v.tensor_scalar(
                        H2Q[:, c * 513:c * 513 + 1], JNK[:, c:c + 1],
                        vecs["ln2g"][:, c:c + 1], vecs["ln2b"][:, c:c + 1],
                        ALU.mult, ALU.add))
                em.wait("v", sV)
            for c in range(NCK):
                em.opi("v", sV, lambda v, c=c, wq=wq: v.tensor_sub(
                    JNK[:, 0:wq], H2Q[:, c * 513 + 1:c * 513 + 1 + wq],
                    H2Q[:, c * 513:c * 513 + wq]))
                em.opi("v", sV, lambda v, c=c, wq=wq: v.scalar_tensor_tensor(
                    T2Q[:, c * 512:c * 512 + wq], JNK[:, 0:wq],
                    vecs["cmuk"][:, c:c + 1], H2Q[:, c * 513:c * 513 + wq],
                    ALU.mult, ALU.add))
            xk2_n = sV.n
            for dc in range(NCK):
                w_w, c_w = stream_wblk(wt["cwk"], dc, 0)
                bi = dc % 4
                acquire(bi)
                em.wait("t", qG, c_w)
                em.wait("t", sV, xk2_n)
                mm_chain(bi, [(WBF[w_w][:, k * 128:(k + 1) * 128],
                               T2Q[:, k * 512:k * 512 + wq]) for k in range(NCK)], wq)
                et_guard.use(em, "v", 0)
                em.wait("v", sT)
                em.opi("v", sV, lambda v, bi=bi, wq=wq: v.tensor_single_scalar(
                    ET[0][:, 0:wq], psb[bi][:, 0:wq], 0.0, ALU.max))
                release(bi, sV)
                em.wait("s", sV)
                em.opi("s", sS, lambda s_, dc=dc, wq=wq: s_.activation(
                    K2Q[:, dc * 512:dc * 512 + wq], ET[0][:, 0:wq], ACT.Square))
                et_guard.done(0, sS)
                wbf_guard.done(w_w, sT)
            k2_n = sS.n
            em.wait("v", sT)
            for c in range(NCK):
                em.opi("v", sV, lambda v, c=c, wq=wq: v.tensor_sub(
                    JNK[:, 0:wq], H2Q[:, c * 513 + 1:c * 513 + 1 + wq],
                    H2Q[:, c * 513:c * 513 + wq]))
                em.opi("v", sV, lambda v, c=c, wq=wq: v.scalar_tensor_tensor(
                    T2Q[:, c * 512:c * 512 + wq], JNK[:, 0:wq],
                    vecs["cmur"][:, c:c + 1], H2Q[:, c * 513:c * 513 + wq],
                    ALU.mult, ALU.add))
            xr2_n = sV.n
            for dc in range(NCK):
                w_r2, c_r2 = stream_wblk(wt["cwr"], dc, 0)
                w_v2, c_v2 = stream_wblk(wt["cwv"], dc, 1)
                b0, b1 = (dc % 2) * 2, (dc % 2) * 2 + 1
                acquire(b0)
                em.wait("t", qG, c_r2)
                em.wait("t", sV, xr2_n)
                mm_chain(b0, [(WBF[w_r2][:, k * 128:(k + 1) * 128],
                               T2Q[:, k * 512:k * 512 + wq]) for k in range(NCK)], wq)
                acquire(b1)
                em.wait("t", qG, c_v2)
                em.wait("t", sS, k2_n)
                mm_chain(b1, [(WBF[w_v2][:, k * 128:(k + 1) * 128],
                               K2Q[:, k * 512:k * 512 + wq]) for k in range(NCK)], wq)
                et_guard.use(em, "s", 0)
                em.wait("s", sT)
                em.opi("s", sS, lambda s_, b0=b0, wq=wq: s_.activation(
                    ET[0][:, 0:wq], psb[b0][:, 0:wq], ACT.Sigmoid))
                release(b0, sS)
                em.wait("v", sS)
                em.opi("v", sV, lambda v, b1=b1, wq=wq: v.tensor_mul(
                    ET[0][:, 0:wq], ET[0][:, 0:wq], psb[b1][:, 0:wq]))
                em.opi("v", sV, lambda v, dc=dc, s=s, wq=wq: v.tensor_add(
                    Fx2[:, dc * W + s:dc * W + s + wq],
                    Fx2[:, dc * W + s:dc * W + s + wq], ET[0][:, 0:wq]))
                release(b1, sV)
                et_guard.done(0, sV)
                wbf_guard.done(w_r2, sT)
                wbf_guard.done(w_v2, sT)
            pa_pump(4)
        for c in range(NCK):
            em.wait("s", sV)
            em.dma("s", qA, JNK[:, 0:W], xT[c * 128:(c + 1) * 128, :])
            em.wait("v", qA)
            em.opi("v", sV, lambda v, c=c: v.tensor_add(
                Fx2[:, c * W:(c + 1) * W], Fx2[:, c * W:(c + 1) * W], JNK[:, 0:W]))
        tap("x1", Fx2)
        em.flush()
        stk4.close()
        stkM2.close()
        stk3w.close()

        stkRX = ExitStack()
        REPX = stkRX.enter_context(nc.sbuf_tensor("REPX", [128, 3 * W], F32, side="right"))
        stkXQ = ExitStack()
        XQ = stkXQ.enter_context(nc.sbuf_tensor("XQ", [128, NCK * W], BF16, side="right"))
        stk5 = ExitStack()
        C5W = max((NCK // 2) * W + 2 * W, 2 * W + 3 * NCK * 128)
        C5 = stk5.enter_context(nc.sbuf_tensor("C5", [128, C5W], F32, side="right"))
        rep5 = stk5.enter_context(nc.sbuf_tensor("rep5", [128, 3 * W], F32, side="right"))
        RI = stk5.enter_context(nc.sbuf_tensor("RI", [128, NCK * 128], I32, side="right"))

        em.phase()
        pa_pump(pa_total)
        em.wait("v", sS)
        for mi, key in enumerate(["g", "u", "d"]):
            s0, e0 = mi * tiles_per_mat, (mi + 1) * tiles_per_mat
            em.opi("v", sV, lambda v, s0=s0, e0=e0: v.reduce_sum(
                STAT[:, 254:255], STAT[:, s0:e0], AXX))
            acquire(6)
            em.wait("t", sV)
            em.opi("t", sT, lambda t: t.matmul(
                psb[6][0:1, 0:1], cc(C_ONE), STAT[:, 254:255], start=True, stop=True))
            em.wait("v", sT)
            em.opi("v", sV, lambda v: v.tensor_copy(ROWS[0:1, 600:601], psb[6][0:1, 0:1]))
            release(6, sV)
            acquire(7)
            em.wait("t", sV)
            em.opi("t", sT, lambda t: t.matmul(
                psb[7][:, 0:1], ROWS[0:1, 0:128], ROWS[0:1, 600:601],
                start=True, stop=True))
            em.wait("v", sT)
            em.opi("v", sV, lambda v, key=key: v.tensor_copy(cc(C_ISW[key]), psb[7][:, 0:1]))
            release(7, sV)
            em.wait("v", sV)
            em.opi("v", sV, lambda v, key=key: v.tensor_single_scalar(
                cc(C_ISW[key]), cc(C_ISW[key]), 1.0 / (H * D), ALU.mult))
            em.wait("v", sV)
            em.opi("v", sV, lambda v, key=key: v.tensor_single_scalar(
                cc(C_ISW[key]), cc(C_ISW[key]), 1e-5, ALU.max))
            em.wait("v", sV)
            em.opi("v", sV, lambda v, key=key: v.tensor_single_scalar(
                cc(C_TAU[key]), cc(C_ISW[key]), 0.5, ALU.mult))
            em.wait("v", sV)
            em.opi("v", sV, lambda v, key=key: v.tensor_single_scalar(
                cc(C_NTAU[key]), cc(C_TAU[key]), -1.0, ALU.mult))
            em.wait("v", sV)

        S1c = rep5[:, 1 * W:2 * W]
        S2c = rep5[:, 2 * W:3 * W]
        stats_sums(Fx2[:, :], C5, S1c, S2c)
        m3 = rep5[:, 0 * W:1 * W]
        rrho = REPX[:, 0 * W:1 * W]
        FREP = REPX[:, 1 * W:2 * W]
        IXR = REPX[:, 2 * W:3 * W]
        tpc = S1c
        em.opi("v", sV, lambda v: v.tensor_single_scalar(m3, S1c, 1.0 / D, ALU.mult))
        em.opi("v", sV, lambda v: v.tensor_single_scalar(S2c, S2c, 1.0 / D, ALU.mult))
        em.opi("v", sV, lambda v: v.tensor_mul(tpc, m3, m3))
        em.opi("v", sV, lambda v: v.tensor_sub(S2c, S2c, tpc))
        em.wait("s", sV)
        em.opi("s", sS, lambda s: s.activation(rrho, S2c, ACT.Sqrt, bias=cc(C_EPS5)))
        em.wait("v", sS)
        em.opi("v", sV, lambda v: v.reciprocal(rrho, rrho))
        em.opi("v", sV, lambda v: v.tensor_mul(tpc, rrho, rrho))
        em.opi("v", sV, lambda v: v.tensor_mul(tpc, S2c, tpc))
        em.wait("s", sV)
        em.opi("s", sS, lambda s: s.activation(tpc, tpc, ACT.Sqrt, bias=cc(C_EPS6)))
        em.wait("v", sS)
        em.opi("v", sV, lambda v: v.reciprocal(tpc, tpc))
        em.opi("v", sV, lambda v: v.tensor_mul(rrho, rrho, tpc))

        MXP = C5[:, 0:W]
        MNP = C5[:, W:2 * W]
        TQ = C5[:, 2 * W:2 * W + NCK * 128]
        RF = C5[:, 2 * W + NCK * 128:2 * W + 2 * NCK * 128]
        FIX = C5[:, 2 * W + 2 * NCK * 128:2 * W + 3 * NCK * 128]
        assert 2 * W + 3 * NCK * 128 <= C5W
        TREE = C5[:, 2 * W:2 * W + (NCK // 2) * W]
        half = NCK // 2
        for op, dstP in ((ALU.max, MXP), (ALU.min, MNP)):
            em.opi("v", sV, lambda v, op=op: v.tensor_tensor(
                TREE[:, 0:half * W], Fx2[:, 0:half * W], Fx2[:, half * W:NCK * W], op))
            n = half
            while n > 1:
                h2 = n // 2
                em.opi("v", sV, lambda v, h2=h2, n=n, op=op: v.tensor_tensor(
                    TREE[:, 0:h2 * W], TREE[:, 0:h2 * W], TREE[:, h2 * W:n * W], op))
                n = h2
            em.opi("v", sV, lambda v, dstP=dstP: v.tensor_copy(dstP[:, 0:W], TREE[:, 0:W]))
        for j in range(NBLK):
            acquire(4)
            em.wait("t", sV)
            em.opi("t", sT, lambda t, j=j: t.transpose(
                psb[4][:, 0:128], MXP[:, R0 + j * 128:R0 + (j + 1) * 128], IDN[:]))
            em.wait("v", sT)
            em.opi("v", sV, lambda v, j=j: v.reduce_max(
                MAXC[:, j:j + 1], psb[4][:, 0:128], AXX))
            release(4, sV)
            acquire(5)
            em.wait("t", sV)
            em.opi("t", sT, lambda t, j=j: t.transpose(
                psb[5][:, 0:128], MNP[:, R0 + j * 128:R0 + (j + 1) * 128], IDN[:]))
            em.wait("v", sT)
            em.opi("v", sV, lambda v, j=j: v.tensor_reduce(
                MAXC[:, NBLK + j:NBLK + j + 1], psb[5][:, 0:128], AXX, ALU.min))
            release(5, sV)
        cols_to_rep(NBLK, tpc, R0, 0)
        cols_to_rep(NBLK, FREP, R0, NBLK)
        em.opi("v", sV, lambda v: v.tensor_sub(tpc, tpc, m3))
        em.opi("v", sV, lambda v: v.tensor_sub(FREP, m3, FREP))
        em.opi("v", sV, lambda v: v.tensor_max(tpc, tpc, FREP))
        em.opi("v", sV, lambda v: v.tensor_mul(tpc, tpc, rrho))
        em.opi("v", sV, lambda v: v.tensor_single_scalar(tpc, tpc, 1e-5, ALU.max))
        em.opi("v", sV, lambda v: v.tensor_single_scalar(IXR, tpc, 1.0 / 127.0, ALU.mult))
        em.opi("v", sV, lambda v: v.reciprocal(tpc, tpc))
        em.opi("v", sV, lambda v: v.tensor_mul(tpc, tpc, rrho))
        em.opi("v", sV, lambda v: v.tensor_single_scalar(FREP, tpc, 127.0, ALU.mult))

        em.wait("s", sV)
        em.dma("s", qA, x1_d.rearrange("(g p) t -> p g t", p=128),
               Fx2[:, :].rearrange("p (g t) -> p g t", g=NCK))
        x1_out_n = qA.n

        for (s, e) in NC_8:
            wq = e - s
            for c in range(NCK):
                em.opi("v", sV, lambda v, c=c, s=s, wq=wq: v.tensor_sub(
                    TQ[:, c * 128:c * 128 + wq], Fx2[:, c * W + s:c * W + s + wq],
                    m3[:, s:s + wq]))
                em.opi("v", sV, lambda v, c=c, s=s, wq=wq: v.tensor_mul(
                    TQ[:, c * 128:c * 128 + wq], TQ[:, c * 128:c * 128 + wq],
                    FREP[:, s:s + wq]))
            em.wait("s", sV)
            em.opi("s", sS, lambda s_: s_.copy(RI[:, :], TQ[:, 0:NCK * 128]))
            em.opi("s", sS, lambda s_: s_.copy(RF[:, 0:NCK * 128], RI[:, :]))
            em.wait("v", sS)
            em.opi("v", sV, lambda v: v.tensor_sub(
                TQ[:, 0:NCK * 128], TQ[:, 0:NCK * 128], RF[:, 0:NCK * 128]))
            em.opi("v", sV, lambda v: v.tensor_single_scalar(
                FIX[:, 0:NCK * 128], TQ[:, 0:NCK * 128], 0.5, ALU.is_gt))
            em.opi("v", sV, lambda v: v.tensor_add(
                RF[:, 0:NCK * 128], RF[:, 0:NCK * 128], FIX[:, 0:NCK * 128]))
            em.opi("v", sV, lambda v: v.tensor_scalar(
                TQ[:, 0:NCK * 128], TQ[:, 0:NCK * 128], -0.5, -1.0, ALU.is_ge, ALU.add))
            for c in range(NCK):
                em.opi("v", sV, lambda v, c=c, s=s, wq=wq: v.tensor_add(
                    XQ[:, c * W + s:c * W + s + wq], RF[:, c * 128:c * 128 + wq],
                    TQ[:, c * 128:c * 128 + wq]))
        if dbg == "xq":
            em.wait("s", sV)
            em.dma("s", qA, dbgq_t.rearrange("(g p) t -> p g t", p=128),
                   XQ[:, :].rearrange("p (g t) -> p g t", g=NCK))
        if dbg == "cc":
            em.wait("s", sV)
            em.dma("s", qA, dbgc_t[:, :], CC[:, :])
        em.flush()
        stk5.close()
        stkFx2.close()
        stkPA.close()

        stk6 = ExitStack()
        GW = stk6.enter_context(nc.sbuf_tensor("GW", [128, 4 * NCK * 128], F32,
                                               side="left"))
        QBF = [stk6.enter_context(nc.sbuf_tensor(f"QBF{i}", [128, NCK * 128], BF16,
                                                 side="left")) for i in range(4)]
        QTT = [stk6.enter_context(nc.sbuf_tensor(f"QTT{i}", [128, NCK * 128], F32,
                                                 side="left")) for i in range(4)]
        GSB = stk6.enter_context(nc.sbuf_tensor("GSB", [128, 2 * NRT], F32, side="right"))
        GUB = [stk6.enter_context(nc.sbuf_tensor(f"GUB{i}", [128, NRT], F32, side="right"))
               for i in range(2)]
        AC3 = stk6.enter_context(nc.sbuf_tensor("AC3", [128, 3 * NRT], F32, side="right"))
        gw_guard = Guard(6)
        qbf_guard = Guard(6)
        qtt_guard = Guard(3)
        gub_guard = Guard(2)

        gtf = gt.rearrange("a b c d -> (a b c d)")
        utf = ut.rearrange("a b c d -> (a b c d)")

        def quantize_gu_blk(src_flat, hb, mat, key):
            half = mat * 2 + hb % 2
            qt = (hb % 2) * 2
            wi = mat * 2 + hb % 2
            stg = GW[:, half * NCK * 128:(half + 1) * NCK * 128]
            gw_guard.use(em, "y", half)
            base = hb * 128 * NCK * 128
            em.dma("y", qW, stg, src_flat[base:base + 128 * NCK * 128]
                   .rearrange("(p m) -> p m", p=128))
            em.wait("v", qW)
            qtt_guard.use(em, "v", hb % 2)
            em.opi("v", sV, lambda v, stg=stg, key=key, qt=qt: v.tensor_single_scalar(
                QTT[qt][:, :], stg, cc(C_TAU[key]), ALU.is_ge))
            em.opi("v", sV, lambda v, stg=stg, key=key, qt=qt: v.tensor_scalar(
                QTT[qt + 1][:, :], stg, cc(C_NTAU[key]), -1.0, ALU.is_ge, ALU.add))
            gw_guard.done(half, sV)
            em.wait("g", sV)
            qbf_guard.use(em, "g", wi)
            em.opi("g", sG, lambda g, wi=wi, qt=qt: g.tensor_add(
                QBF[wi][:, :], QTT[qt][:, :], QTT[qt + 1][:, :]))
            qtt_guard.done(hb % 2, sG)
            return wi, sG.n

        em.phase()
        em.wait("y", qA, x1_out_n)
        em.wait("v", qA, x1_out_n)
        em.wait("g", qA, x1_out_n)
        em.wait("s", qA, x1_out_n)
        qinfo = {}
        for j in range(min(1, HBN)):
            qinfo[j] = (quantize_gu_blk(gtf, j, 0, "g"), quantize_gu_blk(utf, j, 1, "u"))
        for hb in range(HBN):
            (wg_i, qg), (wu_i, qu) = qinfo.pop(hb)
            if hb + 1 < HBN:
                qinfo[hb + 1] = (quantize_gu_blk(gtf, hb + 1, 0, "g"),
                                 quantize_gu_blk(utf, hb + 1, 1, "u"))
            gub_guard.use(em, "v", hb % 2)
            for ci, (s, e) in enumerate(NC_RE):
                wq = e - s
                so = s - R0
                b0 = (hb % 2) * 2
                b1 = b0 + 1
                acquire(b0)
                em.wait("t", sG, qg)
                mm_chain(b0, [(QBF[wg_i][:, k * 128:(k + 1) * 128],
                               XQ[:, k * W + s:k * W + e]) for k in range(NCK)], wq)
                acquire(b1)
                em.wait("t", sG, qu)
                mm_chain(b1, [(QBF[wu_i][:, k * 128:(k + 1) * 128],
                               XQ[:, k * W + s:k * W + e]) for k in range(NCK)], wq)
                em.wait("v", sT)
                em.opi("v", sV, lambda v, b0=b0, so=so, wq=wq, s=s: v.scalar_tensor_tensor(
                    GSB[:, so:so + wq], psb[b0][:, 0:wq], cc(C_ISW["g"]),
                    IXR[:, s:s + wq], ALU.mult, ALU.mult))
                em.opi("v", sV, lambda v, b1=b1, so=so, wq=wq, s=s: v.scalar_tensor_tensor(
                    GSB[:, NRT + so:NRT + so + wq], psb[b1][:, 0:wq], cc(C_ISW["u"]),
                    IXR[:, s:s + wq], ALU.mult, ALU.mult))
                release(b0, sV)
                release(b1, sV)
                em.wait("s", sV)
                em.opi("s", sS, lambda s_, so=so, wq=wq: s_.activation(
                    GSB[:, so:so + wq], GSB[:, so:so + wq], ACT.Silu))
                em.wait("v", sS)
                em.opi("v", sV, lambda v, so=so, wq=wq, hb=hb: v.tensor_mul(
                    GUB[hb % 2][:, so:so + wq], GSB[:, so:so + wq],
                    GSB[:, NRT + so:NRT + so + wq]))
            qbf_guard.done(wg_i, sT)
            qbf_guard.done(wu_i, sT)
            em.wait("s", sV)
            em.dma("s", qA, gu_d[hb, :, :], GUB[hb % 2][:, :])
            em.opi("s", sS, lambda s_, hb=hb: s_.activation(
                JNK[:, 0:NRT], GUB[hb % 2][:, :], ACT.Square))
            em.wait("v", sS)
            if hb == 0:
                em.opi("v", sV, lambda v: v.tensor_copy(AC3[:, 0:NRT], JNK[:, 0:NRT]))
                em.opi("v", sV, lambda v, hb=hb: v.tensor_copy(
                    AC3[:, NRT:2 * NRT], GUB[hb % 2][:, :]))
                em.opi("v", sV, lambda v, hb=hb: v.tensor_copy(
                    AC3[:, 2 * NRT:3 * NRT], GUB[hb % 2][:, :]))
            else:
                em.opi("v", sV, lambda v: v.tensor_add(
                    AC3[:, 0:NRT], AC3[:, 0:NRT], JNK[:, 0:NRT]))
                em.opi("v", sV, lambda v, hb=hb: v.tensor_max(
                    AC3[:, NRT:2 * NRT], AC3[:, NRT:2 * NRT], GUB[hb % 2][:, :]))
                em.opi("v", sV, lambda v, hb=hb: v.tensor_tensor(
                    AC3[:, 2 * NRT:3 * NRT], AC3[:, 2 * NRT:3 * NRT],
                    GUB[hb % 2][:, :], ALU.min))
            gub_guard.done(hb % 2, sV)

        SQ2 = rep5 if False else None
        rep_from_psum_row(AC3[:, 0:NRT], NRT, rrho[:, 0:1] if False else rrho)
        for j in range(NBLK):
            acquire(4)
            em.wait("t", sV)
            em.opi("t", sT, lambda t, j=j: t.transpose(
                psb[4][:, 0:128], AC3[:, NRT + j * 128:NRT + (j + 1) * 128], IDN[:]))
            em.wait("v", sT)
            em.opi("v", sV, lambda v, j=j: v.reduce_max(
                MAXC[:, j:j + 1], psb[4][:, 0:128], AXX))
            release(4, sV)
            acquire(5)
            em.wait("t", sV)
            em.opi("t", sT, lambda t, j=j: t.transpose(
                psb[5][:, 0:128], AC3[:, 2 * NRT + j * 128:2 * NRT + (j + 1) * 128],
                IDN[:]))
            em.wait("v", sT)
            em.opi("v", sV, lambda v, j=j: v.tensor_reduce(
                MAXC[:, NBLK + j:NBLK + j + 1], psb[5][:, 0:128], AXX, ALU.min))
            release(5, sV)
        em.wait("v", sV)
        em.opi("v", sV, lambda v: v.tensor_single_scalar(
            MAXC[:, NBLK:2 * NBLK], MAXC[:, NBLK:2 * NBLK], -1.0, ALU.mult))
        em.wait("v", sV)
        em.opi("v", sV, lambda v: v.tensor_max(
            MAXC[:, 0:NBLK], MAXC[:, 0:NBLK], MAXC[:, NBLK:2 * NBLK]))
        em.wait("v", sV)
        cols_to_rep(NBLK, FREP, 0)
        em.opi("v", sV, lambda v: v.tensor_single_scalar(
            rrho[:, 0:NRT], rrho[:, 0:NRT], 1.0 / H, ALU.mult))
        em.wait("s", sV)
        em.opi("s", sS, lambda s_: s_.activation(
            rrho[:, 0:NRT], rrho[:, 0:NRT], ACT.Sqrt, bias=cc(C_EPS6)))
        em.wait("v", sS)
        em.opi("v", sV, lambda v: v.reciprocal(rrho[:, 0:NRT], rrho[:, 0:NRT]))
        em.opi("v", sV, lambda v: v.tensor_mul(
            FREP[:, 0:NRT], FREP[:, 0:NRT], rrho[:, 0:NRT]))
        em.opi("v", sV, lambda v: v.tensor_single_scalar(
            FREP[:, 0:NRT], FREP[:, 0:NRT], 1e-5, ALU.max))
        em.opi("v", sV, lambda v: v.tensor_single_scalar(
            IXR[:, 0:NRT], FREP[:, 0:NRT], 1.0 / 127.0, ALU.mult))
        em.opi("v", sV, lambda v: v.reciprocal(FREP[:, 0:NRT], FREP[:, 0:NRT]))
        em.opi("v", sV, lambda v: v.tensor_mul(
            FREP[:, 0:NRT], FREP[:, 0:NRT], rrho[:, 0:NRT]))
        em.opi("v", sV, lambda v: v.tensor_single_scalar(
            FREP[:, 0:NRT], FREP[:, 0:NRT], 127.0, ALU.mult))
        if dbg == "gu":
            em.wait("s", sV)
            em.wait("s", qA)
            em.dma("s", qA, dbgg_t[:, :, :], gu_d[:, :, :])
        if dbg == "repx":
            em.wait("s", sV)
            em.dma("s", qA, dbgr_t[:, :], REPX[:, :])
        em.flush()
        stk6.close()
        stkXQ.close()

        stkFx3 = ExitStack()
        Fx3 = stkFx3.enter_context(nc.sbuf_tensor("Fx3", [128, NCK * W], F32, side="left"))
        stk7 = ExitStack()
        XQ2 = [stk7.enter_context(nc.sbuf_tensor(f"XQ2_{i}", [128, HBS * NRT], BF16,
                                                 side="right")) for i in range(3)]
        DW = [stk7.enter_context(nc.sbuf_tensor(f"DW{i}", [128, HBS * D], BF16,
                                                side="right")) for i in range(3)]
        DSTG = stk7.enter_context(nc.sbuf_tensor("DSTG", [128, D], F32, side="left"))
        QT7 = [stk7.enter_context(nc.sbuf_tensor(f"QT7_{i}", [128, D], F32, side="left"))
               for i in range(2)]
        GIN = [stk7.enter_context(nc.sbuf_tensor(f"GIN{i}", [128, NRT], F32, side="right"))
               for i in range(2)]
        RI2 = [stk7.enter_context(nc.sbuf_tensor("RI2_0", [128, NRT], I32, side="right"))] * 2
        RF2 = [stk7.enter_context(nc.sbuf_tensor("RF2_0", [128, NRT], F32, side="right"))] * 2
        ET7 = stk7.enter_context(nc.sbuf_tensor("ET7", [128, 512], F32, side="right"))

        dstg_guard = Guard(1)
        qt7_guard = Guard(2)
        dw_guard = Guard(3)
        xq2_guard = Guard(3)
        gin_guard = Guard(2)
        rif_guard = Guard(1)
        et7_guard = Guard(1)

        dtf = dtw.rearrange("a b -> (a b)")
        n_groups = HBN // HBG
        subs = HBG // HBS

        em.phase()
        em.wait("y", qA)
        em.wait("v", qA)
        em.wait("g", qA)
        em.wait("s", qA)
        em.dma("s", qA, Fx3[:, :].rearrange("p (g t) -> p g t", g=NCK),
               x1_d.rearrange("(g p) t -> p g t", p=128))
        x1_in_n = qA.n
        for gi in range(n_groups):
            sub_ready = []
            for si in range(subs):
                p = (gi * subs + si) % 3
                dw_guard.use(em, "g", p)
                xq2_guard.use(em, "g", p)
                for k in range(HBS):
                    hb = gi * HBG + si * HBS + k
                    dstg_guard.use(em, "y", 0)
                    base = hb * 128 * D
                    em.dma("y", qW, DSTG[:, :], dtf[base:base + 128 * D]
                           .rearrange("(p m) -> p m", p=128))
                    em.wait("v", qW)
                    qt7_guard.use(em, "v", 0)
                    em.opi("v", sV, lambda v: v.tensor_single_scalar(
                        QT7[0][:, :], DSTG[:, :], cc(C_TAU["d"]), ALU.is_ge))
                    em.opi("v", sV, lambda v: v.tensor_scalar(
                        QT7[1][:, :], DSTG[:, :], cc(C_NTAU["d"]), -1.0,
                        ALU.is_ge, ALU.add))
                    dstg_guard.done(0, sV)
                    em.wait("g", sV)
                    em.opi("g", sG, lambda g, p=p, k=k: g.tensor_add(
                        DW[p][:, k * D:(k + 1) * D], QT7[0][:, :], QT7[1][:, :]))
                    qt7_guard.done(0, sG)
                    kb = k % 2
                    gin_guard.use(em, "g", kb)
                    em.dma("s", qA, GIN[kb][:, :], gu_d[hb, :, :])
                    em.wait("v", qA)
                    em.opi("v", sV, lambda v, kb=kb: v.tensor_mul(
                        GIN[kb][:, :], GIN[kb][:, :], FREP[:, 0:NRT]))
                    rif_guard.use(em, "s", 0)
                    em.wait("s", sV)
                    em.opi("s", sS, lambda s_, kb=kb: s_.copy(RI2[kb][:, :], GIN[kb][:, :]))
                    em.opi("s", sS, lambda s_, kb=kb: s_.copy(RF2[kb][:, :], RI2[kb][:, :]))
                    em.wait("v", sS)
                    em.opi("v", sV, lambda v, kb=kb: v.tensor_sub(
                        GIN[kb][:, :], GIN[kb][:, :], RF2[kb][:, :]))
                    em.opi("v", sV, lambda v, kb=kb: v.tensor_single_scalar(
                        JNK[:, 0:NRT], GIN[kb][:, :], 0.5, ALU.is_gt))
                    em.opi("v", sV, lambda v, kb=kb: v.tensor_add(
                        RF2[kb][:, :], RF2[kb][:, :], JNK[:, 0:NRT]))
                    em.opi("v", sV, lambda v, kb=kb: v.tensor_scalar(
                        GIN[kb][:, :], GIN[kb][:, :], -0.5, -1.0,
                        ALU.is_ge, ALU.add))
                    em.wait("g", sV)
                    em.opi("g", sG, lambda g, p=p, k=k, kb=kb: g.tensor_add(
                        XQ2[p][:, k * NRT:(k + 1) * NRT], RF2[kb][:, :], GIN[kb][:, :]))
                    rif_guard.done(0, sG)
                    gin_guard.done(kb, sG)
                sub_ready.append((p, sG.n, sV.n))
                if gi == 0 and si == 0 and dbg == "xq2":
                    em.wait("s", sV)
                    em.dma("s", qA, dbgx2_t[:, :], XQ2[0][:, :])
                if gi == 0 and si == 0 and dbg == "dw":
                    em.wait("s", sG)
                    em.dma("s", qA, dbgdw_t[:, :], DW[0][:, :])
            for ci, (s, e) in enumerate(NC_RE):
                wq = e - s
                for dc in range(NCK):
                    bi = dc % 4
                    acquire(bi)
                    for si in range(subs):
                        p, gn, vn = sub_ready[si]
                        em.wait("t", sG, gn)
                        em.wait("t", sV, vn)
                        so = s - R0
                        for k in range(HBS):
                            st = (si == 0 and k == 0)
                            sp = (si == subs - 1 and k == HBS - 1)
                            l = DW[p][:, k * D + dc * 128:k * D + (dc + 1) * 128]
                            r = XQ2[p][:, k * NRT + so:k * NRT + so + wq]
                            if sp:
                                em.opi("t", sT, lambda t, l=l, r=r, st=st, bi=bi, wq=wq:
                                       t.matmul(psb[bi][:, 0:wq], l, r, start=st,
                                                stop=True))
                            else:
                                em.op("t", lambda t, l=l, r=r, st=st, bi=bi, wq=wq:
                                      t.matmul(psb[bi][:, 0:wq], l, r, start=st,
                                               stop=False))
                    et7_guard.use(em, "v", 0)
                    em.wait("v", sT)
                    em.wait("v", qA, x1_in_n)
                    em.opi("v", sV, lambda v, bi=bi, wq=wq, s=s: v.scalar_tensor_tensor(
                        ET7[:, 0:wq], psb[bi][:, 0:wq], cc(C_ISW["d"]),
                        IXR[:, s - R0:s - R0 + wq], ALU.mult, ALU.mult))
                    em.opi("v", sV, lambda v, dc=dc, s=s, wq=wq: v.tensor_add(
                        Fx3[:, dc * W + s:dc * W + s + wq],
                        Fx3[:, dc * W + s:dc * W + s + wq], ET7[:, 0:wq]))
                    if dbg == "et7" and gi == 0 and ci == 0 and dc == 0:
                        em.wait("s", sV)
                        em.dma("s", qA, dbge_t[:, 0:wq], ET7[:, 0:wq])
                    if dbg == "draw" and gi == 0 and ci == 0 and dc == 0:
                        em.opi("v", sV, lambda v, bi=bi, wq=wq: v.tensor_copy(
                            JNK[:, 0:wq], psb[bi][:, 0:wq]))
                        em.wait("s", sV)
                        em.dma("s", qA, dbge_t[:, 0:wq], JNK[:, 0:wq])
                    release(bi, sV)
                    et7_guard.done(0, sV)
            for p, _, _ in sub_ready:
                dw_guard.done(p, sT)
                xq2_guard.done(p, sT)
        em.flush()
        stk7.close()
        stkRX.close()

        stk8 = ExitStack()
        OUTT = [stk8.enter_context(nc.sbuf_tensor(f"OUTT{i}", [128, D], F32, side="right"))
                for i in range(2)]
        outt_guard = Guard(2)
        em.phase()
        em.wait("t", sV)
        for j in range(NBLK):
            ob = OUTT[j % 2]
            outt_guard.use(em, "v", j % 2)
            for g in range(NCK):
                bi = 4 + (g % 2)
                acquire(bi)
                em.opi("t", sT, lambda t, g=g, j=j, bi=bi: t.transpose(
                    psb[bi][:, 0:128],
                    Fx3[:, g * W + R0 + j * 128:g * W + R0 + (j + 1) * 128], IDN[:]))
                em.wait("v", sT)
                em.opi("v", sV, lambda v, g=g, bi=bi, ob=ob: v.tensor_copy(
                    ob[:, g * 128:(g + 1) * 128], psb[bi][:, 0:128]))
                release(bi, sV)
            em.wait("s", sV)
            em.dma("s", sO, y[j * 128:(j + 1) * 128, :], ob[:, :])
            outt_guard.done(j % 2, sO)
        tap("x2", Fx3)
        em.wait("y", sO)
        em.flush()
        stk8.close()
        stkFx3.close()

    return nc


_CACHE = {}


def _blk(Wt):
    Din, Dout = Wt.shape
    K, C = Din // 128, Dout // 128
    return np.ascontiguousarray(Wt.reshape(K, 128, C, 128).transpose(2, 1, 0, 3))


def kernel(L0_attn_x, L0_attn_ln1_g, L0_attn_ln1_b, L0_attn_ln2_g, L0_attn_ln2_b,
           L0_attn_tm_w, L0_attn_tm_u, L0_attn_tm_mu_r, L0_attn_tm_mu_k,
           L0_attn_tm_mu_v, L0_attn_tm_Wr, L0_attn_tm_Wk, L0_attn_tm_Wv,
           L0_attn_tm_Wout, L0_attn_cm_mu_r, L0_attn_cm_mu_k, L0_attn_cm_Wr,
           L0_attn_cm_Wk, L0_attn_cm_Wv, L0_ffn_w_gate, L0_ffn_w_up, L0_ffn_w_down):
    x = np.ascontiguousarray(np.asarray(L0_attn_x, dtype=np.float32))
    Bsz, T, D = x.shape
    H = np.asarray(L0_ffn_w_gate).shape[0]
    n_cores = 8
    NRT = (Bsz * T) // n_cores

    import os
    dbg = os.environ.get("KDBG") or None
    key = (D, H, NRT, dbg)
    if key not in _CACHE:
        _CACHE[key] = build(D, H, NRT, dbg)
    nc = _CACHE[key]

    def P(a):
        return np.ascontiguousarray(np.asarray(a, dtype=np.float32))

    wts = {
        "wr": _blk(P(L0_attn_tm_Wr).T), "wk": _blk(P(L0_attn_tm_Wk).T),
        "wv": _blk(P(L0_attn_tm_Wv).T), "wo": _blk(P(L0_attn_tm_Wout).T),
        "cwr": _blk(P(L0_attn_cm_Wr).T), "cwk": _blk(P(L0_attn_cm_Wk).T),
        "cwv": _blk(P(L0_attn_cm_Wv).T),
        "gt": _blk(P(L0_ffn_w_gate).T), "ut": _blk(P(L0_ffn_w_up).T),
        "dtw": np.ascontiguousarray(P(L0_ffn_w_down).T),
    }
    NCK = D // 128
    vecs = {
        "ln1g": P(L0_attn_ln1_g), "ln1b": P(L0_attn_ln1_b),
        "ln2g": P(L0_attn_ln2_g), "ln2b": P(L0_attn_ln2_b),
        "tmu": P(L0_attn_tm_u), "mur": P(L0_attn_tm_mu_r),
        "muk": P(L0_attn_tm_mu_k), "muv": P(L0_attn_tm_mu_v),
        "cmur": P(L0_attn_cm_mu_r), "cmuk": P(L0_attn_cm_mu_k),
    }

    halves = T // NRT
    in_maps = []
    for c in range(n_cores):
        b, half = c // halves, c % halves
        t0 = half * NRT
        if t0 == 0:
            sl = np.concatenate([np.zeros((2, D), np.float32), x[b, 0:NRT]], axis=0)
        else:
            sl = x[b, t0 - 2:t0 + NRT]
        m = {"xT": np.ascontiguousarray(sl.T),
             "maskd": np.full((128, 1), 0.0 if t0 == 0 else 1.0, np.float32)}
        m.update(wts)
        for k, v in vecs.items():
            m["v_" + k] = np.ascontiguousarray(v.reshape(NCK, 128))
        in_maps.append(m)

    res = run_bass_kernel_spmd(nc, in_maps, core_ids=list(range(n_cores)))
    if dbg:
        kernel._dbg = [np.asarray(res.results[c]["dbg"]) for c in range(n_cores)]
    out = np.empty((Bsz, T, D), np.float32)
    for c in range(n_cores):
        b, half = c // halves, c % halves
        out[b, half * NRT:(half + 1) * NRT] = np.asarray(res.results[c]["y"])
    return out


# version 18
# speedup vs baseline: 1.3848x; 1.0371x over previous
import numpy as np
from contextlib import ExitStack

import concourse.bass as bass
import concourse.mybir as mybir
from concourse.bass_utils import run_bass_kernel_spmd

F32 = mybir.dt.float32
BF16 = mybir.dt.bfloat16
I32 = mybir.dt.int32
ALU = mybir.AluOpType
ACT = mybir.ActivationFunctionType
AXX = mybir.AxisListType.X

LNEPS8 = 18.420680743952367


class Sem:
    def __init__(self, h):
        self.h = h
        self.n = 0

    def bump(self, k=1):
        self.n += k
        return self.n


class EM:
    ENGS = ("t", "v", "s", "g", "y")

    def __init__(self, nc):
        self.nc = nc
        self.cur = None

    def phase(self):
        self.cur = {e: [] for e in self.ENGS}

    def flush(self):
        ph, self.cur = self.cur, None

        def runner(ops):
            def go(e):
                for f in ops:
                    f(e)
            return go

        with self.nc.Block() as blk:
            if ph["y"]:
                blk.sync(runner(ph["y"]))
            if ph["t"]:
                blk.tensor(runner(ph["t"]))
            if ph["v"]:
                blk.vector(runner(ph["v"]))
            if ph["s"]:
                blk.scalar(runner(ph["s"]))
            if ph["g"]:
                blk.gpsimd(runner(ph["g"]))

    def op(self, eng, fn):
        self.cur[eng].append(fn)

    def wait(self, eng, sem, n=None):
        v = sem.n if n is None else n
        if v > 0:
            self.cur[eng].append(lambda e: e.wait_ge(sem.h, v))

    def opi(self, eng, sem, fn, k=1):
        n = sem.bump(k)
        self.cur[eng].append(lambda e: fn(e).then_inc(sem.h, k))
        return n

    def vs(self):
        self.wait("v", None) if False else None

    def dma(self, eng, sem, out_ap, in_ap, **kw):
        return self.opi(eng, sem, lambda e: e.dma_start(out=out_ap, in_=in_ap, **kw), k=16)


class Guard:
    def __init__(self, n):
        self.rec = [None] * n

    def use(self, em, eng, i):
        r = self.rec[i]
        if r is not None:
            em.wait(eng, r[0], r[1])
            self.rec[i] = None

    def done(self, i, sem):
        self.rec[i] = (sem, sem.n)


def build(D, H, NRT, dbg=None):
    NCK = D // 128
    HBN = H // 128
    W = NRT + 2
    R0 = 2
    NBLK = NRT // 128

    def tchunks(s0, e0, step):
        out, s = [], s0
        while s < e0:
            out.append((s, min(s + step, e0)))
            s += step
        return out

    NC_H1 = [(0, 2)] + tchunks(R0, W, 512)
    NC_RE = tchunks(R0, W, 512)
    NC_8 = tchunks(R0, W, 128)
    HBG = min(8, HBN)
    HBS = min(4, HBG)
    PAW = 2048
    while (H * D // 128) % PAW != 0:
        PAW //= 2

    nc = bass.Bass("TRN2", target_bir_lowering=False)

    xT = nc.dram_tensor("xT", [D, W], F32, kind="ExternalInput").ap()
    wnames = ["wr", "wk", "wv", "wo", "cwr", "cwk", "cwv"]
    wt = {n: nc.dram_tensor(n, [NCK, 128, NCK, 128], F32, kind="ExternalInput").ap()
          for n in wnames}
    gt = nc.dram_tensor("gt", [HBN, 128, NCK, 128], F32, kind="ExternalInput").ap()
    ut = nc.dram_tensor("ut", [HBN, 128, NCK, 128], F32, kind="ExternalInput").ap()
    dtw = nc.dram_tensor("dtw", [H, D], F32, kind="ExternalInput").ap()
    vnames = ["ln1g", "ln1b", "ln2g", "ln2b", "tmu", "mur", "muk", "muv", "cmur", "cmuk"]
    vin = {n: nc.dram_tensor("v_" + n, [NCK, 128], F32, kind="ExternalInput").ap()
           for n in vnames}
    maskd = nc.dram_tensor("maskd", [128, 1], F32, kind="ExternalInput").ap()
    y = nc.dram_tensor("y", [NRT, D], F32, kind="ExternalOutput").ap()

    dbg_t = (nc.dram_tensor("dbg", [D, W], F32, kind="ExternalOutput").ap()
             if dbg in ("a", "h", "xmid", "x1", "x2") else None)
    dbgq_t = (nc.dram_tensor("dbg", [D, W], BF16, kind="ExternalOutput").ap()
              if dbg == "xq" else None)
    dbgg_t = (nc.dram_tensor("dbg", [HBN, 128, NRT], F32, kind="ExternalOutput").ap()
              if dbg == "gu" else None)
    dbgc_t = (nc.dram_tensor("dbg", [128, 16], F32, kind="ExternalOutput").ap()
              if dbg == "cc" else None)
    dbgr_t = (nc.dram_tensor("dbg", [128, 3 * W], F32, kind="ExternalOutput").ap()
              if dbg == "repx" else None)
    dbgx2_t = (nc.dram_tensor("dbg", [128, HBS * NRT], BF16, kind="ExternalOutput").ap()
               if dbg == "xq2" else None)
    dbgdw_t = (nc.dram_tensor("dbg", [128, HBS * D], BF16, kind="ExternalOutput").ap()
               if dbg == "dw" else None)
    dbge_t = (nc.dram_tensor("dbg", [128, 512], F32, kind="ExternalOutput").ap()
              if dbg in ("et7", "draw") else None)

    a_d = nc.dram_tensor("a_d", [D, W], F32, kind="Internal").ap()
    x1_d = nc.dram_tensor("x1_d", [D, W], F32, kind="Internal").ap()
    gu_d = nc.dram_tensor("gu_d", [HBN, 128, NRT], F32, kind="Internal").ap()
    wqd_d = nc.dram_tensor("wqd_d", [HBN, 128, D], BF16, kind="Internal").ap()

    em = EM(nc)
    es = ExitStack()
    with es:
        qW = Sem(es.enter_context(nc.semaphore("qW")))
        qA = Sem(es.enter_context(nc.semaphore("qA")))
        qG = Sem(es.enter_context(nc.semaphore("qG")))
        sT = Sem(es.enter_context(nc.semaphore("sT")))
        sV = Sem(es.enter_context(nc.semaphore("sV")))
        sS = Sem(es.enter_context(nc.semaphore("sS")))
        sG = Sem(es.enter_context(nc.semaphore("sG")))
        sO = Sem(es.enter_context(nc.semaphore("sO")))

        VEC = es.enter_context(nc.sbuf_tensor("VEC", [128, 12 * NCK], F32))
        CC = es.enter_context(nc.sbuf_tensor("CC", [128, 16], F32))
        IDN = es.enter_context(nc.sbuf_tensor("IDN", [128, 128], F32))
        STAT = es.enter_context(nc.sbuf_tensor("STAT", [128, 256], F32))
        MAXC = es.enter_context(nc.sbuf_tensor("MAXC", [128, 2 * NBLK], F32))
        JNK = es.enter_context(nc.sbuf_tensor("JNK", [128, max(PAW, NRT, W)], F32))
        ROWS = es.enter_context(nc.sbuf_tensor("ROWS", [1, 640 + max(NRT, 128)], F32))
        psb = [es.enter_context(nc.psum_tensor(f"pb{i}", [128, 512], F32)) for i in range(8)]

        C_ONE, C_EPS5, C_EPS6, C_MASK = 0, 1, 2, 3
        C_TAU = {"g": 4, "u": 5, "d": 6}
        C_NTAU = {"g": 7, "u": 8, "d": 9}
        C_ISW = {"g": 10, "u": 11, "d": 12}

        def cc(i):
            return CC[:, i:i + 1]

        free_rec = [None] * 8

        def acquire(bi):
            if free_rec[bi] is not None:
                s, n = free_rec[bi]
                em.wait("t", s, n)
                free_rec[bi] = None

        def release(bi, sem):
            free_rec[bi] = (sem, sem.n)

        def mm_chain(bi, pairs, wq):
            pairs = pairs[::-1]
            L = len(pairs)
            for i, (l, r) in enumerate(pairs[:-1]):
                em.op("t", lambda t, l=l, r=r, i=i, bi=bi, wq=wq: t.matmul(
                    psb[bi][:, 0:wq], l, r, start=(i == 0), stop=False))
            l, r = pairs[-1]
            return em.opi("t", sT, lambda t, l=l, r=r, bi=bi, wq=wq, L=L: t.matmul(
                psb[bi][:, 0:wq], l, r, start=(L == 1), stop=True))

        def rep_from_psum_row(src_ap, W_, rep):
            for (s, e) in tchunks(0, W_, 512):
                acquire(6)
                em.wait("t", sV)
                em.opi("t", sT, lambda t, s=s, e=e: t.matmul(
                    psb[6][0:1, 0:e - s], cc(C_ONE), src_ap[:, s:e], start=True, stop=True))
                em.wait("v", sT)
                em.opi("v", sV, lambda v, s=s, e=e: v.tensor_copy(
                    rep[0:1, s:e], psb[6][0:1, 0:e - s]))
                release(6, sV)
            for (s, e) in tchunks(0, W_, 512):
                acquire(7)
                em.wait("t", sV)
                em.opi("t", sT, lambda t, s=s, e=e: t.matmul(
                    psb[7][:, 0:e - s], ROWS[0:1, 0:128], rep[0:1, s:e],
                    start=True, stop=True))
                em.wait("v", sT)
                em.opi("v", sV, lambda v, s=s, e=e: v.tensor_copy(
                    rep[:, s:e], psb[7][:, 0:e - s]))
                release(7, sV)

        def stats_sums(src, scratch, S1rep, S2rep):
            half = NCK // 2
            em.opi("v", sV, lambda v: v.tensor_add(
                scratch[:, 0:half * W], src[:, 0:half * W], src[:, half * W:NCK * W]))
            n = half
            while n > 1:
                h2 = n // 2
                em.opi("v", sV, lambda v, h2=h2, n=n: v.tensor_add(
                    scratch[:, 0:h2 * W], scratch[:, 0:h2 * W], scratch[:, h2 * W:n * W]))
                n = h2
            rep_from_psum_row(scratch[:, 0:W], W, S1rep)
            em.wait("s", sT)
            em.opi("s", sS, lambda s: s.activation(
                scratch[:, 0:half * W], src[:, 0:half * W], ACT.Square))
            for i in range(half):
                if i > 0:
                    em.wait("s", sV)
                em.opi("s", sS, lambda s, i=i: s.activation(
                    JNK[:, 0:W], src[:, (half + i) * W:(half + 1 + i) * W], ACT.Square))
                em.wait("v", sS)
                em.opi("v", sV, lambda v, i=i: v.tensor_add(
                    scratch[:, i * W:(i + 1) * W], scratch[:, i * W:(i + 1) * W],
                    JNK[:, 0:W]))
            n = half
            while n > 1:
                h2 = n // 2
                em.opi("v", sV, lambda v, h2=h2, n=n: v.tensor_add(
                    scratch[:, 0:h2 * W], scratch[:, 0:h2 * W], scratch[:, h2 * W:n * W]))
                n = h2
            rep_from_psum_row(scratch[:, 0:W], W, S2rep)

        def cols_to_rep(ncols, rep_ap, off, src_off=0):
            for j in range(ncols):
                acquire(6)
                em.wait("t", sV)
                em.opi("t", sT, lambda t, j=j, src_off=src_off: t.transpose(
                    psb[6][0:1, 0:128], MAXC[:, src_off + j:src_off + j + 1], IDN[:]))
                em.wait("v", sT)
                em.opi("v", sV, lambda v, j=j: v.tensor_copy(
                    ROWS[0:1, 640 + j * 128:640 + (j + 1) * 128], psb[6][0:1, 0:128]))
                release(6, sV)
            for j in range(ncols):
                acquire(7)
                em.wait("t", sV)
                em.opi("t", sT, lambda t, j=j: t.matmul(
                    psb[7][:, 0:128], ROWS[0:1, 0:128],
                    ROWS[0:1, 640 + j * 128:640 + (j + 1) * 128], start=True, stop=True))
                em.wait("v", sT)
                em.opi("v", sV, lambda v, j=j, off=off: v.tensor_copy(
                    rep_ap[:, off + j * 128:off + (j + 1) * 128], psb[7][:, 0:128]))
                release(7, sV)

        def tap(name, sb):
            if dbg == name:
                em.wait("s", sV)
                em.wait("s", sS)
                em.dma("s", qA, dbg_t.rearrange("(g p) t -> p g t", p=128),
                       sb[:, :].rearrange("p (g t) -> p g t", g=NCK))

        per_mat = H * D
        tiles_per_mat = per_mat // (128 * PAW)
        pa_srcs = [gt.rearrange("a b c d -> (a b c d)"),
                   ut.rearrange("a b c d -> (a b c d)"),
                   dtw.rearrange("a b -> (a b)")]
        pa_total = 3 * tiles_per_mat
        pa_state = {"i": 0}
        pa_guard = Guard(2)
        PA = [None, None]

        def pa_pump(nmax):
            while pa_state["i"] < pa_total and nmax > 0:
                i = pa_state["i"]
                mi, ti = i // tiles_per_mat, i % tiles_per_mat
                buf = PA[i % 2]
                pa_guard.use(em, "y", i % 2)
                base = ti * 128 * PAW
                em.dma("y", qW, buf[:, 0:PAW],
                       pa_srcs[mi][base:base + 128 * PAW].rearrange("(p m) -> p m", p=128))
                em.wait("s", qW)
                em.opi("s", sS, lambda s, buf=buf, i=i: s.activation(
                    buf[:, 0:PAW], buf[:, 0:PAW], ACT.Abs, accum_out=STAT[:, i:i + 1]))
                pa_guard.done(i % 2, sS)
                pa_state["i"] += 1
                nmax -= 1

        stkPA = ExitStack()
        PA[0] = stkPA.enter_context(nc.sbuf_tensor("PA0", [128, PAW], F32, side="left"))
        PA[1] = stkPA.enter_context(nc.sbuf_tensor("PA1", [128, PAW], F32, side="left"))
        stkFx1 = ExitStack()
        Fx1 = stkFx1.enter_context(nc.sbuf_tensor("Fx1", [128, NCK * W], F32, side="left"))
        repA = stkFx1.enter_context(nc.sbuf_tensor("repA", [128, 6 * W], F32, side="left"))
        stkC1 = ExitStack()
        C1 = stkC1.enter_context(nc.sbuf_tensor("C1", [128, (NCK // 2) * W], F32,
                                                side="right"))

        em.phase()
        em.opi("v", sV, lambda v: v.memset(CC[:, :], 0.0))
        em.wait("v", sV)
        em.opi("v", sV, lambda v: v.memset(cc(C_ONE), 1.0))
        em.opi("v", sV, lambda v: v.memset(cc(C_EPS5), 1e-5))
        em.opi("v", sV, lambda v: v.memset(cc(C_EPS6), 1e-6))
        em.opi("v", sV, lambda v: v.memset(ROWS[0:1, 0:128], 1.0))
        em.opi("g", sG, lambda g: g.iota(IDN[:], pattern=[[-1, 128]], base=0,
                                         channel_multiplier=1,
                                         allow_small_or_imprecise_dtypes=True))
        em.opi("g", sG, lambda g: g.tensor_single_scalar(IDN[:], IDN[:], 0.0, ALU.is_equal))
        vecs = {}
        for i, n in enumerate(vnames):
            t = VEC[:, i * NCK:(i + 1) * NCK]
            vecs[n] = t
            em.dma("s", qA, t, vin[n].rearrange("g p -> p g"),
                   allow_slow_non_contiguous=True)
        em.dma("s", qA, cc(C_MASK), maskd[:, :])
        em.dma("s", qA, Fx1[:, :].rearrange("p (g t) -> p g t", g=NCK),
               xT.rearrange("(g p) t -> p g t", p=128))
        em.wait("v", qA)
        UP = VEC[:, 10 * NCK:11 * NCK]
        em.opi("v", sV, lambda v: v.tensor_single_scalar(UP, vecs["tmu"], LNEPS8, ALU.add))
        pa_pump(8)

        S1 = repA[:, 0 * W:1 * W]
        S2 = repA[:, 1 * W:2 * W]
        stats_sums(Fx1[:, :], C1, S1, S2)
        m0 = repA[:, 2 * W:3 * W]
        r0 = repA[:, 3 * W:4 * W]
        r1 = repA[:, 4 * W:5 * W]
        tp = repA[:, 5 * W:6 * W]
        em.opi("v", sV, lambda v: v.tensor_single_scalar(m0, S1, 1.0 / D, ALU.mult))
        em.opi("v", sV, lambda v: v.tensor_single_scalar(S2, S2, 1.0 / D, ALU.mult))
        em.opi("v", sV, lambda v: v.tensor_mul(tp, m0, m0))
        em.opi("v", sV, lambda v: v.tensor_sub(S2, S2, tp))
        em.wait("s", sV)
        em.opi("s", sS, lambda s: s.activation(r0, S2, ACT.Sqrt, bias=cc(C_EPS5)))
        em.wait("v", sS)
        em.opi("v", sV, lambda v: v.reciprocal(r0, r0))
        em.opi("v", sV, lambda v: v.tensor_mul(tp, r0, r0))
        em.opi("v", sV, lambda v: v.tensor_mul(tp, S2, tp))
        em.wait("s", sV)
        em.opi("s", sS, lambda s: s.activation(r1, tp, ACT.Sqrt, bias=cc(C_EPS5)))
        em.wait("v", sS)
        em.opi("v", sV, lambda v: v.reciprocal(r1, r1))
        pa_pump(4)
        em.flush()
        stkC1.close()

        stkB = ExitStack()
        Bh = stkB.enter_context(nc.sbuf_tensor("Bh", [128, NCK * W], F32, side="right"))
        em.phase()
        for c in range(NCK):
            sl = slice(c * W, (c + 1) * W)
            em.opi("v", sV, lambda v, sl=sl: v.tensor_sub(Fx1[:, sl], Fx1[:, sl], m0))
            em.opi("v", sV, lambda v, sl=sl: v.tensor_mul(Fx1[:, sl], Fx1[:, sl], r0))
        em.wait("s", sV)
        em.dma("s", qA, a_d.rearrange("(g p) t -> p g t", p=128),
               Fx1[:, :].rearrange("p (g t) -> p g t", g=NCK))
        a_out_n = qA.n
        for c in range(NCK):
            sl = slice(c * W, (c + 1) * W)
            em.opi("v", sV, lambda v, sl=sl: v.tensor_mul(Bh[:, sl], Fx1[:, sl], r1))
            em.opi("v", sV, lambda v, c=c, sl=sl: v.tensor_scalar(
                Bh[:, sl], Bh[:, sl], vecs["ln1g"][:, c:c + 1],
                vecs["ln1b"][:, c:c + 1], ALU.mult, ALU.add))
            em.wait("v", sV)
            em.opi("v", sV, lambda v, c=c: v.tensor_single_scalar(
                Bh[:, c * W:c * W + 2], Bh[:, c * W:c * W + 2], cc(C_MASK), ALU.mult))
        tap("a", Fx1)
        tap("h", Bh)
        pa_pump(8)
        em.flush()
        stkFx1.close()

        stkT = ExitStack()
        TRIO = stkT.enter_context(nc.sbuf_tensor("TRIO", [128, 3 * NCK * W], BF16,
                                                 side="left"))

        def trio_ap(i, c, s, e):
            return TRIO[:, (i * NCK + c) * W + s:(i * NCK + c) * W + e]

        em.phase()
        em.wait("v", qA)
        for c in range(NCK):
            em.opi("v", sV, lambda v, c=c: v.tensor_sub(
                JNK[:, 0:W - 1], Bh[:, c * W + 1:(c + 1) * W],
                Bh[:, c * W:(c + 1) * W - 1]))
            for i, mu in enumerate(["mur", "muk", "muv"]):
                em.opi("v", sV, lambda v, c=c, i=i, mu=mu: v.scalar_tensor_tensor(
                    trio_ap(i, c, 1, W), JNK[:, 0:W - 1], vecs[mu][:, c:c + 1],
                    Bh[:, c * W:(c + 1) * W - 1], ALU.mult, ALU.add))
                em.opi("v", sV, lambda v, c=c, i=i: v.tensor_copy(
                    trio_ap(i, c, 0, 1), Bh[:, c * W:c * W + 1]))
        pa_pump(4)
        em.flush()
        stkB.close()

        stk3w = ExitStack()
        WBF = [stk3w.enter_context(nc.sbuf_tensor(f"WBF{i}", [128, NCK * 128], BF16,
                                                  side="right")) for i in range(6)]
        ET = [stk3w.enter_context(nc.sbuf_tensor(f"ET{i}", [128, 512], F32, side="right"))
              for i in range(2)]
        stkR = ExitStack()
        RWKV = stkR.enter_context(nc.sbuf_tensor("RWKV", [128, NCK * W], BF16,
                                                 side="right"))
        wbf_guard = Guard(6)
        et_guard = Guard(1)

        def stream_wblk(src_ap, dc, mat):
            wi = mat * 2 + dc % 2
            wbf_guard.use(em, "g", wi)
            em.dma("g", qG, WBF[wi][:, :],
                   src_ap[dc, :, :, :].rearrange("p k j -> p (k j)"))
            return wi, qG.n

        em.phase()
        em.wait("y", qA)
        em.wait("v", qA)
        for dc in range(NCK):
            w_r, c_r = stream_wblk(wt["wr"], dc, 0)
            w_k, c_k = stream_wblk(wt["wk"], dc, 1)
            w_v, c_v = stream_wblk(wt["wv"], dc, 2)
            b0, b1, b2 = (dc % 2) * 3, (dc % 2) * 3 + 1, (dc % 2) * 3 + 2
            for (s, e) in NC_H1:
                wq = e - s
                acquire(b0)
                em.wait("t", qG, c_r)
                mm_chain(b0, [(WBF[w_r][:, k * 128:(k + 1) * 128], trio_ap(0, k, s, e))
                              for k in range(NCK)], wq)
                acquire(b1)
                em.wait("t", qG, c_k)
                mm_chain(b1, [(WBF[w_k][:, k * 128:(k + 1) * 128], trio_ap(1, k, s, e))
                              for k in range(NCK)], wq)
                acquire(b2)
                em.wait("t", qG, c_v)
                mm_chain(b2, [(WBF[w_v][:, k * 128:(k + 1) * 128], trio_ap(2, k, s, e))
                              for k in range(NCK)], wq)
                et_guard.use(em, "s", 0)
                em.wait("s", sT)
                em.opi("s", sS, lambda s_, b0=b0, wq=wq: s_.activation(
                    ET[0][:, 0:wq], psb[b0][:, 0:wq], ACT.Sigmoid))
                em.opi("s", sS, lambda s_, b1=b1, wq=wq, dc=dc: s_.activation(
                    ET[1][:, 0:wq], psb[b1][:, 0:wq], ACT.Sigmoid, bias=UP[:, dc:dc + 1]))
                release(b0, sS)
                release(b1, sS)
                em.wait("v", sS)
                em.opi("v", sV, lambda v, wq=wq: v.tensor_mul(
                    ET[0][:, 0:wq], ET[0][:, 0:wq], ET[1][:, 0:wq]))
                em.opi("v", sV, lambda v, b2=b2, dc=dc, s=s, e=e, wq=wq: v.tensor_mul(
                    RWKV[:, dc * W + s:dc * W + e], ET[0][:, 0:wq], psb[b2][:, 0:wq]))
                release(b2, sV)
                et_guard.done(0, sV)
            for wi in (w_r, w_k, w_v):
                wbf_guard.done(wi, sT)
            pa_pump(6)
        rwkv_n = sV.n
        em.flush()
        stkT.close()

        stkFx2 = ExitStack()
        Fx2 = stkFx2.enter_context(nc.sbuf_tensor("Fx2", [128, NCK * W], F32, side="left"))
        em.phase()
        em.wait("s", qA, a_out_n)
        em.dma("s", qA, Fx2[:, :].rearrange("p (g t) -> p g t", g=NCK),
               a_d.rearrange("(g p) t -> p g t", p=128))
        a_in_n = qA.n
        for dc in range(NCK):
            w_o, c_o = stream_wblk(wt["wo"], dc, 0)
            bi = dc % 4
            for (s, e) in NC_H1:
                wq = e - s
                acquire(bi)
                em.wait("t", qG, c_o)
                em.wait("t", sV, rwkv_n)
                mm_chain(bi, [(WBF[w_o][:, k * 128:(k + 1) * 128],
                               RWKV[:, k * W + s:k * W + e]) for k in range(NCK)], wq)
                em.wait("v", sT)
                em.wait("v", qA, a_in_n)
                em.opi("v", sV, lambda v, dc=dc, s=s, wq=wq, bi=bi: v.tensor_add(
                    Fx2[:, dc * W + s:dc * W + s + wq],
                    Fx2[:, dc * W + s:dc * W + s + wq], psb[bi][:, 0:wq]))
                release(bi, sV)
            wbf_guard.done(w_o, sT)
            pa_pump(6)
        tap("xmid", Fx2)
        em.flush()
        stkR.close()

        stkM2 = ExitStack()
        rep4 = stkM2.enter_context(nc.sbuf_tensor("rep4", [128, 2 * W], F32, side="right"))
        stk4s = ExitStack()
        C4 = stk4s.enter_context(nc.sbuf_tensor("C4", [128, (NCK // 2) * W], F32,
                                                side="right"))
        repS = stk4s.enter_context(nc.sbuf_tensor("repS", [128, 2 * W], F32, side="right"))
        em.phase()
        S1b = repS[:, 0 * W:1 * W]
        S2b = repS[:, 1 * W:2 * W]
        stats_sums(Fx2[:, :], C4, S1b, S2b)
        m2 = rep4[:, 0 * W:1 * W]
        r2l = rep4[:, 1 * W:2 * W]
        em.opi("v", sV, lambda v: v.tensor_single_scalar(m2, S1b, 1.0 / D, ALU.mult))
        em.opi("v", sV, lambda v: v.tensor_single_scalar(S2b, S2b, 1.0 / D, ALU.mult))
        em.opi("v", sV, lambda v: v.tensor_mul(S1b, m2, m2))
        em.opi("v", sV, lambda v: v.tensor_sub(S2b, S2b, S1b))
        em.wait("s", sV)
        em.opi("s", sS, lambda s: s.activation(r2l, S2b, ACT.Sqrt, bias=cc(C_EPS5)))
        em.wait("v", sS)
        em.opi("v", sV, lambda v: v.reciprocal(r2l, r2l))
        em.flush()
        stk4s.close()

        stk4 = ExitStack()
        H2Q = stk4.enter_context(nc.sbuf_tensor("H2Q", [128, NCK * 513], F32, side="right"))
        T2Q = stk4.enter_context(nc.sbuf_tensor("T2Q", [128, NCK * 512], BF16, side="right"))
        K2Q = stk4.enter_context(nc.sbuf_tensor("K2Q", [128, NCK * 512], BF16, side="right"))
        em.phase()
        STASH = VEC[:, 11 * NCK:12 * NCK]
        for ci, (s, e) in enumerate(NC_RE):
            if ci == 0:
                continue
            for c in range(NCK):
                em.opi("v", sV, lambda v, c=c, s=s: v.tensor_copy(
                    STASH[:, c:c + 1], Fx2[:, c * W + s - 1:c * W + s]))
        for ci, (s, e) in enumerate(NC_RE):
            wq = e - s
            em.wait("v", sT)
            em.wait("s", sT)
            if ci == 0:
                for c in range(NCK):
                    em.opi("v", sV, lambda v, c=c, s=s, e=e, wq=wq: v.tensor_sub(
                        H2Q[:, c * 513:c * 513 + wq + 1],
                        Fx2[:, c * W + s - 1:c * W + e], m2[:, s - 1:e]))
                    em.opi("v", sV, lambda v, c=c, s=s, e=e, wq=wq: v.tensor_mul(
                        H2Q[:, c * 513:c * 513 + wq + 1],
                        H2Q[:, c * 513:c * 513 + wq + 1], r2l[:, s - 1:e]))
                    em.opi("v", sV, lambda v, c=c, wq=wq: v.tensor_scalar(
                        H2Q[:, c * 513:c * 513 + wq + 1],
                        H2Q[:, c * 513:c * 513 + wq + 1],
                        vecs["ln2g"][:, c:c + 1], vecs["ln2b"][:, c:c + 1],
                        ALU.mult, ALU.add))
                    em.wait("v", sV)
                    em.opi("v", sV, lambda v, c=c: v.tensor_single_scalar(
                        H2Q[:, c * 513:c * 513 + 1], H2Q[:, c * 513:c * 513 + 1],
                        cc(C_MASK), ALU.mult))
            else:
                for c in range(NCK):
                    em.opi("v", sV, lambda v, c=c, s=s, e=e, wq=wq: v.tensor_sub(
                        H2Q[:, c * 513 + 1:c * 513 + wq + 1],
                        Fx2[:, c * W + s:c * W + e], m2[:, s:e]))
                    em.opi("v", sV, lambda v, c=c, s=s, e=e, wq=wq: v.tensor_mul(
                        H2Q[:, c * 513 + 1:c * 513 + wq + 1],
                        H2Q[:, c * 513 + 1:c * 513 + wq + 1], r2l[:, s:e]))
                    em.opi("v", sV, lambda v, c=c, wq=wq: v.tensor_scalar(
                        H2Q[:, c * 513 + 1:c * 513 + wq + 1],
                        H2Q[:, c * 513 + 1:c * 513 + wq + 1],
                        vecs["ln2g"][:, c:c + 1], vecs["ln2b"][:, c:c + 1],
                        ALU.mult, ALU.add))
                em.wait("v", sV)
                em.opi("v", sV, lambda v, s=s: v.tensor_single_scalar(
                    JNK[:, 0:NCK], STASH[:, 0:NCK], m2[:, s - 1:s], ALU.subtract))
                em.wait("v", sV)
                em.opi("v", sV, lambda v, s=s: v.tensor_single_scalar(
                    JNK[:, 0:NCK], JNK[:, 0:NCK], r2l[:, s - 1:s], ALU.mult))
                em.wait("v", sV)
                for c in range(NCK):
                    em.opi("v", sV, lambda v, c=c: v.tensor_scalar(
                        H2Q[:, c * 513:c * 513 + 1], JNK[:, c:c + 1],
                        vecs["ln2g"][:, c:c + 1], vecs["ln2b"][:, c:c + 1],
                        ALU.mult, ALU.add))
                em.wait("v", sV)
            for c in range(NCK):
                em.opi("v", sV, lambda v, c=c, wq=wq: v.tensor_sub(
                    JNK[:, 0:wq], H2Q[:, c * 513 + 1:c * 513 + 1 + wq],
                    H2Q[:, c * 513:c * 513 + wq]))
                em.opi("v", sV, lambda v, c=c, wq=wq: v.scalar_tensor_tensor(
                    T2Q[:, c * 512:c * 512 + wq], JNK[:, 0:wq],
                    vecs["cmuk"][:, c:c + 1], H2Q[:, c * 513:c * 513 + wq],
                    ALU.mult, ALU.add))
            xk2_n = sV.n
            for dc in range(NCK):
                w_w, c_w = stream_wblk(wt["cwk"], dc, 0)
                bi = dc % 4
                acquire(bi)
                em.wait("t", qG, c_w)
                em.wait("t", sV, xk2_n)
                mm_chain(bi, [(WBF[w_w][:, k * 128:(k + 1) * 128],
                               T2Q[:, k * 512:k * 512 + wq]) for k in range(NCK)], wq)
                et_guard.use(em, "v", 0)
                em.wait("v", sT)
                em.opi("v", sV, lambda v, bi=bi, wq=wq: v.tensor_single_scalar(
                    ET[0][:, 0:wq], psb[bi][:, 0:wq], 0.0, ALU.max))
                release(bi, sV)
                em.wait("s", sV)
                em.opi("s", sS, lambda s_, dc=dc, wq=wq: s_.activation(
                    K2Q[:, dc * 512:dc * 512 + wq], ET[0][:, 0:wq], ACT.Square))
                et_guard.done(0, sS)
                wbf_guard.done(w_w, sT)
            k2_n = sS.n
            em.wait("v", sT)
            for c in range(NCK):
                em.opi("v", sV, lambda v, c=c, wq=wq: v.tensor_sub(
                    JNK[:, 0:wq], H2Q[:, c * 513 + 1:c * 513 + 1 + wq],
                    H2Q[:, c * 513:c * 513 + wq]))
                em.opi("v", sV, lambda v, c=c, wq=wq: v.scalar_tensor_tensor(
                    T2Q[:, c * 512:c * 512 + wq], JNK[:, 0:wq],
                    vecs["cmur"][:, c:c + 1], H2Q[:, c * 513:c * 513 + wq],
                    ALU.mult, ALU.add))
            xr2_n = sV.n
            for dc in range(NCK):
                w_r2, c_r2 = stream_wblk(wt["cwr"], dc, 0)
                w_v2, c_v2 = stream_wblk(wt["cwv"], dc, 1)
                b0, b1 = (dc % 2) * 2, (dc % 2) * 2 + 1
                acquire(b0)
                em.wait("t", qG, c_r2)
                em.wait("t", sV, xr2_n)
                mm_chain(b0, [(WBF[w_r2][:, k * 128:(k + 1) * 128],
                               T2Q[:, k * 512:k * 512 + wq]) for k in range(NCK)], wq)
                acquire(b1)
                em.wait("t", qG, c_v2)
                em.wait("t", sS, k2_n)
                mm_chain(b1, [(WBF[w_v2][:, k * 128:(k + 1) * 128],
                               K2Q[:, k * 512:k * 512 + wq]) for k in range(NCK)], wq)
                et_guard.use(em, "s", 0)
                em.wait("s", sT)
                em.opi("s", sS, lambda s_, b0=b0, wq=wq: s_.activation(
                    ET[0][:, 0:wq], psb[b0][:, 0:wq], ACT.Sigmoid))
                release(b0, sS)
                em.wait("v", sS)
                em.opi("v", sV, lambda v, b1=b1, wq=wq: v.tensor_mul(
                    ET[0][:, 0:wq], ET[0][:, 0:wq], psb[b1][:, 0:wq]))
                em.opi("v", sV, lambda v, dc=dc, s=s, wq=wq: v.tensor_add(
                    Fx2[:, dc * W + s:dc * W + s + wq],
                    Fx2[:, dc * W + s:dc * W + s + wq], ET[0][:, 0:wq]))
                release(b1, sV)
                et_guard.done(0, sV)
                wbf_guard.done(w_r2, sT)
                wbf_guard.done(w_v2, sT)
            pa_pump(4)
        for c in range(NCK):
            em.wait("s", sV)
            em.dma("s", qA, JNK[:, 0:W], xT[c * 128:(c + 1) * 128, :])
            em.wait("v", qA)
            em.opi("v", sV, lambda v, c=c: v.tensor_add(
                Fx2[:, c * W:(c + 1) * W], Fx2[:, c * W:(c + 1) * W], JNK[:, 0:W]))
        tap("x1", Fx2)
        em.flush()
        stk4.close()
        stkM2.close()
        stk3w.close()

        stkRX = ExitStack()
        REPX = stkRX.enter_context(nc.sbuf_tensor("REPX", [128, 3 * W], F32, side="right"))
        stkXQ = ExitStack()
        XQ = stkXQ.enter_context(nc.sbuf_tensor("XQ", [128, NCK * W], BF16, side="right"))
        stk5 = ExitStack()
        C5W = max((NCK // 2) * W + 2 * W, 2 * W + 3 * NCK * 128)
        C5 = stk5.enter_context(nc.sbuf_tensor("C5", [128, C5W], F32, side="right"))
        rep5 = stk5.enter_context(nc.sbuf_tensor("rep5", [128, 3 * W], F32, side="right"))
        RI = stk5.enter_context(nc.sbuf_tensor("RI", [128, NCK * 128], I32, side="right"))

        em.phase()
        pa_pump(pa_total)
        em.wait("v", sS)
        for mi, key in enumerate(["g", "u", "d"]):
            s0, e0 = mi * tiles_per_mat, (mi + 1) * tiles_per_mat
            em.opi("v", sV, lambda v, s0=s0, e0=e0: v.reduce_sum(
                STAT[:, 254:255], STAT[:, s0:e0], AXX))
            acquire(6)
            em.wait("t", sV)
            em.opi("t", sT, lambda t: t.matmul(
                psb[6][0:1, 0:1], cc(C_ONE), STAT[:, 254:255], start=True, stop=True))
            em.wait("v", sT)
            em.opi("v", sV, lambda v: v.tensor_copy(ROWS[0:1, 600:601], psb[6][0:1, 0:1]))
            release(6, sV)
            acquire(7)
            em.wait("t", sV)
            em.opi("t", sT, lambda t: t.matmul(
                psb[7][:, 0:1], ROWS[0:1, 0:128], ROWS[0:1, 600:601],
                start=True, stop=True))
            em.wait("v", sT)
            em.opi("v", sV, lambda v, key=key: v.tensor_copy(cc(C_ISW[key]), psb[7][:, 0:1]))
            release(7, sV)
            em.wait("v", sV)
            em.opi("v", sV, lambda v, key=key: v.tensor_single_scalar(
                cc(C_ISW[key]), cc(C_ISW[key]), 1.0 / (H * D), ALU.mult))
            em.wait("v", sV)
            em.opi("v", sV, lambda v, key=key: v.tensor_single_scalar(
                cc(C_ISW[key]), cc(C_ISW[key]), 1e-5, ALU.max))
            em.wait("v", sV)
            em.opi("v", sV, lambda v, key=key: v.tensor_single_scalar(
                cc(C_TAU[key]), cc(C_ISW[key]), 0.5, ALU.mult))
            em.wait("v", sV)
            em.opi("v", sV, lambda v, key=key: v.tensor_single_scalar(
                cc(C_NTAU[key]), cc(C_TAU[key]), -1.0, ALU.mult))
            em.wait("v", sV)
            em.opi("v", sV, lambda v, key=key: v.tensor_single_scalar(
                cc(C_ISW[key]), cc(C_ISW[key]), 0.5, ALU.mult))
            em.wait("v", sV)

        S1c = rep5[:, 1 * W:2 * W]
        S2c = rep5[:, 2 * W:3 * W]
        stats_sums(Fx2[:, :], C5, S1c, S2c)
        m3 = rep5[:, 0 * W:1 * W]
        rrho = REPX[:, 0 * W:1 * W]
        FREP = REPX[:, 1 * W:2 * W]
        IXR = REPX[:, 2 * W:3 * W]
        tpc = S1c
        em.opi("v", sV, lambda v: v.tensor_single_scalar(m3, S1c, 1.0 / D, ALU.mult))
        em.opi("v", sV, lambda v: v.tensor_single_scalar(S2c, S2c, 1.0 / D, ALU.mult))
        em.opi("v", sV, lambda v: v.tensor_mul(tpc, m3, m3))
        em.opi("v", sV, lambda v: v.tensor_sub(S2c, S2c, tpc))
        em.wait("s", sV)
        em.opi("s", sS, lambda s: s.activation(rrho, S2c, ACT.Sqrt, bias=cc(C_EPS5)))
        em.wait("v", sS)
        em.opi("v", sV, lambda v: v.reciprocal(rrho, rrho))
        em.opi("v", sV, lambda v: v.tensor_mul(tpc, rrho, rrho))
        em.opi("v", sV, lambda v: v.tensor_mul(tpc, S2c, tpc))
        em.wait("s", sV)
        em.opi("s", sS, lambda s: s.activation(tpc, tpc, ACT.Sqrt, bias=cc(C_EPS6)))
        em.wait("v", sS)
        em.opi("v", sV, lambda v: v.reciprocal(tpc, tpc))
        em.opi("v", sV, lambda v: v.tensor_mul(rrho, rrho, tpc))

        MXP = C5[:, 0:W]
        MNP = C5[:, W:2 * W]
        TQ = C5[:, 2 * W:2 * W + NCK * 128]
        RF = C5[:, 2 * W + NCK * 128:2 * W + 2 * NCK * 128]
        FIX = C5[:, 2 * W + 2 * NCK * 128:2 * W + 3 * NCK * 128]
        assert 2 * W + 3 * NCK * 128 <= C5W
        TREE = C5[:, 2 * W:2 * W + (NCK // 2) * W]
        half = NCK // 2
        for op, dstP in ((ALU.max, MXP), (ALU.min, MNP)):
            em.opi("v", sV, lambda v, op=op: v.tensor_tensor(
                TREE[:, 0:half * W], Fx2[:, 0:half * W], Fx2[:, half * W:NCK * W], op))
            n = half
            while n > 1:
                h2 = n // 2
                em.opi("v", sV, lambda v, h2=h2, n=n, op=op: v.tensor_tensor(
                    TREE[:, 0:h2 * W], TREE[:, 0:h2 * W], TREE[:, h2 * W:n * W], op))
                n = h2
            em.opi("v", sV, lambda v, dstP=dstP: v.tensor_copy(dstP[:, 0:W], TREE[:, 0:W]))
        for j in range(NBLK):
            acquire(4)
            em.wait("t", sV)
            em.opi("t", sT, lambda t, j=j: t.transpose(
                psb[4][:, 0:128], MXP[:, R0 + j * 128:R0 + (j + 1) * 128], IDN[:]))
            em.wait("v", sT)
            em.opi("v", sV, lambda v, j=j: v.reduce_max(
                MAXC[:, j:j + 1], psb[4][:, 0:128], AXX))
            release(4, sV)
            acquire(5)
            em.wait("t", sV)
            em.opi("t", sT, lambda t, j=j: t.transpose(
                psb[5][:, 0:128], MNP[:, R0 + j * 128:R0 + (j + 1) * 128], IDN[:]))
            em.wait("v", sT)
            em.opi("v", sV, lambda v, j=j: v.tensor_reduce(
                MAXC[:, NBLK + j:NBLK + j + 1], psb[5][:, 0:128], AXX, ALU.min))
            release(5, sV)
        cols_to_rep(NBLK, tpc, R0, 0)
        cols_to_rep(NBLK, FREP, R0, NBLK)
        em.opi("v", sV, lambda v: v.tensor_sub(tpc, tpc, m3))
        em.opi("v", sV, lambda v: v.tensor_sub(FREP, m3, FREP))
        em.opi("v", sV, lambda v: v.tensor_max(tpc, tpc, FREP))
        em.opi("v", sV, lambda v: v.tensor_mul(tpc, tpc, rrho))
        em.opi("v", sV, lambda v: v.tensor_single_scalar(tpc, tpc, 1e-5, ALU.max))
        em.opi("v", sV, lambda v: v.tensor_single_scalar(IXR, tpc, 1.0 / 127.0, ALU.mult))
        em.opi("v", sV, lambda v: v.reciprocal(tpc, tpc))
        em.opi("v", sV, lambda v: v.tensor_mul(tpc, tpc, rrho))
        em.opi("v", sV, lambda v: v.tensor_single_scalar(FREP, tpc, 127.0, ALU.mult))

        em.wait("s", sV)
        em.dma("s", qA, x1_d.rearrange("(g p) t -> p g t", p=128),
               Fx2[:, :].rearrange("p (g t) -> p g t", g=NCK))
        x1_out_n = qA.n

        for (s, e) in NC_8:
            wq = e - s
            for c in range(NCK):
                em.opi("v", sV, lambda v, c=c, s=s, wq=wq: v.tensor_sub(
                    TQ[:, c * 128:c * 128 + wq], Fx2[:, c * W + s:c * W + s + wq],
                    m3[:, s:s + wq]))
                em.opi("v", sV, lambda v, c=c, s=s, wq=wq: v.tensor_mul(
                    TQ[:, c * 128:c * 128 + wq], TQ[:, c * 128:c * 128 + wq],
                    FREP[:, s:s + wq]))
            em.wait("s", sV)
            em.opi("s", sS, lambda s_: s_.copy(RI[:, :], TQ[:, 0:NCK * 128]))
            em.opi("s", sS, lambda s_: s_.copy(RF[:, 0:NCK * 128], RI[:, :]))
            em.wait("v", sS)
            em.opi("v", sV, lambda v: v.tensor_sub(
                TQ[:, 0:NCK * 128], TQ[:, 0:NCK * 128], RF[:, 0:NCK * 128]))
            em.opi("v", sV, lambda v: v.scalar_tensor_tensor(
                RF[:, 0:NCK * 128], TQ[:, 0:NCK * 128], 0.5, RF[:, 0:NCK * 128],
                ALU.is_gt, ALU.add))
            em.opi("v", sV, lambda v: v.tensor_scalar(
                TQ[:, 0:NCK * 128], TQ[:, 0:NCK * 128], -0.5, -1.0, ALU.is_ge, ALU.add))
            for c in range(NCK):
                em.opi("v", sV, lambda v, c=c, s=s, wq=wq: v.tensor_add(
                    XQ[:, c * W + s:c * W + s + wq], RF[:, c * 128:c * 128 + wq],
                    TQ[:, c * 128:c * 128 + wq]))
        if dbg == "xq":
            em.wait("s", sV)
            em.dma("s", qA, dbgq_t.rearrange("(g p) t -> p g t", p=128),
                   XQ[:, :].rearrange("p (g t) -> p g t", g=NCK))
        if dbg == "cc":
            em.wait("s", sV)
            em.dma("s", qA, dbgc_t[:, :], CC[:, :])
        em.flush()
        stk5.close()
        stkFx2.close()
        stkPA.close()

        stk6 = ExitStack()
        GW = stk6.enter_context(nc.sbuf_tensor("GW", [128, 4 * NCK * 128], F32,
                                               side="left"))
        QBF = [stk6.enter_context(nc.sbuf_tensor(f"QBF{i}", [128, NCK * 128], BF16,
                                                 side="left")) for i in range(4)]
        QTT = [stk6.enter_context(nc.sbuf_tensor(f"QTT{i}", [128, NCK * 128], F32,
                                                 side="left")) for i in range(4)]
        DS6 = [stk6.enter_context(nc.sbuf_tensor("DS6_0", [128, D], F32, side="left"))] * 2
        QD6 = [stk6.enter_context(nc.sbuf_tensor("QD6_0", [128, D], F32, side="left"))] * 2
        DWQ = [stk6.enter_context(nc.sbuf_tensor(f"DWQ{i}", [128, D], BF16, side="left"))
               for i in range(2)]
        GSB = stk6.enter_context(nc.sbuf_tensor("GSB", [128, 2 * NRT], F32, side="right"))
        GUB = [stk6.enter_context(nc.sbuf_tensor(f"GUB{i}", [128, NRT], F32, side="right"))
               for i in range(2)]
        AC3 = stk6.enter_context(nc.sbuf_tensor("AC3", [128, 3 * NRT], F32, side="right"))
        gw_guard = Guard(6)
        qbf_guard = Guard(6)
        qtt_guard = Guard(3)
        gub_guard = Guard(2)
        ds6_guard = Guard(1)
        qd6_guard = Guard(1)
        dwq_guard = Guard(2)

        gtf = gt.rearrange("a b c d -> (a b c d)")
        utf = ut.rearrange("a b c d -> (a b c d)")

        def quantize_gu_blk(src_flat, hb, mat, key):
            half = mat * 2 + hb % 2
            qt = (hb % 2) * 2
            wi = mat * 2 + hb % 2
            stg = GW[:, half * NCK * 128:(half + 1) * NCK * 128]
            gw_guard.use(em, "y", half)
            base = hb * 128 * NCK * 128
            em.dma("y", qW, stg, src_flat[base:base + 128 * NCK * 128]
                   .rearrange("(p m) -> p m", p=128))
            em.wait("s", qW)
            qtt_guard.use(em, "s", hb % 2)
            em.opi("s", sS, lambda s_, stg=stg, key=key, qt=qt: s_.sign(
                QTT[qt][:, :], stg, bias=cc(C_NTAU[key])))
            em.opi("s", sS, lambda s_, stg=stg, key=key, qt=qt: s_.sign(
                QTT[qt + 1][:, :], stg, bias=cc(C_TAU[key])))
            gw_guard.done(half, sS)
            em.wait("g", sS)
            qbf_guard.use(em, "g", wi)
            em.opi("g", sG, lambda g, wi=wi, qt=qt: g.tensor_add(
                QBF[wi][:, :], QTT[qt][:, :], QTT[qt + 1][:, :]))
            qtt_guard.done(hb % 2, sG)
            return wi, sG.n

        em.phase()
        em.wait("y", qA, x1_out_n)
        em.wait("v", qA, x1_out_n)
        em.wait("g", qA, x1_out_n)
        em.wait("s", qA, x1_out_n)
        dtf6 = dtw.rearrange("a b -> (a b)")

        def quantize_down_blk(hb):
            p = hb % 2
            ds6_guard.use(em, "y", 0)
            base = hb * 128 * D
            em.dma("y", qW, DS6[p][:, :], dtf6[base:base + 128 * D]
                   .rearrange("(p m) -> p m", p=128))
            em.wait("s", qW)
            qd6_guard.use(em, "s", 0)
            em.opi("s", sS, lambda s_, p=p: s_.sign(
                QD6[p][:, :], DS6[p][:, :], bias=cc(C_NTAU["d"])))
            em.opi("s", sS, lambda s_, p=p: s_.sign(
                DS6[p][:, :], DS6[p][:, :], bias=cc(C_TAU["d"])))
            ds6_guard.done(0, sS)
            em.wait("g", sS)
            dwq_guard.use(em, "g", p)
            em.opi("g", sG, lambda g, p=p: g.tensor_add(
                DWQ[p][:, :], QD6[p][:, :], DS6[p][:, :]))
            qd6_guard.done(0, sG)
            em.wait("y", sG)
            em.dma("y", qW, wqd_d[hb, :, :], DWQ[p][:, :])
            dwq_guard.done(p, qW)

        qinfo = {}
        for j in range(min(1, HBN)):
            qinfo[j] = (quantize_gu_blk(gtf, j, 0, "g"), quantize_gu_blk(utf, j, 1, "u"))
        for hb in range(HBN):
            (wg_i, qg), (wu_i, qu) = qinfo.pop(hb)
            if hb + 1 < HBN:
                qinfo[hb + 1] = (quantize_gu_blk(gtf, hb + 1, 0, "g"),
                                 quantize_gu_blk(utf, hb + 1, 1, "u"))
            quantize_down_blk(hb)
            gub_guard.use(em, "v", hb % 2)
            banks = []
            for ci, (s, e) in enumerate(NC_RE):
                wq = e - s
                b0 = (hb % 2) * 4 + ci * 2
                b1 = b0 + 1
                acquire(b0)
                em.wait("t", sG, qg)
                mm_chain(b0, [(QBF[wg_i][:, k * 128:(k + 1) * 128],
                               XQ[:, k * W + s:k * W + e]) for k in range(NCK)], wq)
                acquire(b1)
                em.wait("t", sG, qu)
                mm_chain(b1, [(QBF[wu_i][:, k * 128:(k + 1) * 128],
                               XQ[:, k * W + s:k * W + e]) for k in range(NCK)], wq)
                banks.append((b0, b1, s, e, wq))
            em.wait("v", sT)
            for (b0, b1, s, e, wq) in banks:
                so = s - R0
                em.opi("v", sV, lambda v, b0=b0, so=so, wq=wq, s=s: v.scalar_tensor_tensor(
                    GSB[:, so:so + wq], psb[b0][:, 0:wq], cc(C_ISW["g"]),
                    IXR[:, s:s + wq], ALU.mult, ALU.mult))
                em.opi("v", sV, lambda v, b1=b1, so=so, wq=wq, s=s: v.scalar_tensor_tensor(
                    GSB[:, NRT + so:NRT + so + wq], psb[b1][:, 0:wq], cc(C_ISW["u"]),
                    IXR[:, s:s + wq], ALU.mult, ALU.mult))
                release(b0, sV)
                release(b1, sV)
            em.wait("s", sV)
            em.opi("s", sS, lambda s_: s_.activation(
                GSB[:, 0:NRT], GSB[:, 0:NRT], ACT.Silu))
            em.wait("v", sS)
            em.opi("v", sV, lambda v, hb=hb: v.tensor_mul(
                GUB[hb % 2][:, 0:NRT], GSB[:, 0:NRT], GSB[:, NRT:2 * NRT]))
            qbf_guard.done(wg_i, sT)
            qbf_guard.done(wu_i, sT)
            em.wait("s", sV)
            em.dma("s", qA, gu_d[hb, :, :], GUB[hb % 2][:, :])
            em.opi("s", sS, lambda s_, hb=hb: s_.activation(
                JNK[:, 0:NRT], GUB[hb % 2][:, :], ACT.Square))
            em.wait("v", sS)
            if hb == 0:
                em.opi("v", sV, lambda v: v.tensor_copy(AC3[:, 0:NRT], JNK[:, 0:NRT]))
                em.opi("v", sV, lambda v, hb=hb: v.tensor_copy(
                    AC3[:, NRT:2 * NRT], GUB[hb % 2][:, :]))
                em.opi("v", sV, lambda v, hb=hb: v.tensor_copy(
                    AC3[:, 2 * NRT:3 * NRT], GUB[hb % 2][:, :]))
            else:
                em.opi("v", sV, lambda v: v.tensor_add(
                    AC3[:, 0:NRT], AC3[:, 0:NRT], JNK[:, 0:NRT]))
                em.opi("v", sV, lambda v, hb=hb: v.tensor_max(
                    AC3[:, NRT:2 * NRT], AC3[:, NRT:2 * NRT], GUB[hb % 2][:, :]))
                em.opi("v", sV, lambda v, hb=hb: v.tensor_tensor(
                    AC3[:, 2 * NRT:3 * NRT], AC3[:, 2 * NRT:3 * NRT],
                    GUB[hb % 2][:, :], ALU.min))
            gub_guard.done(hb % 2, sV)

        SQ2 = rep5 if False else None
        rep_from_psum_row(AC3[:, 0:NRT], NRT, rrho[:, 0:1] if False else rrho)
        for j in range(NBLK):
            acquire(4)
            em.wait("t", sV)
            em.opi("t", sT, lambda t, j=j: t.transpose(
                psb[4][:, 0:128], AC3[:, NRT + j * 128:NRT + (j + 1) * 128], IDN[:]))
            em.wait("v", sT)
            em.opi("v", sV, lambda v, j=j: v.reduce_max(
                MAXC[:, j:j + 1], psb[4][:, 0:128], AXX))
            release(4, sV)
            acquire(5)
            em.wait("t", sV)
            em.opi("t", sT, lambda t, j=j: t.transpose(
                psb[5][:, 0:128], AC3[:, 2 * NRT + j * 128:2 * NRT + (j + 1) * 128],
                IDN[:]))
            em.wait("v", sT)
            em.opi("v", sV, lambda v, j=j: v.tensor_reduce(
                MAXC[:, NBLK + j:NBLK + j + 1], psb[5][:, 0:128], AXX, ALU.min))
            release(5, sV)
        em.wait("v", sV)
        em.opi("v", sV, lambda v: v.tensor_single_scalar(
            MAXC[:, NBLK:2 * NBLK], MAXC[:, NBLK:2 * NBLK], -1.0, ALU.mult))
        em.wait("v", sV)
        em.opi("v", sV, lambda v: v.tensor_max(
            MAXC[:, 0:NBLK], MAXC[:, 0:NBLK], MAXC[:, NBLK:2 * NBLK]))
        em.wait("v", sV)
        cols_to_rep(NBLK, FREP, 0)
        em.opi("v", sV, lambda v: v.tensor_single_scalar(
            rrho[:, 0:NRT], rrho[:, 0:NRT], 1.0 / H, ALU.mult))
        em.wait("s", sV)
        em.opi("s", sS, lambda s_: s_.activation(
            rrho[:, 0:NRT], rrho[:, 0:NRT], ACT.Sqrt, bias=cc(C_EPS6)))
        em.wait("v", sS)
        em.opi("v", sV, lambda v: v.reciprocal(rrho[:, 0:NRT], rrho[:, 0:NRT]))
        em.opi("v", sV, lambda v: v.tensor_mul(
            FREP[:, 0:NRT], FREP[:, 0:NRT], rrho[:, 0:NRT]))
        em.opi("v", sV, lambda v: v.tensor_single_scalar(
            FREP[:, 0:NRT], FREP[:, 0:NRT], 1e-5, ALU.max))
        em.opi("v", sV, lambda v: v.tensor_single_scalar(
            IXR[:, 0:NRT], FREP[:, 0:NRT], 1.0 / 127.0, ALU.mult))
        em.opi("v", sV, lambda v: v.reciprocal(FREP[:, 0:NRT], FREP[:, 0:NRT]))
        em.opi("v", sV, lambda v: v.tensor_mul(
            FREP[:, 0:NRT], FREP[:, 0:NRT], rrho[:, 0:NRT]))
        em.opi("v", sV, lambda v: v.tensor_single_scalar(
            FREP[:, 0:NRT], FREP[:, 0:NRT], 127.0, ALU.mult))
        if dbg == "gu":
            em.wait("s", sV)
            em.wait("s", qA)
            em.dma("s", qA, dbgg_t[:, :, :], gu_d[:, :, :])
        if dbg == "repx":
            em.wait("s", sV)
            em.dma("s", qA, dbgr_t[:, :], REPX[:, :])
        em.flush()
        stk6.close()
        stkXQ.close()

        stkFx3 = ExitStack()
        Fx3 = stkFx3.enter_context(nc.sbuf_tensor("Fx3", [128, NCK * W], F32, side="left"))
        stk7 = ExitStack()
        XQ2 = [stk7.enter_context(nc.sbuf_tensor(f"XQ2_{i}", [128, HBS * NRT], BF16,
                                                 side="right")) for i in range(4)]
        DW = [stk7.enter_context(nc.sbuf_tensor(f"DW{i}", [128, HBS * D], BF16,
                                                side="right")) for i in range(3)]
        GIN = [stk7.enter_context(nc.sbuf_tensor(f"GIN{i}", [128, NRT], F32, side="right"))
               for i in range(2)]
        RI2 = [stk7.enter_context(nc.sbuf_tensor(f"RI2_{i}", [128, NRT], I32, side="right"))
               for i in range(2)]
        RF2 = [stk7.enter_context(nc.sbuf_tensor(f"RF2_{i}", [128, NRT], F32, side="right"))
               for i in range(2)]
        ET7 = stk7.enter_context(nc.sbuf_tensor("ET7", [128, 512], F32, side="right"))

        dw_guard = Guard(3)
        xq2_guard = Guard(4)
        gin_guard = Guard(2)
        rif_guard = Guard(2)
        et7_guard = Guard(1)

        n_groups = HBN // HBG
        subs = HBG // HBS

        em.phase()
        em.wait("y", qA)
        em.wait("v", qA)
        em.wait("g", qA)
        em.wait("s", qA)
        em.dma("s", qA, Fx3[:, :].rearrange("p (g t) -> p g t", g=NCK),
               x1_d.rearrange("(g p) t -> p g t", p=128))
        x1_in_n = qA.n
        def group_rounds(gi):
            sub_ready = []
            for si in range(subs):
                idx = gi * subs + si
                p4, p3 = idx % 4, idx % 3
                dw_guard.use(em, "y", p3)
                xq2_guard.use(em, "g", p4)
                for k in range(HBS):
                    hb = gi * HBG + si * HBS + k
                    em.dma("y", qW, DW[p3][:, k * D:(k + 1) * D], wqd_d[hb, :, :])
                    kb = k % 2
                    gin_guard.use(em, "g", kb)
                    em.dma("s", qA, GIN[kb][:, :], gu_d[hb, :, :])
                    em.wait("g", qA)
                    em.opi("g", sG, lambda g, kb=kb: g.tensor_mul(
                        GIN[kb][:, :], GIN[kb][:, :], FREP[:, 0:NRT]))
                    rif_guard.use(em, "s", kb)
                    em.wait("s", sG)
                    em.opi("s", sS, lambda s_, kb=kb: s_.copy(RI2[kb][:, :], GIN[kb][:, :]))
                    em.opi("s", sS, lambda s_, kb=kb: s_.copy(RF2[kb][:, :], RI2[kb][:, :]))
                    em.wait("v", sS)
                    em.opi("v", sV, lambda v, kb=kb: v.tensor_sub(
                        GIN[kb][:, :], GIN[kb][:, :], RF2[kb][:, :]))
                    em.opi("v", sV, lambda v, kb=kb: v.scalar_tensor_tensor(
                        RF2[kb][:, :], GIN[kb][:, :], 0.5, RF2[kb][:, :],
                        ALU.is_gt, ALU.add))
                    em.opi("v", sV, lambda v, kb=kb: v.tensor_scalar(
                        GIN[kb][:, :], GIN[kb][:, :], -0.5, -1.0,
                        ALU.is_ge, ALU.add))
                    em.wait("g", sV)
                    em.opi("g", sG, lambda g, p4=p4, k=k, kb=kb: g.tensor_add(
                        XQ2[p4][:, k * NRT:(k + 1) * NRT], RF2[kb][:, :], GIN[kb][:, :]))
                    rif_guard.done(kb, sG)
                    gin_guard.done(kb, sG)
                sub_ready.append((p4, p3, sG.n, qW.n))
                if gi == 0 and si == 0 and dbg == "xq2":
                    em.wait("s", sV)
                    em.dma("s", qA, dbgx2_t[:, :], XQ2[0][:, :])
                if gi == 0 and si == 0 and dbg == "dw":
                    em.wait("s", sG)
                    em.dma("s", qA, dbgdw_t[:, :], DW[0][:, :])
            return sub_ready

        pend7 = group_rounds(0)
        for gi in range(n_groups):
            sub_ready = pend7
            if gi + 1 < n_groups:
                fut = sT.n + len(NC_RE) * NCK
                for p4, p3, _, _ in sub_ready:
                    dw_guard.rec[p3] = (sT, fut)
                pend7 = group_rounds(gi + 1)
            for ci, (s, e) in enumerate(NC_RE):
                wq = e - s
                for dc in range(NCK):
                    bi = dc % 4
                    acquire(bi)
                    for si in range(subs):
                        p4, p3, gn, wn = sub_ready[si]
                        em.wait("t", sG, gn)
                        em.wait("t", qW, wn)
                        so = s - R0
                        for k in range(HBS):
                            st = (si == 0 and k == 0)
                            sp = (si == subs - 1 and k == HBS - 1)
                            l = DW[p3][:, k * D + dc * 128:k * D + (dc + 1) * 128]
                            r = XQ2[p4][:, k * NRT + so:k * NRT + so + wq]
                            if sp:
                                em.opi("t", sT, lambda t, l=l, r=r, st=st, bi=bi, wq=wq:
                                       t.matmul(psb[bi][:, 0:wq], l, r, start=st,
                                                stop=True))
                            else:
                                em.op("t", lambda t, l=l, r=r, st=st, bi=bi, wq=wq:
                                      t.matmul(psb[bi][:, 0:wq], l, r, start=st,
                                               stop=False))
                    et7_guard.use(em, "v", 0)
                    em.wait("v", sT)
                    em.wait("v", qA, x1_in_n)
                    em.opi("v", sV, lambda v, bi=bi, wq=wq, s=s: v.scalar_tensor_tensor(
                        ET7[:, 0:wq], psb[bi][:, 0:wq], cc(C_ISW["d"]),
                        IXR[:, s - R0:s - R0 + wq], ALU.mult, ALU.mult))
                    em.opi("v", sV, lambda v, dc=dc, s=s, wq=wq: v.tensor_add(
                        Fx3[:, dc * W + s:dc * W + s + wq],
                        Fx3[:, dc * W + s:dc * W + s + wq], ET7[:, 0:wq]))
                    if dbg == "et7" and gi == 0 and ci == 0 and dc == 0:
                        em.wait("s", sV)
                        em.dma("s", qA, dbge_t[:, 0:wq], ET7[:, 0:wq])
                    if dbg == "draw" and gi == 0 and ci == 0 and dc == 0:
                        em.opi("v", sV, lambda v, bi=bi, wq=wq: v.tensor_copy(
                            JNK[:, 0:wq], psb[bi][:, 0:wq]))
                        em.wait("s", sV)
                        em.dma("s", qA, dbge_t[:, 0:wq], JNK[:, 0:wq])
                    release(bi, sV)
                    et7_guard.done(0, sV)
            for p4, p3, _, _ in sub_ready:
                dw_guard.done(p3, sT)
                xq2_guard.done(p4, sT)
        em.flush()
        stk7.close()
        stkRX.close()

        stk8 = ExitStack()
        OUTT = [stk8.enter_context(nc.sbuf_tensor(f"OUTT{i}", [128, D], F32, side="right"))
                for i in range(2)]
        outt_guard = Guard(2)
        em.phase()
        em.wait("t", sV)
        for j in range(NBLK):
            ob = OUTT[j % 2]
            outt_guard.use(em, "v", j % 2)
            for g in range(NCK):
                bi = 4 + (g % 2)
                acquire(bi)
                em.opi("t", sT, lambda t, g=g, j=j, bi=bi: t.transpose(
                    psb[bi][:, 0:128],
                    Fx3[:, g * W + R0 + j * 128:g * W + R0 + (j + 1) * 128], IDN[:]))
                em.wait("v", sT)
                em.opi("v", sV, lambda v, g=g, bi=bi, ob=ob: v.tensor_copy(
                    ob[:, g * 128:(g + 1) * 128], psb[bi][:, 0:128]))
                release(bi, sV)
            em.wait("s", sV)
            em.dma("s", sO, y[j * 128:(j + 1) * 128, :], ob[:, :])
            outt_guard.done(j % 2, sO)
        tap("x2", Fx3)
        em.wait("y", sO)
        em.flush()
        stk8.close()
        stkFx3.close()

    return nc


_CACHE = {}


def _blk(Wt):
    Din, Dout = Wt.shape
    K, C = Din // 128, Dout // 128
    return np.ascontiguousarray(Wt.reshape(K, 128, C, 128).transpose(2, 1, 0, 3))


def kernel(L0_attn_x, L0_attn_ln1_g, L0_attn_ln1_b, L0_attn_ln2_g, L0_attn_ln2_b,
           L0_attn_tm_w, L0_attn_tm_u, L0_attn_tm_mu_r, L0_attn_tm_mu_k,
           L0_attn_tm_mu_v, L0_attn_tm_Wr, L0_attn_tm_Wk, L0_attn_tm_Wv,
           L0_attn_tm_Wout, L0_attn_cm_mu_r, L0_attn_cm_mu_k, L0_attn_cm_Wr,
           L0_attn_cm_Wk, L0_attn_cm_Wv, L0_ffn_w_gate, L0_ffn_w_up, L0_ffn_w_down):
    x = np.ascontiguousarray(np.asarray(L0_attn_x, dtype=np.float32))
    Bsz, T, D = x.shape
    H = np.asarray(L0_ffn_w_gate).shape[0]
    n_cores = 8
    NRT = (Bsz * T) // n_cores

    import os
    dbg = os.environ.get("KDBG") or None
    key = (D, H, NRT, dbg)
    if key not in _CACHE:
        _CACHE[key] = build(D, H, NRT, dbg)
    nc = _CACHE[key]

    def P(a):
        return np.ascontiguousarray(np.asarray(a, dtype=np.float32))

    wts = {
        "wr": _blk(P(L0_attn_tm_Wr).T), "wk": _blk(P(L0_attn_tm_Wk).T),
        "wv": _blk(P(L0_attn_tm_Wv).T), "wo": _blk(P(L0_attn_tm_Wout).T),
        "cwr": _blk(P(L0_attn_cm_Wr).T), "cwk": _blk(P(L0_attn_cm_Wk).T),
        "cwv": _blk(P(L0_attn_cm_Wv).T),
        "gt": _blk(P(L0_ffn_w_gate).T), "ut": _blk(P(L0_ffn_w_up).T),
        "dtw": np.ascontiguousarray(P(L0_ffn_w_down).T),
    }
    NCK = D // 128
    vecs = {
        "ln1g": P(L0_attn_ln1_g), "ln1b": P(L0_attn_ln1_b),
        "ln2g": P(L0_attn_ln2_g), "ln2b": P(L0_attn_ln2_b),
        "tmu": P(L0_attn_tm_u), "mur": P(L0_attn_tm_mu_r),
        "muk": P(L0_attn_tm_mu_k), "muv": P(L0_attn_tm_mu_v),
        "cmur": P(L0_attn_cm_mu_r), "cmuk": P(L0_attn_cm_mu_k),
    }

    halves = T // NRT
    in_maps = []
    for c in range(n_cores):
        b, half = c // halves, c % halves
        t0 = half * NRT
        if t0 == 0:
            sl = np.concatenate([np.zeros((2, D), np.float32), x[b, 0:NRT]], axis=0)
        else:
            sl = x[b, t0 - 2:t0 + NRT]
        m = {"xT": np.ascontiguousarray(sl.T),
             "maskd": np.full((128, 1), 0.0 if t0 == 0 else 1.0, np.float32)}
        m.update(wts)
        for k, v in vecs.items():
            m["v_" + k] = np.ascontiguousarray(v.reshape(NCK, 128))
        in_maps.append(m)

    res = run_bass_kernel_spmd(nc, in_maps, core_ids=list(range(n_cores)))
    if dbg:
        kernel._dbg = [np.asarray(res.results[c]["dbg"]) for c in range(n_cores)]
    out = np.empty((Bsz, T, D), np.float32)
    for c in range(n_cores):
        b, half = c // halves, c % halves
        out[b, half * NRT:(half + 1) * NRT] = np.asarray(res.results[c]["y"])
    return out
